# version 06
# baseline (speedup 1.0000x reference)
import numpy as np
import ml_dtypes
from contextlib import ExitStack
import concourse.bass as bass
import concourse.mybir as mybir
from concourse.bass import AP, IndirectOffsetOnAxis
from concourse.bass_utils import run_bass_kernel_spmd

F32 = mybir.dt.float32
BF16 = mybir.dt.bfloat16
I32 = mybir.dt.int32
AF = mybir.ActivationFunctionType
ALU = mybir.AluOpType
AX = mybir.AxisListType
NPBF = ml_dtypes.bfloat16

E = 2048
H0, D0 = 16, 128
H1, D1 = 32, 64
FF0 = 8192
NEXP, FI, FSH = 8, 1408, 5632
LN_EPS = 1e-5


class _Rec:

    def __init__(self):
        self.calls = []

    def __getattr__(self, name):
        def meth(*a, **kw):
            self.calls.append((name, a, kw))
            return None
        return meth


class Ctx:

    def __init__(self, nc, T):
        self.nc = nc
        self.T = T
        self.NT = T // 128
        self.NC = T // 512
        self.stack = ExitStack()
        self.cnt = {}
        self.sems = {}
        self.q = {'Y': [], 'G': [], 'T': [], 'V': [], 'S': []}
        self.engsem = {'T': 'sT', 'V': 'sV', 'S': 'sS', 'G': 'sG'}
        self.hw = {}
        self.bufs = {}

    def sem(self, name):
        self.sems[name] = self.stack.enter_context(self.nc.semaphore(name))
        self.cnt[name] = 0

    DMA_SEMS = frozenset(('yA', 'yB', 'yC', 'yD', 'yE', 'yF', 'yO', 'yP', 'yM',
                          'gA', 'gB', 'gC', 'gD', 'gM', 'gR', 'gW'))

    def wait(self, ek, sname, v):
        if sname in self.DMA_SEMS:
            v = self.cnt[sname]
        if v <= 0 or self.hw.get((ek, sname), 0) >= v:
            return
        sem = self.sems[sname]
        self.q[ek].append(lambda eng, v=v: eng.wait_ge(sem, v))
        self.hw[(ek, sname)] = v

    def dep(self, ek, reads, writes):
        best = {}

        def add(tok):
            if tok is not None and best.get(tok[0], 0) < tok[1]:
                best[tok[0]] = tok[1]

        for b in reads:
            st = self.bufs.get(b)
            if st:
                add(st['w'])
        for b in writes:
            st = self.bufs.get(b)
            if st:
                add(st['w'])
                for s, v in st['r'].items():
                    add((s, v))
        for s, v in best.items():
            self.wait(ek, s, v)

    def note(self, tok, reads, writes):
        for b in writes:
            self.bufs[b] = {'w': tok, 'r': {}}
        if tok is not None:
            s, v = tok
            for b in reads:
                st = self.bufs.setdefault(b, {'w': None, 'r': {}})
                if st['r'].get(s, 0) < v:
                    st['r'][s] = v

    def op(self, ek, fn, reads=(), writes=(), inc=True):
        self.dep(ek, reads, writes)
        rec = _Rec()
        fn(rec)
        assert len(rec.calls) == 1, "op() lambda must make exactly one engine call"
        name, a, kw = rec.calls[0]
        if inc:
            sname = self.engsem[ek]
            self.cnt[sname] += 1
            sem = self.sems[sname]
            self.q[ek].append(lambda eng, name=name, a=a, kw=kw:
                              getattr(eng, name)(*a, **kw).then_inc(sem, 1))
            tok = (sname, self.cnt[sname])
        else:
            self.q[ek].append(lambda eng, name=name, a=a, kw=kw: getattr(eng, name)(*a, **kw))
            tok = None
        self.note(tok, reads, writes)
        return tok

    def mm(self, lhsT, rhs, out, reads, psum_key, start, stop, force_inc=False):
        self.dep('T', reads, (psum_key,) if start else ())
        if stop or force_inc:
            self.cnt['sT'] += 1
            sem = self.sems['sT']
            self.q['T'].append(lambda eng: eng.matmul(out, lhsT, rhs, start=start, stop=stop,
                                                      skip_group_check=True).then_inc(sem, 1))
            tok = ('sT', self.cnt['sT'])
            self.note(tok, reads, (psum_key,) if stop else ())
            return tok
        self.q['T'].append(lambda eng: eng.matmul(out, lhsT, rhs, start=start, stop=stop,
                                                  skip_group_check=True))
        self.note(None, reads, ())
        return None

    def dma(self, ek, sname, out, in_, reads=(), writes=(), accum_op=None, indirect=None):
        self.dep(ek, reads, writes)
        self.wait(ek, sname, self.cnt[sname])
        sem = self.sems[sname]

        def closure(eng, out=out, in_=in_, accum_op=accum_op, indirect=indirect):
            if indirect is not None:
                instr = eng.indirect_dma_start(out=out, out_offset=None, in_=in_,
                                               in_offset=indirect)
            elif accum_op is not None:
                instr = eng.dma_start(out=out, in_=in_, accum_op=accum_op)
            else:
                instr = eng.dma_start(out=out, in_=in_)
            instr.then_inc(sem, 16)
        self.q[ek].append(closure)
        self.cnt[sname] += 16
        tok = (sname, self.cnt[sname])
        self.note(tok, reads, writes)
        return tok

    def barrier(self):
        snap = list(self.cnt.items())
        for ek in ('Y', 'G', 'T', 'V', 'S'):
            for sname, v in snap:
                self.wait(ek, sname, v)
        self.bufs = {}


def build_program(T=2048, taps=()):
    nc = bass.Bass("TRN2", target_bir_lowering=False, debug=False)
    k = Ctx(nc, T)
    NT, NC = k.NT, k.NC
    _uid = [0]

    def _un(n):
        _uid[0] += 1
        return "%s_u%d" % (n, _uid[0])
    taps = set(taps)

    def dram(name, shape, dtype):
        kind = "ExternalOutput" if name in taps else "Internal"
        return nc.dram_tensor(name, shape, dtype, kind=kind).ap()

    def inp(name, shape, dtype):
        return nc.dram_tensor(name, shape, dtype, kind="ExternalInput").ap()

    x_in = inp("x", [T, E], F32)
    w_qkv = inp("w_qkv", [E, 3 * E], F32)
    w_out0 = inp("w_out0", [E, E], F32)
    ln0_s = inp("ln0_s", [1, 512], F32)
    ln0_b = inp("ln0_b", [1, 512], F32)
    wgT0 = inp("wgT0", [E, FF0], F32)
    wuT0 = inp("wuT0", [E, FF0], F32)
    wdT0 = inp("wdT0", [FF0, E], F32)
    wqT1 = inp("wqT1", [E, E], F32)
    wkT1 = inp("wkT1", [E, E], F32)
    wvT1 = inp("wvT1", [E, E], F32)
    woT1 = inp("woT1", [E, E], F32)
    relpos = inp("relpos", [1023, 64], F32)
    rwT = inp("rwT", [E, NEXP], F32)
    sgw = inp("sgw", [E, 1], F32)
    wg1 = inp("wg1", [NEXP, E, FI], F32)
    wu1 = inp("wu1", [NEXP, E, FI], F32)
    wd1 = inp("wd1", [NEXP, FI, E], F32)
    swg = inp("swg", [E, FSH], F32)
    swu = inp("swu", [E, FSH], F32)
    swd = inp("swd", [FSH, E], F32)
    ropec_d = inp("ropec_d", [T, 256], BF16)
    ropes_d = inp("ropes_d", [T, 256], BF16)
    mask0_d = inp("mask0_d", [128, 4 * 512], BF16)
    bidx_d = inp("bidx_d", [128, 15], I32)

    y_out = nc.dram_tensor("y", [T, E], F32, kind="ExternalOutput").ap()
    res1 = dram("res1", [T, E], F32)
    res2 = dram("res2", [T, E], F32)
    res3 = dram("res3", [T, E], F32)
    pre1 = dram("pre1", [T, E], F32)
    pre2 = dram("pre2", [T, E], F32)
    pre3 = dram("pre3", [T, E], F32)
    QT0 = dram("QT0", [H0, D0, T], BF16)
    KT0 = dram("KT0", [H0, D0, T], BF16)
    V0 = dram("V0", [H0, T, D0], BF16)
    G0 = dram("G0", [FF0, T], BF16)
    QT1 = dram("QT1", [E, T], BF16)
    KT1 = dram("KT1", [E, T], BF16)
    V1 = dram("V1", [T, E], BF16)
    G1 = dram("G1", [FSH, T], BF16)
    racc = dram("racc", [T, E], F32)
    relm = dram("relm", [1023, 1], F32)
    expB_d = dram("expB_d", [128, 15 * 128], BF16)
    ext_d = nc.dram_tensor("ext", [4095], F32, kind="Internal")
    ext = ext_d.ap()
    ext2d = ext.rearrange("(a b) -> a b", b=1)

    for s in ('sT', 'sV', 'sS', 'sG', 'yA', 'yB', 'yC', 'yD', 'yE', 'yF', 'yO', 'yP',
              'yM', 'gA', 'gB', 'gC', 'gD', 'gM', 'gR', 'gW'):
        k.sem(s)

    st = k.stack
    sb = lambda name, shape, dt: st.enter_context(nc.sbuf_tensor(name, shape, dt))

    BIG1 = sb("BIG1", [128, 32768], BF16)
    BIG2 = sb("BIG2", [128, 32768], BF16)

    ln0sb = sb("ln0sb", [128, 512], F32)
    ln0bb = sb("ln0bb", [128, 512], F32)
    idbf = sb("idbf", [128, 128], BF16)
    onescol = sb("onescol", [128, 1], BF16)
    ones1f = sb("ones1f", [1, 128], F32)
    combine = sb("combine", [128, NT, NEXP], F32)
    sgate = sb("sgate", [128, NT, 1], F32)
    idf32 = sb("idf32", [128, 128], F32)
    mid_stack = ExitStack()
    cLow = mid_stack.enter_context(nc.sbuf_tensor("cLow", [128, 1], F32))
    cHigh = mid_stack.enter_context(nc.sbuf_tensor("cHigh", [128, 1], F32))
    early_stack = ExitStack()
    ropec = early_stack.enter_context(nc.sbuf_tensor("ropec", [128, NT, 256], BF16))
    ropes = early_stack.enter_context(nc.sbuf_tensor("ropes", [128, NT, 256], BF16))

    def XT(et, t0, t1):
        return BIG1[:, et * T + t0: et * T + t1]

    def CTXT(p0, p1, ct, t0, t1):
        return BIG2[p0:p1, ct * T + t0: ct * T + t1]

    if True:
        def emit_ln(zap, zkey, W, outs, sc, p):
            k.op('V', lambda v: v.reduce_sum(sc['s1'], zap, axis=AX.X), (zkey,), (p + 's1',))
            emit_ln_post(zap, zkey, W, outs, sc, p)

        def emit_ln_post(zap, zkey, W, outs, sc, p):
            k.op('S', lambda s: s.activation(sc['sq'], zap, AF.Square, accum_out=sc['s2']),
                 (zkey,), (p + 'sq', p + 's2'))
            k.op('V', lambda v: v.tensor_scalar(sc['m'], sc['s1'], scalar1=1.0 / W, scalar2=None,
                                                op0=ALU.mult), (p + 's1',), (p + 'm',))
            k.op('V', lambda v: v.tensor_scalar(sc['t2'], sc['s2'], scalar1=1.0 / W, scalar2=None,
                                                op0=ALU.mult), (p + 's2',), (p + 't2',))
            k.op('V', lambda v: v.tensor_tensor(sc['var'], sc['m'], sc['m'], op=ALU.mult),
                 (p + 'm',), (p + 'var',))
            k.op('V', lambda v: v.tensor_tensor(sc['var'], sc['t2'], sc['var'], op=ALU.subtract),
                 (p + 't2', p + 'var'), (p + 'var',))
            k.op('V', lambda v: v.tensor_scalar(sc['var'], sc['var'], scalar1=LN_EPS, scalar2=None,
                                                op0=ALU.add), (p + 'var',), (p + 'var',))
            k.op('S', lambda s: s.activation(sc['sd'], sc['var'], AF.Sqrt),
                 (p + 'var',), (p + 'sd',))
            k.op('V', lambda v: v.reciprocal(sc['rstd'], sc['sd']), (p + 'sd',), (p + 'rstd',))
            k.op('V', lambda v: v.scalar_tensor_tensor(sc['nm'], sc['m'], -1.0, sc['rstd'],
                                                       op0=ALU.mult, op1=ALU.mult),
                 (p + 'm', p + 'rstd'), (p + 'nm',))
            for oap, okey in outs:
                k.op('S', lambda s, oap=oap: s.activation(oap, zap, AF.Identity, bias=sc['nm'],
                                                          scale=sc['rstd']),
                     (zkey, p + 'nm', p + 'rstd'), (okey,))

        def mk_lnsc(ph, name, W):
            d = {n: ph.enter_context(nc.sbuf_tensor(_un(name + n), [128, 1], F32))[:]
                 for n in ('s1', 's2', 'm', 't2', 'var', 'sd', 'rstd', 'nm')}
            d['sq'] = ph.enter_context(nc.sbuf_tensor(_un(name + 'sq'), [128, W], F32))[:]
            return d

        with ExitStack() as ph:
            psb = lambda n, sh, dt=F32: ph.enter_context(nc.psum_tensor(_un(n), sh, dt))
            ps_tr = psb("ps_tr", [128, 2, 1024], BF16)
            ps_trf = psb("ps_trf", [128, 512])
            k.dma('Y', 'yM', ropec[:], ropec_d.rearrange("(a p) f -> p a f", p=128), writes=('ropec',))
            k.dma('Y', 'yM', ropes[:], ropes_d.rearrange("(a p) f -> p a f", p=128), writes=('ropes',))
            k.dma('Y', 'yM', ln0sb[:], ln0_s.broadcast_to([128, 512]), writes=('ln0sb',))
            k.dma('Y', 'yM', ln0bb[:], ln0_b.broadcast_to([128, 512]), writes=('ln0bb',))
            k.op('G', lambda g: g.memset(idbf[:], 0.0), (), ('idbf',))
            k.op('G', lambda g: g.affine_select(out=idbf[:], in_=idbf[:], compare_op=ALU.not_equal,
                                                fill=1.0, base=0, pattern=[[-1, 128]],
                                                channel_multiplier=1), ('idbf',), ('idbf',))
            k.op('G', lambda g: g.memset(onescol[:], 1.0), (), ('onescol',))
            k.op('G', lambda g: g.memset(ones1f[:], 1.0), (), ('ones1f',))

            xstage = ph.enter_context(nc.sbuf_tensor(_un("xstage"), [128, 2, E], BF16))
            for tt in range(NT):
                sl = tt % 2
                k.dma('G', 'gA' if sl == 0 else 'gB', xstage[:, sl, :],
                      x_in[tt * 128:(tt + 1) * 128, :], writes=('xs%d' % sl,))
                for et in range(16):
                    pk = 'ps_tr%d' % (et % 2)
                    k.op('T', lambda t: t.transpose(ps_tr[:, et % 2, 0:128],
                                                    xstage[:, sl, et * 128:(et + 1) * 128], idbf[:]),
                         ('xs%d' % sl, 'idbf'), (pk,))
                    k.op('V', lambda v: v.tensor_copy(XT(et, tt * 128, (tt + 1) * 128),
                                                      ps_tr[:, et % 2, 0:128]), (pk,), ('XT',))

            rpbuf = ph.enter_context(nc.sbuf_tensor(_un("rpbuf"), [128, 64], F32))
            rmcol = ph.enter_context(nc.sbuf_tensor(_un("rmcol"), [128, 1], F32))
            relm_toks = []
            for i in range(8):
                rows = 128 if i < 7 else 127
                k.dma('Y', 'yM', rpbuf[0:rows, :], relpos[i * 128:i * 128 + rows, :], writes=('rpbuf',))
                k.op('V', lambda v: v.reduce_sum(rmcol[0:rows, :], rpbuf[0:rows, :], axis=AX.X),
                     ('rpbuf',), ('rmcol',))
                k.op('V', lambda v: v.tensor_scalar(rmcol[0:rows, :], rmcol[0:rows, :], scalar1=1.0 / 64,
                                                    scalar2=None, op0=ALU.mult), ('rmcol',), ('rmcol',))
                k.dma('Y', 'yM', relm[i * 128:i * 128 + rows, :], rmcol[0:rows, :],
                      reads=('rmcol',), writes=('relm',))
            k.dma('Y', 'yM', ext2d[1536:2559, :], relm[:, :], reads=('relm',), writes=('extm',))
            padt = ph.enter_context(nc.sbuf_tensor(_un("padt"), [128, 12], F32))
            k.dma('Y', 'yM', cLow[:], relm[0:1, :].broadcast_to([128, 1]), reads=('relm',), writes=('cLow',))
            k.dma('Y', 'yM', cHigh[:], relm[1022:1023, :].broadcast_to([128, 1]), reads=('relm',),
                  writes=('cHigh',))
            k.op('G', lambda g: g.memset(padt[:], 0.0), (), ('padt',))
            k.op('V', lambda v: v.tensor_scalar(padt[:], padt[:], scalar1=cLow[:], scalar2=None,
                                                op0=ALU.add), ('padt', 'cLow'), ('padt',))
            k.dma('Y', 'yM', AP(ext_d, 0, [[12, 128], [1, 12]]), padt[:],
                  reads=('padt',), writes=('extl',))
            k.op('G', lambda g: g.memset(padt[:], 0.0), (), ('padt',))
            k.op('V', lambda v: v.tensor_scalar(padt[:], padt[:], scalar1=cHigh[:], scalar2=None,
                                                op0=ALU.add), ('padt', 'cHigh'), ('padt',))
            k.dma('Y', 'yM', AP(ext_d, 2559, [[12, 128], [1, 12]]), padt[:],
                  reads=('padt',), writes=('extr',))
            natb = ph.enter_context(nc.sbuf_tensor(_un("natb"), [128, 128], F32))
            btmp = ph.enter_context(nc.sbuf_tensor(_un("btmp"), [128, 128], F32))
            bidx = ph.enter_context(nc.sbuf_tensor(_un("bidx"), [128, 15], I32))
            expstg = ph.enter_context(nc.sbuf_tensor(_un("expstg"), [128, 128], BF16))
            k.dma('Y', 'yM', bidx[:], bidx_d[:, :], writes=('bidx',))
            k.op('G', lambda g: g.memset(idf32[:], 0.0), (), ('idf32',))
            k.op('G', lambda g: g.affine_select(out=idf32[:], in_=idf32[:], compare_op=ALU.not_equal,
                                                fill=1.0, base=0, pattern=[[-1, 128]],
                                                channel_multiplier=1), ('idf32',), ('idf32',))
            for d in range(15):
                k.dma('G', 'gM', natb[:], ext2d[:, :], reads=('bidx', 'extm', 'extl', 'extr'),
                      writes=('natb',), indirect=IndirectOffsetOnAxis(bidx[:, d:d + 1], 0))
                k.op('T', lambda t: t.transpose(ps_trf[:, 0:128], natb[:], idf32[:]),
                     ('natb', 'idf32'), ('ps_trf',))
                k.op('V', lambda v: v.tensor_copy(btmp[:], ps_trf[:, 0:128]), ('ps_trf',), ('btmp',))
                k.op('S', lambda s, d=d: s.activation(expstg[:], btmp[:], AF.Exp),
                     ('btmp', 'expstg'), ('expstg',))
                k.dma('Y', 'yM', expB_d[:, d * 128:(d + 1) * 128], expstg[:], reads=('expstg',))
            k.barrier()

        with ExitStack() as ph:
            psb = lambda n, sh, dt=F32: ph.enter_context(nc.psum_tensor(_un(n), sh, dt))
            ps_mm = psb("ps_mm", [128, 2, 512])
            ps_tr = psb("ps_tr", [128, 2, 1024], BF16)
            wqkvb = ph.enter_context(nc.sbuf_tensor(_un("wqkvb"), [128, 2, 16, 512], BF16))
            ropeb = ph.enter_context(nc.sbuf_tensor(_un("ropeb"), [128, 2, 4, 64], F32))
            lnin = ph.enter_context(nc.sbuf_tensor(_un("lnin"), [128, 4, 128], F32))
            a1 = ph.enter_context(nc.sbuf_tensor(_un("a1"), [128, 4, 128], F32))
            qkbf = ph.enter_context(nc.sbuf_tensor(_un("qkbf"), [128, 4, 128], BF16))
            qstage = ph.enter_context(nc.sbuf_tensor(_un("qstage"), [128, 4, T], BF16))
            vstage = ph.enter_context(nc.sbuf_tensor(_un("vstage"), [128, 512], BF16))
            lsc = {n: ph.enter_context(nc.sbuf_tensor(_un("p1l" + n), [128, 4], F32))[:]
                   for n in ('s1', 's2', 'm', 't2', 'var', 'sd', 'rstd', 'nm')}
            lsq = ph.enter_context(nc.sbuf_tensor(_un("p1lsq"), [128, 128], F32))
            for cc in range(12):
                wsl = cc % 2
                k.dma('G', 'gA' if wsl == 0 else 'gB', wqkvb[:, wsl],
                      w_qkv.rearrange("(a p) c -> p a c", p=128)[:, :, cc * 512:(cc + 1) * 512],
                      writes=('wqkv%d' % wsl,))
                is_v = cc >= 8
                for tt in range(NT):
                    pk = 'ps_mm%d' % (tt % 2)
                    for et in range(16):
                        k.mm(XT(et, tt * 128, (tt + 1) * 128), wqkvb[:, wsl, et, :],
                             ps_mm[:, tt % 2, :], ('XT', 'wqkv%d' % wsl), pk,
                             start=(et == 0), stop=(et == 15))
                    if is_v:
                        k.op('V', lambda v: v.tensor_copy(vstage[:], ps_mm[:, tt % 2, :]),
                             (pk,), ('vstage',))
                        for j in range(4):
                            h = (cc - 8) * 4 + j
                            k.dma('Y', 'yO', V0.rearrange("h (a p) d -> h p a d", p=128)[h, :, tt, :],
                                  vstage[:, j * 128:(j + 1) * 128], reads=('vstage',))
                    else:
                        rc4 = ropec[:, tt, :].rearrange("p (j f) -> p j f", f=64)
                        rs4 = ropes[:, tt, :].rearrange("p (j f) -> p j f", f=64)
                        pmm = ps_mm[:, tt % 2, :]
                        q1v = pmm.rearrange("p (j f) -> p j f", f=128)[:, :, 0:64]
                        q2v = pmm.rearrange("p (j f) -> p j f", f=128)[:, :, 64:128]
                        rb = lambda i: ropeb[:, i % 2, :, :]
                        k.op('V', lambda v: v.tensor_tensor(rb(0), q1v, rc4, op=ALU.mult),
                             (pk, 'ropec'), ('rb0',))
                        k.op('V', lambda v: v.tensor_tensor(rb(1), q2v, rs4, op=ALU.mult),
                             (pk, 'ropes'), ('rb1',))
                        k.op('V', lambda v: v.tensor_tensor(lnin[:, :, 0:64], rb(0), rb(1),
                                                            op=ALU.subtract), ('rb0', 'rb1'), ('lnin',))
                        k.op('V', lambda v: v.tensor_tensor(rb(2), q1v, rs4, op=ALU.mult),
                             (pk, 'ropes'), ('rb0',))
                        k.op('V', lambda v: v.tensor_tensor(rb(3), q2v, rc4, op=ALU.mult),
                             (pk, 'ropec'), ('rb1',))
                        k.op('V', lambda v: v.tensor_tensor(lnin[:, :, 64:128], rb(2), rb(3),
                                                            op=ALU.add), ('rb0', 'rb1'), ('lnin',))
                        for j in range(4):
                            k.op('S', lambda s, j=j: s.activation(lsq[:], lnin[:, j, :],
                                                                  AF.Square, accum_out=lsc['s2'][:, j:j + 1]),
                                 ('lnin', 'lsqd'), ('ls2', 'lsqd'))
                        k.op('V', lambda v: v.reduce_sum(lsc['s1'], lnin[:].rearrange("p j f -> p j f"),
                                                         axis=AX.X), ('lnin',), ('ls1',))
                        k.op('V', lambda v: v.tensor_scalar(lsc['m'], lsc['s1'], scalar1=1.0 / D0,
                                                            scalar2=None, op0=ALU.mult), ('ls1',), ('lm',))
                        k.op('V', lambda v: v.tensor_scalar(lsc['t2'], lsc['s2'], scalar1=1.0 / D0,
                                                            scalar2=None, op0=ALU.mult), ('ls2',), ('lt2',))
                        k.op('V', lambda v: v.tensor_tensor(lsc['var'], lsc['m'], lsc['m'],
                                                            op=ALU.mult), ('lm',), ('lvar',))
                        k.op('V', lambda v: v.tensor_tensor(lsc['var'], lsc['t2'], lsc['var'],
                                                            op=ALU.subtract), ('lt2', 'lvar'), ('lvar',))
                        k.op('V', lambda v: v.tensor_scalar(lsc['var'], lsc['var'], scalar1=LN_EPS,
                                                            scalar2=None, op0=ALU.add), ('lvar',), ('lvar',))
                        k.op('S', lambda s: s.activation(lsc['sd'], lsc['var'], AF.Sqrt),
                             ('lvar',), ('lsd',))
                        k.op('V', lambda v: v.reciprocal(lsc['rstd'], lsc['sd']), ('lsd',), ('lrstd',))
                        k.op('V', lambda v: v.scalar_tensor_tensor(lsc['nm'], lsc['m'], -1.0,
                                                                   lsc['rstd'], op0=ALU.mult, op1=ALU.mult),
                             ('lm', 'lrstd'), ('lnm',))
                        for j in range(4):
                            k.op('S', lambda s, j=j: s.activation(a1[:, j, :], lnin[:, j, :], AF.Identity,
                                                                  bias=lsc['nm'][:, j:j + 1],
                                                                  scale=lsc['rstd'][:, j:j + 1]),
                                 ('lnin', 'lnm', 'lrstd'), ('a1%d' % j,))
                        a1v = a1[:].rearrange("p j f -> p (j f)")
                        qkv_ = qkbf[:].rearrange("p j f -> p (j f)")
                        k.op('V', lambda v: v.tensor_tensor(a1v, a1v, ln0sb[:], op=ALU.mult),
                             ('a10', 'a11', 'a12', 'a13', 'ln0sb'), ('a1m',))
                        k.op('V', lambda v: v.tensor_tensor(qkv_, a1v, ln0bb[:], op=ALU.add),
                             ('a1m', 'ln0bb'), ('qkbf',))
                        for j in range(4):
                            trk = 'ps_tr%d' % (j % 2)
                            k.op('T', lambda t, j=j: t.transpose(ps_tr[:, j % 2, 0:128], qkbf[:, j, :],
                                                                 idbf[:]), ('qkbf', 'idbf'), (trk,))
                            k.op('V', lambda v, j=j, tt=tt: v.tensor_copy(
                                    qstage[:, j, tt * 128:(tt + 1) * 128], ps_tr[:, j % 2, 0:128]),
                                 (trk,), ('qstage',))
                if not is_v:
                    tgt = QT0 if cc < 4 else KT0
                    for j in range(4):
                        h = (cc % 4) * 4 + j
                        k.dma('Y', 'yO', tgt[h, :, :], qstage[:, j, :], reads=('qstage',))
            k.barrier()

        early_stack.close()

        with ExitStack() as ph:
            psb = lambda n, sh, dt=F32: ph.enter_context(nc.psum_tensor(_un(n), sh, dt))
            ps_s = psb("ps_s", [128, 4, 512])
            ps_ctx = psb("ps_ctx", [128, 2, 512])
            ps_den = psb("ps_den", [128, 512])
            ps_bc = psb("ps_bc", [128, 512])
            mask0 = ph.enter_context(nc.sbuf_tensor(_un("mask0"), [128, 4, 512], BF16))
            k.dma('Y', 'yM', mask0[:].rearrange("p a b -> p (a b)"), mask0_d[:, :], writes=('mask0',))
            qth = ph.enter_context(nc.sbuf_tensor(_un("qth"), [128, 2, 2, T], BF16))
            kth = ph.enter_context(nc.sbuf_tensor(_un("kth"), [128, 2, 2, T], BF16))
            vth = ph.enter_context(nc.sbuf_tensor(_un("vth"), [128, 2, 2, NT, 128], BF16))
            sqb = ph.enter_context(nc.sbuf_tensor(_un("sqb"), [128, 4, 512], F32))
            pbuf = ph.enter_context(nc.sbuf_tensor(_un("pbuf"), [128, 4, 512], BF16))
            recb = ph.enter_context(nc.sbuf_tensor(_un("recb"), [1, 2, 512], F32))
            bcb = ph.enter_context(nc.sbuf_tensor(_un("bcb"), [128, 2, 512], F32))
            for hp in range(H0 // 2):
                sl = hp % 2
                for hh in range(2):
                    h = 2 * hp + hh
                    k.dma('Y', ('yA', 'yB', 'yC', 'yD')[2 * sl + hh], qth[:, sl, hh, :],
                          QT0[h, :, :], writes=('qth%d%d' % (sl, hh),))
                    k.dma('Y', ('yC', 'yD', 'yA', 'yB')[2 * sl + hh], kth[:, sl, hh, :],
                          KT0[h, :, :], writes=('kth%d%d' % (sl, hh),))
                    k.dma('Y', 'yE' if sl == 0 else 'yF', vth[:, sl, hh],
                          V0.rearrange("h (a p) d -> h p a d", p=128)[h],
                          writes=('vth%d%d' % (sl, hh),))

                def mm1(hh, qc, kt):
                    sl_ = hp % 2
                    k.mm(kth[:, sl_, hh, kt * 128:(kt + 1) * 128],
                         qth[:, sl_, hh, qc * 512:(qc + 1) * 512],
                         ps_s[:, 2 * hh + kt % 2, :],
                         ('kth%d%d' % (sl_, hh), 'qth%d%d' % (sl_, hh)),
                         'ps_s%d' % (2 * hh + kt % 2), True, True)

                def chain(hh, qc, kt, nkt):
                    sl_ = hp % 2
                    pslot = 2 * hh + kt % 2
                    skey = 'ps_s%d' % pslot
                    pkk = 'pb%d' % pslot
                    k.op('S', lambda s, pslot=pslot: s.activation(
                            sqb[:, pslot, :], ps_s[:, pslot, :], AF.Square),
                         (skey,), ('sqb%d' % pslot,))
                    k.op('V', lambda v, pslot=pslot: v.tensor_tensor(
                            pbuf[:, pslot, :], sqb[:, pslot, :], sqb[:, pslot, :],
                            op=ALU.mult), ('sqb%d' % pslot,), (pkk,))
                    if kt >= 4 * qc:
                        k.op('V', lambda v, kt=kt, qc=qc, pslot=pslot: v.tensor_tensor(
                                pbuf[:, pslot, :], pbuf[:, pslot, :],
                                mask0[:, kt - 4 * qc, :], op=ALU.mult),
                             (pkk, 'mask0'), (pkk,))
                    k.mm(vth[:, sl_, hh, kt, :], pbuf[:, pslot, :], ps_ctx[:, hh, :],
                         ('vth%d%d' % (sl_, hh), pkk), 'ps_ctx%d' % hh,
                         start=(kt == 0), stop=(kt == nkt - 1), force_inc=True)
                    k.mm(onescol[:], pbuf[:, pslot, :], ps_den[64 * hh:64 * hh + 1, :],
                         ('onescol', pkk), 'ps_den%d' % hh,
                         start=(kt == 0), stop=(kt == nkt - 1), force_inc=True)

                for qc in range(NC):
                    nkt = 4 * qc + 4
                    for hh in range(2):
                        mm1(hh, qc, 0)
                    for kt in range(nkt):
                        for hh in range(2):
                            if kt + 1 < nkt:
                                mm1(hh, qc, kt + 1)
                            chain(hh, qc, kt, nkt)
                    for hh in range(2):
                        h = 2 * hp + hh
                        ck = 'ps_ctx%d' % hh
                        k.op('V', lambda v, hh=hh: v.reciprocal(recb[:, hh, :],
                                                                ps_den[64 * hh:64 * hh + 1, :]),
                             ('ps_den0', 'ps_den1'), ('recb%d' % hh,))
                        k.op('T', lambda t, hh=hh: t.matmul(ps_bc[:, :], ones1f[:],
                                                            recb[:, hh, :], start=True, stop=True),
                             ('ones1f', 'recb%d' % hh), ('ps_bc',))
                        k.op('S', lambda s, hh=hh: s.copy(bcb[:, hh, :], ps_bc[:, :]),
                             ('ps_bc',), ('bcb%d' % hh,))
                        k.op('V', lambda v, h=h, hh=hh, qc=qc: v.tensor_tensor(
                                CTXT(0, 128, h, qc * 512, (qc + 1) * 512),
                                ps_ctx[:, hh, :], bcb[:, hh, :], op=ALU.mult),
                             (ck, 'bcb%d' % hh), ('CTXT',))
            k.barrier()

        def outproj(wsrc, out_dram, wname):
            with ExitStack() as ph:
                psb = lambda n, sh, dt=F32: ph.enter_context(nc.psum_tensor(_un(n), sh, dt))
                ps_mm = psb("ps_mmo", [128, 2, 4, 512])
                est2 = ph.enter_context(nc.sbuf_tensor(_un("esto"), [128, 2, 4, 512], F32))
                k.dma('G', 'gM', BIG1[:, 0:16 * E].rearrange("p (a b) -> p a b", b=E),
                      wsrc.rearrange("(a p) c -> p a c", p=128), writes=('woB',))
                for tt in range(NT):
                    for ct in range(16):
                        for ec in range(4):
                            k.mm(CTXT(0, 128, ct, tt * 128, (tt + 1) * 128),
                                 BIG1[:, ct * E + ec * 512: ct * E + (ec + 1) * 512],
                                 ps_mm[:, tt % 2, ec, :], ('CTXT', 'woB'),
                                 'ps_mmo%d_%d' % (tt % 2, ec), start=(ct == 0), stop=(ct == 15))
                    for ec in range(4):
                        pk = 'ps_mmo%d_%d' % (tt % 2, ec)
                        ekey = 'esto%d_%d' % (tt % 2, ec)
                        k.op('V', lambda v, tt=tt, ec=ec: v.tensor_copy(
                                est2[:, tt % 2, ec, :], ps_mm[:, tt % 2, ec, :]), (pk,), (ekey,))
                        k.dma('Y', 'yO' if ec % 2 == 0 else 'yP',
                              out_dram[tt * 128:(tt + 1) * 128, ec * 512:(ec + 1) * 512],
                              est2[:, tt % 2, ec, :], reads=(ekey,))
                k.barrier()

        outproj(w_out0, pre1, 'wo0')

        def boundary(res_in, pre_in, res_out, pname, final=False):
            with ExitStack() as ph:
                psb = lambda n, sh, dt=F32: ph.enter_context(nc.psum_tensor(_un(n), sh, dt))
                ps_tr = psb("ps_trb", [128, 2, 1024], BF16)
                bz = ph.enter_context(nc.sbuf_tensor(_un("bz"), [128, 2, E], F32))
                br = ph.enter_context(nc.sbuf_tensor(_un("br"), [128, 2, E], F32))
                bzf = ph.enter_context(nc.sbuf_tensor(_un("bzf"), [128, E], F32))
                bzb = ph.enter_context(nc.sbuf_tensor(_un("bzb"), [128, E], BF16))
                sc = mk_lnsc(ph, "bln", E)
                for tt in range(NT):
                    sl = tt % 2
                    k.dma('Y', 'yA' if sl == 0 else 'yB', br[:, sl, :],
                          res_in[tt * 128:(tt + 1) * 128, :], writes=('br%d' % sl,))
                    k.dma('Y', 'yC' if sl == 0 else 'yD', bz[:, sl, :],
                          pre_in[tt * 128:(tt + 1) * 128, :], writes=('bz%d' % sl,))
                    zkey = 'bz%d' % sl
                    k.op('V', lambda v, sl=sl: v.tensor_tensor(bz[:, sl, :], bz[:, sl, :],
                                                               br[:, sl, :], op=ALU.add),
                         (zkey, 'br%d' % sl), (zkey,))
                    emit_ln(bz[:, sl, :], zkey, E, [(bzf[:], 'bzf')], sc, pname)
                    k.dma('Y', 'yO', res_out[tt * 128:(tt + 1) * 128, :], bzf[:], reads=('bzf',))
                    if not final:
                        k.op('V', lambda v: v.tensor_copy(bzb[:], bzf[:]), ('bzf',), ('bzb',))
                        for et in range(16):
                            trk = 'ps_trb%d' % (et % 2)
                            k.op('T', lambda t: t.transpose(ps_tr[:, et % 2, 0:128],
                                                            bzb[:, et * 128:(et + 1) * 128], idbf[:]),
                                 ('bzb', 'idbf'), (trk,))
                            k.op('V', lambda v: v.tensor_copy(XT(et, tt * 128, (tt + 1) * 128),
                                                              ps_tr[:, et % 2, 0:128]), (trk,), ('XT',))
                k.barrier()

        boundary(x_in, pre1, res1, 'b1.')

        def swiglu_A(wg_src, wu_src, n_it, G_dram):
            with ExitStack() as ph:
                psb = lambda n, sh, dt=F32: ph.enter_context(nc.psum_tensor(_un(n), sh, dt))
                ps_g = psb("ps_g", [128, NC, 512])
                ps_u = psb("ps_u", [128, NC, 512])
                gwb = ph.enter_context(nc.sbuf_tensor(_un("gwb"), [128, 2, 16, 128], BF16))
                uwb = ph.enter_context(nc.sbuf_tensor(_un("uwb"), [128, 2, 16, 128], BF16))
                sgb = ph.enter_context(nc.sbuf_tensor(_un("sgb"), [128, NC, 512], F32))
                gst = ph.enter_context(nc.sbuf_tensor(_un("gst"), [128, NC, 512], BF16))
                for it in range(n_it):
                    wsl = it % 2
                    k.dma('G', 'gA' if wsl == 0 else 'gB', gwb[:, wsl],
                          wg_src.rearrange("(a p) c -> p a c", p=128)[:, :, it * 128:(it + 1) * 128],
                          writes=('gw%d' % wsl,))
                    k.dma('G', 'gC' if wsl == 0 else 'gD', uwb[:, wsl],
                          wu_src.rearrange("(a p) c -> p a c", p=128)[:, :, it * 128:(it + 1) * 128],
                          writes=('uw%d' % wsl,))
                    for et in range(16):
                        for tc in range(NC):
                            k.mm(gwb[:, wsl, et, :], XT(et, tc * 512, (tc + 1) * 512),
                                 ps_g[:, tc, :], ('gw%d' % wsl, 'XT'), 'ps_g%d' % tc,
                                 start=(et == 0), stop=(et == 15))
                        for tc in range(NC):
                            k.mm(uwb[:, wsl, et, :], XT(et, tc * 512, (tc + 1) * 512),
                                 ps_u[:, tc, :], ('uw%d' % wsl, 'XT'), 'ps_u%d' % tc,
                                 start=(et == 0), stop=(et == 15))
                    for tc in range(NC):
                        k.op('S', lambda s, tc=tc: s.activation(sgb[:, tc, :], ps_g[:, tc, :], AF.Silu),
                             ('ps_g%d' % tc,), ('sgb%d' % tc,))
                        k.op('V', lambda v, tc=tc: v.tensor_tensor(gst[:, tc, :], sgb[:, tc, :],
                                                                   ps_u[:, tc, :], op=ALU.mult),
                             ('sgb%d' % tc, 'ps_u%d' % tc), ('gst',))
                    k.dma('Y', 'yO' if it % 2 == 0 else 'yP',
                          G_dram.rearrange("(a p) t -> a p t", p=128)[it],
                          gst[:].rearrange("p a b -> p (a b)"), reads=('gst',))
                k.barrier()

        def down_B(G_dram, n_it, wd_src, out_dram, scale_col=None, accum=False):
            with ExitStack() as ph:
                psb = lambda n, sh, dt=F32: ph.enter_context(nc.psum_tensor(_un(n), sh, dt))
                ps_mm = psb("ps_mmd", [128, 2, 2, 512])
                gtb = ph.enter_context(nc.sbuf_tensor(_un("gtb"), [128, 2, n_it, 128], BF16))
                est2 = ph.enter_context(nc.sbuf_tensor(_un("est2"), [128, 2, 2, 512], F32))
                for pas in range(2):
                    for eco in range(2):
                        dst = BIG1 if eco == 0 else BIG2
                        k.dma('G', 'gA' if eco == 0 else 'gB',
                              dst[:, 0:n_it * 512].rearrange("p (a b) -> p a b", b=512),
                              wd_src.rearrange("(a p) c -> p a c", p=128)[:, :, (2 * pas + eco) * 512:(2 * pas + eco + 1) * 512],
                              writes=('dW%d' % eco,))
                    for tt in range(NT):
                        tsl = tt % 2
                        k.dma('Y', 'yA' if tsl == 0 else 'yB', gtb[:, tsl],
                              G_dram.rearrange("(a p) t -> p a t", p=128)[:, :, tt * 128:(tt + 1) * 128],
                              writes=('gtb%d' % tsl,))
                        for it in range(n_it):
                            for eco in range(2):
                                dst = BIG1 if eco == 0 else BIG2
                                k.mm(gtb[:, tsl, it, :], dst[:, it * 512:(it + 1) * 512],
                                     ps_mm[:, tt % 2, eco, :], ('gtb%d' % tsl, 'dW%d' % eco),
                                     'ps_mmd%d_%d' % (tt % 2, eco),
                                     start=(it == 0), stop=(it == n_it - 1))
                        for eco in range(2):
                            ec = 2 * pas + eco
                            pk = 'ps_mmd%d_%d' % (tt % 2, eco)
                            ekey = 'est2_%d_%d' % (tt % 2, eco)
                            if scale_col is None:
                                k.op('V', lambda v, tt=tt, eco=eco: v.tensor_copy(
                                        est2[:, tt % 2, eco, :], ps_mm[:, tt % 2, eco, :]),
                                     (pk,), (ekey,))
                            else:
                                k.op('V', lambda v, tt=tt, eco=eco: v.tensor_scalar(
                                        est2[:, tt % 2, eco, :], ps_mm[:, tt % 2, eco, :],
                                        scalar1=scale_col(tt), scalar2=None, op0=ALU.mult),
                                     (pk, 'sgate'), (ekey,))
                            if accum:
                                k.dma('G', 'gR', out_dram[tt * 128:(tt + 1) * 128, ec * 512:(ec + 1) * 512],
                                      est2[:, tt % 2, eco, :], reads=(ekey,), accum_op=ALU.add)
                            else:
                                k.dma('Y', 'yO' if eco == 0 else 'yP',
                                      out_dram[tt * 128:(tt + 1) * 128, ec * 512:(ec + 1) * 512],
                                      est2[:, tt % 2, eco, :], reads=(ekey,))
                k.barrier()

        swiglu_A(wgT0, wuT0, FF0 // 128, G0)
        down_B(G0, FF0 // 128, wdT0, pre2)
        boundary(res1, pre2, res2, 'b2.')

        with ExitStack() as ph:
            psb = lambda n, sh, dt=F32: ph.enter_context(nc.psum_tensor(_un(n), sh, dt))
            ps_mm = psb("ps_mmp", [128, 2, max(NC, 4), 512])
            wpb = ph.enter_context(nc.sbuf_tensor(_un("wpb"), [128, 2, 16, 128], BF16))
            pstage = ph.enter_context(nc.sbuf_tensor(_un("pstage"), [128, 2, NC, 512], BF16))
            for (wsrc, tgt, nm) in ((wqT1, QT1, 'q'), (wkT1, KT1, 'k')):
                for ct in range(16):
                    wsl = ct % 2
                    wk = 'wpb%d' % wsl
                    k.dma('G', 'gA' if wsl == 0 else 'gB', wpb[:, wsl],
                          wsrc.rearrange("(a p) c -> p a c", p=128)[:, :, ct * 128:(ct + 1) * 128],
                          writes=(wk,))
                    for et in range(16):
                        for tc in range(NC):
                            k.mm(wpb[:, wsl, et, :], XT(et, tc * 512, (tc + 1) * 512),
                                 ps_mm[:, ct % 2, tc, :], (wk, 'XT'), 'ps_mmp%d_%d' % (ct % 2, tc),
                                 start=(et == 0), stop=(et == 15))
                    for tc in range(NC):
                        k.op('V', lambda v, ct=ct, tc=tc: v.tensor_copy(
                                pstage[:, ct % 2, tc, :], ps_mm[:, ct % 2, tc, :]),
                             ('ps_mmp%d_%d' % (ct % 2, tc),), ('pstage%d' % (ct % 2),))
                    k.dma('Y', 'yO' if ct % 2 == 0 else 'yP', tgt[ct * 128:(ct + 1) * 128, :],
                          pstage[:, ct % 2].rearrange("p a b -> p (a b)"),
                          reads=('pstage%d' % (ct % 2),))
            k.dma('G', 'gM', BIG2[:, 0:16 * E].rearrange("p (a b) -> p a b", b=E),
                  wvT1.rearrange("(a p) c -> p a c", p=128), writes=('wvB',))
            for tt in range(NT):
                for et in range(16):
                    for cc in range(4):
                        k.mm(XT(et, tt * 128, (tt + 1) * 128),
                             BIG2[:, et * E + cc * 512: et * E + (cc + 1) * 512],
                             ps_mm[:, tt % 2, cc, :], ('XT', 'wvB'), 'ps_mmp%d_%d' % (tt % 2, cc),
                             start=(et == 0), stop=(et == 15))
                for cc in range(4):
                    pk = 'ps_mmp%d_%d' % (tt % 2, cc)
                    k.op('V', lambda v, tt=tt, cc=cc: v.tensor_copy(
                            pstage[:, tt % 2, 0, :] if NC == 1 else pstage[:, tt % 2, cc % NC, :],
                            ps_mm[:, tt % 2, cc, :]), (pk,), ('pstage%d' % (tt % 2),))
                    k.dma('Y', 'yO' if cc % 2 == 0 else 'yP',
                          V1[tt * 128:(tt + 1) * 128, cc * 512:(cc + 1) * 512],
                          pstage[:, tt % 2, 0, :] if NC == 1 else pstage[:, tt % 2, cc % NC, :],
                          reads=('pstage%d' % (tt % 2),))
            k.barrier()

        with ExitStack() as ph:
            psb = lambda n, sh, dt=F32: ph.enter_context(nc.psum_tensor(_un(n), sh, dt))
            ps_s = psb("ps_s1", [128, 4, 512])
            ps_ctx = psb("ps_ctx1", [128, 2, 512])
            ps_bc = psb("ps_bc1", [128, 512])
            expBc = ph.enter_context(nc.sbuf_tensor(_un("expBc"), [128, 15 * 128], BF16))
            k.dma('Y', 'yM', expBc[:], expB_d[:, :], writes=('expBc',))
            qb1 = ph.enter_context(nc.sbuf_tensor(_un("qb1"), [128, 2, T], BF16))
            kb1 = ph.enter_context(nc.sbuf_tensor(_un("kb1"), [128, 2, T], BF16))
            vx1 = ph.enter_context(nc.sbuf_tensor(_un("vx1"), [128, 2, 2, NT, 65], BF16))
            pe1 = ph.enter_context(nc.sbuf_tensor(_un("pe1"), [128, 4, 512], BF16))
            pb1 = ph.enter_context(nc.sbuf_tensor(_un("pb1"), [128, 4, 512], BF16))
            rec1 = ph.enter_context(nc.sbuf_tensor(_un("rec1"), [1, 2, 512], F32))
            bcb1 = ph.enter_context(nc.sbuf_tensor(_un("bcb1"), [64, 2, 512], F32))
            k.op('G', lambda g: g.memset(vx1[:, :, :, :, 64:65], 1.0), (), ('vxones',))
            for hp in range(H1 // 2):
                sl = hp % 2
                for hh in range(2):
                    h = 2 * hp + hh
                    p0 = 64 * hh
                    k.dma('Y', ('yA', 'yB', 'yC', 'yD')[2 * sl + hh], qb1[p0:p0 + 64, sl, :],
                          QT1.rearrange("(h d) t -> h d t", d=64)[h], writes=('qb1_%d%d' % (sl, hh),))
                    k.dma('Y', ('yC', 'yD', 'yA', 'yB')[2 * sl + hh], kb1[p0:p0 + 64, sl, :],
                          KT1.rearrange("(h d) t -> h d t", d=64)[h], writes=('kb1_%d%d' % (sl, hh),))
                    k.dma('Y', 'yE' if sl == 0 else 'yF', vx1[:, sl, hh, :, 0:64],
                          V1.rearrange("(a p) c -> p a c", p=128)[:, :, h * 64:(h + 1) * 64],
                          writes=('vx1_%d%d' % (sl, hh),))

                def mm1(hh, qc, kt, sl=None):
                    sl_ = hp % 2
                    p0 = 64 * hh
                    k.mm(kb1[p0:p0 + 64, sl_, kt * 128:(kt + 1) * 128],
                         qb1[p0:p0 + 64, sl_, qc * 512:(qc + 1) * 512],
                         ps_s[:, 2 * hh + kt % 2, :],
                         ('kb1_%d%d' % (sl_, hh), 'qb1_%d%d' % (sl_, hh)),
                         'ps_s1%d' % (2 * hh + kt % 2), True, True)

                def chain(hh, qc, kt):
                    sl_ = hp % 2
                    skey = 'ps_s1%d' % (2 * hh + kt % 2)
                    dc = kt - 4 * qc
                    pslot = 2 * hh + kt % 2
                    pkk = 'pb1%d' % pslot
                    if -4 <= dc <= 7:
                        k.op('S', lambda s, pslot=pslot: s.activation(
                                pe1[:, pslot, :], ps_s[:, pslot, :], AF.Exp, scale=0.125),
                             (skey,), ('pe1%d' % pslot,))
                        k.op('V', lambda v, pslot=pslot, dc=dc: v.tensor_tensor(
                                pb1[:, pslot, :], pe1[:, pslot, :],
                                expBc[:, (7 - dc) * 128:(7 - dc) * 128 + 512], op=ALU.mult),
                             ('pe1%d' % pslot, 'expBc'), (pkk,))
                    else:
                        ccol, ckn = (cHigh, 'cHigh') if dc > 7 else (cLow, 'cLow')
                        k.op('S', lambda s, ccol=ccol, pslot=pslot: s.activation(
                                pb1[:, pslot, :], ps_s[:, pslot, :], AF.Exp, scale=0.125,
                                bias=ccol[:]), (skey, ckn), (pkk,))
                    k.mm(vx1[:, sl_, hh, kt, :], pb1[:, pslot, :], ps_ctx[0:65, hh, :],
                         ('vx1_%d%d' % (sl_, hh), 'vxones', pkk), 'ps_ctx1%d' % hh,
                         start=(kt == 0), stop=(kt == NT - 1), force_inc=True)

                for qc in range(NC):
                    for hh in range(2):
                        mm1(hh, qc, 0)
                    for kt in range(NT):
                        for hh in range(2):
                            if kt + 1 < NT:
                                mm1(hh, qc, kt + 1)
                            chain(hh, qc, kt)
                    for hh in range(2):
                        h = 2 * hp + hh
                        ck = 'ps_ctx1%d' % hh
                        k.op('V', lambda v, hh=hh: v.reciprocal(rec1[:, hh, :],
                                                                ps_ctx[64:65, hh, :]),
                             (ck,), ('rec1%d' % hh,))
                        k.op('T', lambda t, hh=hh: t.matmul(ps_bc[0:64, :], ones1f[0:1, 0:64],
                                                            rec1[:, hh, :], start=True, stop=True),
                             ('ones1f', 'rec1%d' % hh), ('ps_bc1',))
                        k.op('S', lambda s, hh=hh: s.copy(bcb1[:, hh, :], ps_bc[0:64, :]),
                             ('ps_bc1',), ('bcb1%d' % hh,))
                        k.op('V', lambda v, h=h, hh=hh, qc=qc: v.tensor_tensor(
                                CTXT(64 * (h % 2), 64 * (h % 2) + 64, h // 2, qc * 512,
                                     (qc + 1) * 512),
                                ps_ctx[0:64, hh, :], bcb1[:, hh, :], op=ALU.mult),
                             (ck, 'bcb1%d' % hh), ('CTXT',))
            k.barrier()

        mid_stack.close()
        outproj(woT1, pre3, 'wo1')
        boundary(res2, pre3, res3, 'b3.')

        with ExitStack() as ph:
            psb = lambda n, sh, dt=F32: ph.enter_context(nc.psum_tensor(_un(n), sh, dt))
            ps_r = psb("ps_r", [128, 512])
            ps_sg = psb("ps_sg", [128, 512])
            rwb = ph.enter_context(nc.sbuf_tensor(_un("rwb"), [128, 16, NEXP], F32))
            xr32 = ph.enter_context(nc.sbuf_tensor(_un("xr32"), [128, 2, E], F32))
            xt32 = ph.enter_context(nc.sbuf_tensor(_un("xt32"), [128, 16, 128], F32))
            sgwb = ph.enter_context(nc.sbuf_tensor(_un("sgwb"), [128, 16, 1], BF16))
            lgb = ph.enter_context(nc.sbuf_tensor(_un("lgb"), [128, NEXP], F32))
            mskb = ph.enter_context(nc.sbuf_tensor(_un("mskb"), [128, NEXP], F32))
            eq1b = ph.enter_context(nc.sbuf_tensor(_un("eq1b"), [128, NEXP], F32))
            eq2b = ph.enter_context(nc.sbuf_tensor(_un("eq2b"), [128, NEXP], F32))
            scs = {n: ph.enter_context(nc.sbuf_tensor(_un("rt" + n), [128, 1], F32))[:]
                   for n in ('m1', 'm2', 'dl', 'w1', 'w2')}
            k.dma('Y', 'yM', rwb[:], rwT.rearrange("(a p) c -> p a c", p=128), writes=('rwb',))
            for et in range(16):
                k.dma('G', 'gM', sgwb[:, et, :], sgw[et * 128:(et + 1) * 128, :], writes=('sgwb',))
            ps_rt = psb("ps_rt", [128, 512])
            for tt in range(NT):
                sl = tt % 2
                k.dma('Y', 'yA' if sl == 0 else 'yB', xr32[:, sl, :],
                      res3[tt * 128:(tt + 1) * 128, :], writes=('xr32%d' % sl,))
                for et in range(16):
                    k.op('T', lambda t, sl=sl, et=et: t.transpose(
                            ps_rt[:, 0:128], xr32[:, sl, et * 128:(et + 1) * 128], idf32[:]),
                         ('xr32%d' % sl, 'idf32'), ('ps_rt',))
                    k.op('V', lambda v, et=et: v.tensor_copy(xt32[:, et, :], ps_rt[:, 0:128]),
                         ('ps_rt',), ('xt32',))
                for et in range(16):
                    k.mm(xt32[:, et, :], rwb[:, et, :], ps_r[:, 0:NEXP],
                         ('xt32', 'rwb'), 'ps_r', start=(et == 0), stop=(et == 15))
                for et in range(16):
                    k.mm(XT(et, tt * 128, (tt + 1) * 128), sgwb[:, et, :], ps_sg[:, 0:1],
                         ('XT', 'sgwb'), 'ps_sg', start=(et == 0), stop=(et == 15))
                k.op('V', lambda v: v.tensor_copy(lgb[:], ps_r[:, 0:NEXP]), ('ps_r',), ('lgb',))
                k.op('S', lambda s, tt=tt: s.activation(sgate[:, tt, :], ps_sg[:, 0:1], AF.Sigmoid),
                     ('ps_sg',), ('sgate',))
                k.op('V', lambda v: v.reduce_max(scs['m1'], lgb[:], axis=AX.X), ('lgb',), ('m1',))
                k.op('V', lambda v: v.tensor_scalar(eq1b[:], lgb[:], scalar1=scs['m1'], scalar2=None,
                                                    op0=ALU.is_equal), ('lgb', 'm1'), ('eq1b',))
                k.op('V', lambda v: v.scalar_tensor_tensor(mskb[:], eq1b[:], -1e30, lgb[:],
                                                           op0=ALU.mult, op1=ALU.add),
                     ('eq1b', 'lgb'), ('mskb',))
                k.op('V', lambda v: v.reduce_max(scs['m2'], mskb[:], axis=AX.X), ('mskb',), ('m2',))
                k.op('V', lambda v: v.tensor_scalar(eq2b[:], mskb[:], scalar1=scs['m2'], scalar2=None,
                                                    op0=ALU.is_equal), ('mskb', 'm2'), ('eq2b',))
                k.op('V', lambda v: v.tensor_tensor(scs['dl'], scs['m1'], scs['m2'], op=ALU.subtract),
                     ('m1', 'm2'), ('dl',))
                k.op('S', lambda s: s.activation(scs['w1'], scs['dl'], AF.Sigmoid), ('dl',), ('w1',))
                k.op('V', lambda v: v.tensor_scalar(scs['w2'], scs['w1'], scalar1=-1.0, scalar2=1.0,
                                                    op0=ALU.mult, op1=ALU.add), ('w1',), ('w2',))
                k.op('V', lambda v: v.tensor_scalar(eq1b[:], eq1b[:], scalar1=scs['w1'], scalar2=None,
                                                    op0=ALU.mult), ('eq1b', 'w1'), ('eq1b',))
                k.op('V', lambda v, tt=tt: v.scalar_tensor_tensor(combine[:, tt, :], eq2b[:],
                                                                  scs['w2'], eq1b[:],
                                                                  op0=ALU.mult, op1=ALU.add),
                     ('eq2b', 'w2', 'eq1b'), ('combine',))
            k.barrier()

        with ExitStack() as ph:
            psb = lambda n, sh, dt=F32: ph.enter_context(nc.psum_tensor(_un(n), sh, dt))
            ps_g = psb("ps_ge", [128, max(NC, 4), 512])
            ps_u = psb("ps_ue", [128, max(NC, 4), 512])
            gwb = ph.enter_context(nc.sbuf_tensor(_un("gwbe"), [128, 2, 16, 128], BF16))
            uwb = ph.enter_context(nc.sbuf_tensor(_un("uwbe"), [128, 2, 16, 128], BF16))
            sgbe2 = ph.enter_context(nc.sbuf_tensor(_un("sgbe"), [128, NC, 512], F32))
            este = ph.enter_context(nc.sbuf_tensor(_un("este"), [128, 2, 512], F32))
            GeB = ph.enter_context(nc.sbuf_tensor(_un("GeB"), [128, 11 * T], BF16))
            for ex in range(NEXP):
                k.dma('G', 'gW', BIG2[:, 0:11 * E].rearrange("p (a b) -> p a b", b=E),
                      wd1.rearrange("e (a p) c -> e p a c", p=128)[ex], writes=('ewd',))
                for it in range(11):
                    wsl = it % 2
                    k.dma('G', 'gA' if wsl == 0 else 'gB', gwb[:, wsl],
                          wg1.rearrange("e (a p) c -> e p a c", p=128)[ex][:, :, it * 128:(it + 1) * 128],
                          writes=('gw%d' % wsl,))
                    k.dma('G', 'gC' if wsl == 0 else 'gD', uwb[:, wsl],
                          wu1.rearrange("e (a p) c -> e p a c", p=128)[ex][:, :, it * 128:(it + 1) * 128],
                          writes=('uw%d' % wsl,))
                    for et in range(16):
                        for tc in range(NC):
                            k.mm(gwb[:, wsl, et, :], XT(et, tc * 512, (tc + 1) * 512),
                                 ps_g[:, tc, :], ('gw%d' % wsl, 'XT'), 'ps_ge%d' % tc,
                                 start=(et == 0), stop=(et == 15))
                        for tc in range(NC):
                            k.mm(uwb[:, wsl, et, :], XT(et, tc * 512, (tc + 1) * 512),
                                 ps_u[:, tc, :], ('uw%d' % wsl, 'XT'), 'ps_ue%d' % tc,
                                 start=(et == 0), stop=(et == 15))
                    for tc in range(NC):
                        k.op('S', lambda s, tc=tc: s.activation(sgbe2[:, tc, :], ps_g[:, tc, :], AF.Silu),
                             ('ps_ge%d' % tc,), ('sgb%d' % tc,))
                        k.op('V', lambda v, it=it, tc=tc: v.tensor_tensor(
                                GeB[:, it * T + tc * 512: it * T + (tc + 1) * 512], sgbe2[:, tc, :],
                                ps_u[:, tc, :], op=ALU.mult), ('sgb%d' % tc, 'ps_ue%d' % tc), ('Ge',))
                for tt in range(NT):
                    for it in range(11):
                        for hc in range(4):
                            k.mm(GeB[:, it * T + tt * 128: it * T + (tt + 1) * 128],
                                 BIG2[:, it * E + hc * 512: it * E + (hc + 1) * 512],
                                 ps_g[:, hc, :] if tt % 2 == 0 else ps_u[:, hc, :],
                                 ('Ge', 'ewd'), 'ps_%se%d' % ('g' if tt % 2 == 0 else 'u', hc),
                                 start=(it == 0), stop=(it == 10))
                    for hp in range(2):
                        for hh in range(2):
                            hc = hp * 2 + hh
                            psrc = ps_g[:, hc, :] if tt % 2 == 0 else ps_u[:, hc, :]
                            pk = 'ps_%se%d' % ('g' if tt % 2 == 0 else 'u', hc)
                            k.op('V', lambda v, tt=tt, ex=ex, hh=hh, psrc=psrc: v.tensor_scalar(
                                    este[:, hh, :], psrc,
                                    scalar1=combine[:, tt, ex:ex + 1], scalar2=None, op0=ALU.mult),
                                 (pk, 'combine'), ('este%d' % hh,))
                        sem = 'gR' if hp == 0 else 'gM'
                        if ex == 0:
                            k.dma('G', sem, racc[tt * 128:(tt + 1) * 128, hp * 1024:(hp + 1) * 1024],
                                  este[:].rearrange("p a b -> p (a b)"), reads=('este0', 'este1'))
                        else:
                            k.dma('G', sem, racc[tt * 128:(tt + 1) * 128, hp * 1024:(hp + 1) * 1024],
                                  este[:].rearrange("p a b -> p (a b)"), reads=('este0', 'este1'),
                                  accum_op=ALU.add)
            k.barrier()

        swiglu_A(swg, swu, FSH // 128, G1)
        down_B(G1, FSH // 128, swd, racc, scale_col=lambda tt: sgate[:, tt, :], accum=True)

        boundary(res3, racc, y_out, 'bf.', final=True)

    with nc.Block() as block:
        @block.sync
        def _(e):
            for c in k.q['Y']:
                c(e)

        @block.gpsimd
        def _(e):
            for c in k.q['G']:
                c(e)

        @block.tensor
        def _(e):
            for c in k.q['T']:
                c(e)

        @block.vector
        def _(e):
            for c in k.q['V']:
                c(e)

        @block.scalar
        def _(e):
            for c in k.q['S']:
                c(e)

    k.stack.close()
    return nc


def _host_consts(T):
    inv = 1.0 / (10000.0 ** (np.arange(0, D0, 2, dtype=np.float32) / D0))
    t = np.arange(T, dtype=np.float32)[:, None] * inv[None, :].astype(np.float32)
    ropec = np.ascontiguousarray(np.tile(np.cos(t).astype(np.float32), (1, 4))).astype(NPBF)
    ropes = np.ascontiguousarray(np.tile(np.sin(t).astype(np.float32), (1, 4))).astype(NPBF)
    qq = np.arange(512)
    pp = np.arange(128)[:, None]
    mask = np.zeros((128, 4 * 512), np.float32)
    for j in range(4):
        mask[:, j * 512:(j + 1) * 512] = (qq[None, :] >= 128 * j + pp).astype(np.float32)
    mask0 = mask.astype(NPBF)
    bidx = np.zeros((128, 15), np.int32)
    for b in range(15):
        delta = 7 - b
        bidx[:, b] = 128 * delta + 2047 - np.arange(128)
    return ropec, ropes, mask0, bidx


def host_inputs(inp, T=2048):
    f32 = lambda a: np.ascontiguousarray(np.asarray(a, dtype=np.float32))
    ropec, ropes, mask0, bidx = _host_consts(T)
    return {
        "w_qkv": f32(inp["L0_attn_w_qkv"]),
        "w_out0": f32(inp["L0_attn_w_out"]),
        "ln0_s": np.ascontiguousarray(np.tile(f32(inp["L0_attn_ln_scale"]).reshape(1, D0), (1, 4))),
        "ln0_b": np.ascontiguousarray(np.tile(f32(inp["L0_attn_ln_bias"]).reshape(1, D0), (1, 4))),
        "wgT0": np.ascontiguousarray(f32(inp["L0_ffn_w_gate"]).T),
        "wuT0": np.ascontiguousarray(f32(inp["L0_ffn_w_up"]).T),
        "wdT0": np.ascontiguousarray(f32(inp["L0_ffn_w_down"]).T),
        "wqT1": np.ascontiguousarray(f32(inp["L1_attn_wq"]).T),
        "wkT1": np.ascontiguousarray(f32(inp["L1_attn_wk"]).T),
        "wvT1": np.ascontiguousarray(f32(inp["L1_attn_wv"]).T),
        "woT1": np.ascontiguousarray(f32(inp["L1_attn_wo"]).T),
        "relpos": f32(inp["L1_attn_rel_pos"]),
        "rwT": np.ascontiguousarray(f32(inp["L1_ffn_router_w"]).T),
        "sgw": f32(inp["L1_ffn_shared_gate_w"]).reshape(E, 1),
        "wg1": f32(inp["L1_ffn_w_gate"]),
        "wu1": f32(inp["L1_ffn_w_up"]),
        "wd1": f32(inp["L1_ffn_w_down"]),
        "swg": f32(inp["L1_ffn_sw_gate"]),
        "swu": f32(inp["L1_ffn_sw_up"]),
        "swd": f32(inp["L1_ffn_sw_down"]),
        "ropec_d": ropec, "ropes_d": ropes, "mask0_d": mask0, "bidx_d": bidx,
    }


def kernel(**inp):
    T = 2048
    x = np.ascontiguousarray(np.asarray(inp["L0_attn_x"], dtype=np.float32))
    common = host_inputs(inp, T)
    nc = build_program(T=T)
    in_maps = [dict(common, x=np.ascontiguousarray(x[b])) for b in range(8)]
    res = run_bass_kernel_spmd(nc, in_maps, core_ids=list(range(8)))
    return np.stack([np.asarray(r["y"], dtype=np.float32) for r in res.results], axis=0)


# version 08
# speedup vs baseline: 1.0410x; 1.0410x over previous
import numpy as np
import ml_dtypes
from contextlib import ExitStack
import concourse.bass as bass
import concourse.mybir as mybir
from concourse.bass import AP, IndirectOffsetOnAxis
from concourse.bass_utils import run_bass_kernel_spmd

F32 = mybir.dt.float32
BF16 = mybir.dt.bfloat16
I32 = mybir.dt.int32
AF = mybir.ActivationFunctionType
ALU = mybir.AluOpType
AX = mybir.AxisListType
NPBF = ml_dtypes.bfloat16

E = 2048
H0, D0 = 16, 128
H1, D1 = 32, 64
FF0 = 8192
NEXP, FI, FSH = 8, 1408, 5632
LN_EPS = 1e-5


class _Rec:

    def __init__(self):
        self.calls = []

    def __getattr__(self, name):
        def meth(*a, **kw):
            self.calls.append((name, a, kw))
            return None
        return meth


class Ctx:

    def __init__(self, nc, T):
        self.nc = nc
        self.T = T
        self.NT = T // 128
        self.NC = T // 512
        self.stack = ExitStack()
        self.cnt = {}
        self.sems = {}
        self.q = {'Y': [], 'G': [], 'T': [], 'V': [], 'S': []}
        self.engsem = {'T': 'sT', 'V': 'sV', 'S': 'sS', 'G': 'sG'}
        self.hw = {}
        self.bufs = {}

    def sem(self, name):
        self.sems[name] = self.stack.enter_context(self.nc.semaphore(name))
        self.cnt[name] = 0

    DMA_SEMS = frozenset(('yA', 'yB', 'yC', 'yD', 'yE', 'yF', 'yO', 'yP', 'yM',
                          'gA', 'gB', 'gC', 'gD', 'gM', 'gR', 'gW'))

    def wait(self, ek, sname, v):
        if sname in self.DMA_SEMS:
            v = self.cnt[sname]
        if v <= 0 or self.hw.get((ek, sname), 0) >= v:
            return
        sem = self.sems[sname]
        self.q[ek].append(lambda eng, v=v: eng.wait_ge(sem, v))
        self.hw[(ek, sname)] = v

    def dep(self, ek, reads, writes):
        best = {}

        def add(tok):
            if tok is not None and best.get(tok[0], 0) < tok[1]:
                best[tok[0]] = tok[1]

        for b in reads:
            st = self.bufs.get(b)
            if st:
                add(st['w'])
        for b in writes:
            st = self.bufs.get(b)
            if st:
                add(st['w'])
                for s, v in st['r'].items():
                    add((s, v))
        for s, v in best.items():
            self.wait(ek, s, v)

    def note(self, tok, reads, writes):
        for b in writes:
            self.bufs[b] = {'w': tok, 'r': {}}
        if tok is not None:
            s, v = tok
            for b in reads:
                st = self.bufs.setdefault(b, {'w': None, 'r': {}})
                if st['r'].get(s, 0) < v:
                    st['r'][s] = v

    def op(self, ek, fn, reads=(), writes=(), inc=True):
        self.dep(ek, reads, writes)
        rec = _Rec()
        fn(rec)
        assert len(rec.calls) == 1, "op() lambda must make exactly one engine call"
        name, a, kw = rec.calls[0]
        if inc:
            sname = self.engsem[ek]
            self.cnt[sname] += 1
            sem = self.sems[sname]
            self.q[ek].append(lambda eng, name=name, a=a, kw=kw:
                              getattr(eng, name)(*a, **kw).then_inc(sem, 1))
            tok = (sname, self.cnt[sname])
        else:
            self.q[ek].append(lambda eng, name=name, a=a, kw=kw: getattr(eng, name)(*a, **kw))
            tok = None
        self.note(tok, reads, writes)
        return tok

    def mm(self, lhsT, rhs, out, reads, psum_key, start, stop, force_inc=False):
        self.dep('T', reads, (psum_key,) if start else ())
        if stop or force_inc:
            self.cnt['sT'] += 1
            sem = self.sems['sT']
            self.q['T'].append(lambda eng: eng.matmul(out, lhsT, rhs, start=start, stop=stop,
                                                      skip_group_check=True).then_inc(sem, 1))
            tok = ('sT', self.cnt['sT'])
            self.note(tok, reads, (psum_key,) if stop else ())
            return tok
        self.q['T'].append(lambda eng: eng.matmul(out, lhsT, rhs, start=start, stop=stop,
                                                  skip_group_check=True))
        self.note(None, reads, ())
        return None

    def dma(self, ek, sname, out, in_, reads=(), writes=(), accum_op=None, indirect=None):
        self.dep(ek, reads, writes)
        self.wait(ek, sname, self.cnt[sname])
        sem = self.sems[sname]

        def closure(eng, out=out, in_=in_, accum_op=accum_op, indirect=indirect):
            if indirect is not None:
                instr = eng.indirect_dma_start(out=out, out_offset=None, in_=in_,
                                               in_offset=indirect)
            elif accum_op is not None:
                instr = eng.dma_start(out=out, in_=in_, accum_op=accum_op)
            else:
                instr = eng.dma_start(out=out, in_=in_)
            instr.then_inc(sem, 16)
        self.q[ek].append(closure)
        self.cnt[sname] += 16
        tok = (sname, self.cnt[sname])
        self.note(tok, reads, writes)
        return tok

    def barrier(self):
        snap = list(self.cnt.items())
        for ek in ('Y', 'G', 'T', 'V', 'S'):
            for sname, v in snap:
                self.wait(ek, sname, v)
        self.bufs = {}


def build_program(T=2048, taps=()):
    nc = bass.Bass("TRN2", target_bir_lowering=False, debug=False)
    k = Ctx(nc, T)
    NT, NC = k.NT, k.NC
    _uid = [0]

    def _un(n):
        _uid[0] += 1
        return "%s_u%d" % (n, _uid[0])
    taps = set(taps)

    def dram(name, shape, dtype):
        kind = "ExternalOutput" if name in taps else "Internal"
        return nc.dram_tensor(name, shape, dtype, kind=kind).ap()

    def inp(name, shape, dtype):
        return nc.dram_tensor(name, shape, dtype, kind="ExternalInput").ap()

    x_in = inp("x", [T, E], F32)
    w_qkv = inp("w_qkv", [E, 3 * E], F32)
    w_out0 = inp("w_out0", [E, E], F32)
    ln0_s = inp("ln0_s", [1, 512], F32)
    ln0_b = inp("ln0_b", [1, 512], F32)
    wgT0 = inp("wgT0", [E, FF0], F32)
    wuT0 = inp("wuT0", [E, FF0], F32)
    wdT0 = inp("wdT0", [FF0, E], F32)
    wqT1 = inp("wqT1", [E, E], F32)
    wkT1 = inp("wkT1", [E, E], F32)
    wvT1 = inp("wvT1", [E, E], F32)
    woT1 = inp("woT1", [E, E], F32)
    relpos = inp("relpos", [1023, 64], F32)
    rwT = inp("rwT", [E, NEXP], F32)
    sgw = inp("sgw", [E, 1], F32)
    wg1 = inp("wg1", [NEXP, E, FI], F32)
    wu1 = inp("wu1", [NEXP, E, FI], F32)
    wd1 = inp("wd1", [NEXP, FI, E], F32)
    swg = inp("swg", [E, FSH], F32)
    swu = inp("swu", [E, FSH], F32)
    swd = inp("swd", [FSH, E], F32)
    ropec_d = inp("ropec_d", [T, 256], BF16)
    ropes_d = inp("ropes_d", [T, 256], BF16)
    mask0_d = inp("mask0_d", [128, 4 * 512], BF16)
    bidx_d = inp("bidx_d", [128, 15], I32)

    y_out = nc.dram_tensor("y", [T, E], F32, kind="ExternalOutput").ap()
    res1 = dram("res1", [T, E], F32)
    res2 = dram("res2", [T, E], F32)
    res3 = dram("res3", [T, E], F32)
    pre1 = dram("pre1", [T, E], F32)
    pre2 = dram("pre2", [T, E], F32)
    pre3 = dram("pre3", [T, E], F32)
    QT0 = dram("QT0", [H0, D0, T], BF16)
    KT0 = dram("KT0", [H0, D0, T], BF16)
    V0 = dram("V0", [H0, T, D0], BF16)
    G0 = dram("G0", [T // 128, 128, FF0 // 128, 128], BF16)
    QT1 = dram("QT1", [E, T], BF16)
    KT1 = dram("KT1", [E, T], BF16)
    V1 = dram("V1", [T, E], BF16)
    G1 = dram("G1", [T // 128, 128, FSH // 128, 128], BF16)
    racc = dram("racc", [T, E], F32)
    relm = dram("relm", [1023, 1], F32)
    expB_d = dram("expB_d", [128, 15 * 128], BF16)
    ext_d = nc.dram_tensor("ext", [4095], F32, kind="Internal")
    ext = ext_d.ap()
    ext2d = ext.rearrange("(a b) -> a b", b=1)

    for s in ('sT', 'sV', 'sS', 'sG', 'yA', 'yB', 'yC', 'yD', 'yE', 'yF', 'yO', 'yP',
              'yM', 'gA', 'gB', 'gC', 'gD', 'gM', 'gR', 'gW'):
        k.sem(s)

    st = k.stack
    sb = lambda name, shape, dt: st.enter_context(nc.sbuf_tensor(name, shape, dt))

    BIG1 = sb("BIG1", [128, 32768], BF16)
    BIG2 = sb("BIG2", [128, 32768], BF16)

    ln0sb = sb("ln0sb", [128, 512], F32)
    ln0bb = sb("ln0bb", [128, 512], F32)
    idbf = sb("idbf", [128, 128], BF16)
    onescol = sb("onescol", [128, 1], BF16)
    ones1f = sb("ones1f", [1, 128], F32)
    combine = sb("combine", [128, NT, NEXP], F32)
    sgate = sb("sgate", [128, NT, 1], F32)
    idf32 = sb("idf32", [128, 128], F32)
    mid_stack = ExitStack()
    cLow = mid_stack.enter_context(nc.sbuf_tensor("cLow", [128, 1], F32))
    cHigh = mid_stack.enter_context(nc.sbuf_tensor("cHigh", [128, 1], F32))
    early_stack = ExitStack()
    ropec = early_stack.enter_context(nc.sbuf_tensor("ropec", [128, NT, 256], BF16))
    ropes = early_stack.enter_context(nc.sbuf_tensor("ropes", [128, NT, 256], BF16))

    def XT(et, t0, t1):
        return BIG1[:, et * T + t0: et * T + t1]

    def CTXT(p0, p1, ct, t0, t1):
        return BIG2[p0:p1, ct * T + t0: ct * T + t1]

    if True:
        def emit_ln(zap, zkey, W, outs, sc, p):
            k.op('V', lambda v: v.reduce_sum(sc['s1'], zap, axis=AX.X), (zkey,), (p + 's1',))
            emit_ln_post(zap, zkey, W, outs, sc, p)

        def emit_ln_post(zap, zkey, W, outs, sc, p):
            k.op('S', lambda s: s.activation(sc['sq'], zap, AF.Square, accum_out=sc['s2']),
                 (zkey,), (p + 'sq', p + 's2'))
            k.op('V', lambda v: v.tensor_scalar(sc['m'], sc['s1'], scalar1=1.0 / W, scalar2=None,
                                                op0=ALU.mult), (p + 's1',), (p + 'm',))
            k.op('V', lambda v: v.tensor_scalar(sc['t2'], sc['s2'], scalar1=1.0 / W, scalar2=None,
                                                op0=ALU.mult), (p + 's2',), (p + 't2',))
            k.op('V', lambda v: v.tensor_tensor(sc['var'], sc['m'], sc['m'], op=ALU.mult),
                 (p + 'm',), (p + 'var',))
            k.op('V', lambda v: v.tensor_tensor(sc['var'], sc['t2'], sc['var'], op=ALU.subtract),
                 (p + 't2', p + 'var'), (p + 'var',))
            k.op('V', lambda v: v.tensor_scalar(sc['var'], sc['var'], scalar1=LN_EPS, scalar2=None,
                                                op0=ALU.add), (p + 'var',), (p + 'var',))
            k.op('S', lambda s: s.activation(sc['sd'], sc['var'], AF.Sqrt),
                 (p + 'var',), (p + 'sd',))
            k.op('V', lambda v: v.reciprocal(sc['rstd'], sc['sd']), (p + 'sd',), (p + 'rstd',))
            k.op('V', lambda v: v.scalar_tensor_tensor(sc['nm'], sc['m'], -1.0, sc['rstd'],
                                                       op0=ALU.mult, op1=ALU.mult),
                 (p + 'm', p + 'rstd'), (p + 'nm',))
            for oap, okey in outs:
                k.op('S', lambda s, oap=oap: s.activation(oap, zap, AF.Identity, bias=sc['nm'],
                                                          scale=sc['rstd']),
                     (zkey, p + 'nm', p + 'rstd'), (okey,))

        def mk_lnsc(ph, name, W):
            d = {n: ph.enter_context(nc.sbuf_tensor(_un(name + n), [128, 1], F32))[:]
                 for n in ('s1', 's2', 'm', 't2', 'var', 'sd', 'rstd', 'nm')}
            d['sq'] = ph.enter_context(nc.sbuf_tensor(_un(name + 'sq'), [128, W], F32))[:]
            return d

        with ExitStack() as ph:
            psb = lambda n, sh, dt=F32: ph.enter_context(nc.psum_tensor(_un(n), sh, dt))
            ps_tr = psb("ps_tr", [128, 2, 1024], BF16)
            ps_trf = psb("ps_trf", [128, 512])
            k.dma('Y', 'yM', ropec[:], ropec_d.rearrange("(a p) f -> p a f", p=128), writes=('ropec',))
            k.dma('Y', 'yM', ropes[:], ropes_d.rearrange("(a p) f -> p a f", p=128), writes=('ropes',))
            k.dma('Y', 'yM', ln0sb[:], ln0_s.broadcast_to([128, 512]), writes=('ln0sb',))
            k.dma('Y', 'yM', ln0bb[:], ln0_b.broadcast_to([128, 512]), writes=('ln0bb',))
            k.op('G', lambda g: g.memset(idbf[:], 0.0), (), ('idbf',))
            k.op('G', lambda g: g.affine_select(out=idbf[:], in_=idbf[:], compare_op=ALU.not_equal,
                                                fill=1.0, base=0, pattern=[[-1, 128]],
                                                channel_multiplier=1), ('idbf',), ('idbf',))
            k.op('G', lambda g: g.memset(onescol[:], 1.0), (), ('onescol',))
            k.op('G', lambda g: g.memset(ones1f[:], 1.0), (), ('ones1f',))

            xstage = ph.enter_context(nc.sbuf_tensor(_un("xstage"), [128, 2, E], BF16))
            for tt in range(NT):
                sl = tt % 2
                k.dma('G', 'gA' if sl == 0 else 'gB', xstage[:, sl, :],
                      x_in[tt * 128:(tt + 1) * 128, :], writes=('xs%d' % sl,))
                for et in range(16):
                    pk = 'ps_tr%d' % (et % 2)
                    k.op('T', lambda t: t.transpose(ps_tr[:, et % 2, 0:128],
                                                    xstage[:, sl, et * 128:(et + 1) * 128], idbf[:]),
                         ('xs%d' % sl, 'idbf'), (pk,))
                    k.op('V', lambda v: v.tensor_copy(XT(et, tt * 128, (tt + 1) * 128),
                                                      ps_tr[:, et % 2, 0:128]), (pk,), ('XT',))

            rpbuf = ph.enter_context(nc.sbuf_tensor(_un("rpbuf"), [128, 64], F32))
            rmcol = ph.enter_context(nc.sbuf_tensor(_un("rmcol"), [128, 1], F32))
            relm_toks = []
            for i in range(8):
                rows = 128 if i < 7 else 127
                k.dma('Y', 'yM', rpbuf[0:rows, :], relpos[i * 128:i * 128 + rows, :], writes=('rpbuf',))
                k.op('V', lambda v: v.reduce_sum(rmcol[0:rows, :], rpbuf[0:rows, :], axis=AX.X),
                     ('rpbuf',), ('rmcol',))
                k.op('V', lambda v: v.tensor_scalar(rmcol[0:rows, :], rmcol[0:rows, :], scalar1=1.0 / 64,
                                                    scalar2=None, op0=ALU.mult), ('rmcol',), ('rmcol',))
                k.dma('Y', 'yM', relm[i * 128:i * 128 + rows, :], rmcol[0:rows, :],
                      reads=('rmcol',), writes=('relm',))
            k.dma('Y', 'yM', ext2d[1536:2559, :], relm[:, :], reads=('relm',), writes=('extm',))
            padt = ph.enter_context(nc.sbuf_tensor(_un("padt"), [128, 12], F32))
            k.dma('Y', 'yM', cLow[:], relm[0:1, :].broadcast_to([128, 1]), reads=('relm',), writes=('cLow',))
            k.dma('Y', 'yM', cHigh[:], relm[1022:1023, :].broadcast_to([128, 1]), reads=('relm',),
                  writes=('cHigh',))
            k.op('G', lambda g: g.memset(padt[:], 0.0), (), ('padt',))
            k.op('V', lambda v: v.tensor_scalar(padt[:], padt[:], scalar1=cLow[:], scalar2=None,
                                                op0=ALU.add), ('padt', 'cLow'), ('padt',))
            k.dma('Y', 'yM', AP(ext_d, 0, [[12, 128], [1, 12]]), padt[:],
                  reads=('padt',), writes=('extl',))
            k.op('G', lambda g: g.memset(padt[:], 0.0), (), ('padt',))
            k.op('V', lambda v: v.tensor_scalar(padt[:], padt[:], scalar1=cHigh[:], scalar2=None,
                                                op0=ALU.add), ('padt', 'cHigh'), ('padt',))
            k.dma('Y', 'yM', AP(ext_d, 2559, [[12, 128], [1, 12]]), padt[:],
                  reads=('padt',), writes=('extr',))
            natb = ph.enter_context(nc.sbuf_tensor(_un("natb"), [128, 128], F32))
            btmp = ph.enter_context(nc.sbuf_tensor(_un("btmp"), [128, 128], F32))
            bidx = ph.enter_context(nc.sbuf_tensor(_un("bidx"), [128, 15], I32))
            expstg = ph.enter_context(nc.sbuf_tensor(_un("expstg"), [128, 128], BF16))
            k.dma('Y', 'yM', bidx[:], bidx_d[:, :], writes=('bidx',))
            k.op('G', lambda g: g.memset(idf32[:], 0.0), (), ('idf32',))
            k.op('G', lambda g: g.affine_select(out=idf32[:], in_=idf32[:], compare_op=ALU.not_equal,
                                                fill=1.0, base=0, pattern=[[-1, 128]],
                                                channel_multiplier=1), ('idf32',), ('idf32',))
            for d in range(15):
                k.dma('G', 'gM', natb[:], ext2d[:, :], reads=('bidx', 'extm', 'extl', 'extr'),
                      writes=('natb',), indirect=IndirectOffsetOnAxis(bidx[:, d:d + 1], 0))
                k.op('T', lambda t: t.transpose(ps_trf[:, 0:128], natb[:], idf32[:]),
                     ('natb', 'idf32'), ('ps_trf',))
                k.op('V', lambda v: v.tensor_copy(btmp[:], ps_trf[:, 0:128]), ('ps_trf',), ('btmp',))
                k.op('S', lambda s, d=d: s.activation(expstg[:], btmp[:], AF.Exp),
                     ('btmp', 'expstg'), ('expstg',))
                k.dma('Y', 'yM', expB_d[:, d * 128:(d + 1) * 128], expstg[:], reads=('expstg',))
            k.barrier()

        with ExitStack() as ph:
            psb = lambda n, sh, dt=F32: ph.enter_context(nc.psum_tensor(_un(n), sh, dt))
            ps_mm = psb("ps_mm", [128, 2, 512])
            ps_tr = psb("ps_tr", [128, 2, 1024], BF16)
            wqkvb = ph.enter_context(nc.sbuf_tensor(_un("wqkvb"), [128, 2, 16, 512], BF16))
            ropeb = ph.enter_context(nc.sbuf_tensor(_un("ropeb"), [128, 2, 2, 4, 64], F32))
            lnin = ph.enter_context(nc.sbuf_tensor(_un("lnin"), [128, 2, 4, 128], F32))
            a1 = ph.enter_context(nc.sbuf_tensor(_un("a1"), [128, 2, 4, 128], F32))
            qkbf = ph.enter_context(nc.sbuf_tensor(_un("qkbf"), [128, 2, 4, 128], BF16))
            qst4 = ph.enter_context(nc.sbuf_tensor(_un("qst4"), [128, 2, 4, 128], BF16))
            xev = ph.enter_context(nc.sbuf_tensor(_un("xev"), [128, 2, 512], F32))
            lsc = {n: ph.enter_context(nc.sbuf_tensor(_un("p1l" + n), [128, 2, 4], F32))[:]
                   for n in ('s1', 's2', 'm', 't2', 'var', 'sd', 'rstd', 'nm')}
            pend_tr = []

            def flush_tr():
                for (cc2, tt2, u2) in pend_tr:
                    for j in range(4):
                        trk = 'ps_tr%d' % (j % 2)
                        k.op('T', lambda t, u2=u2, j=j: t.transpose(
                                ps_tr[:, j % 2, 0:128], qkbf[:, u2, j, :], idbf[:]),
                             ('qkbf%d' % u2, 'idbf'), (trk,))
                        k.op('V', lambda v, u2=u2, j=j: v.tensor_copy(
                                qst4[:, u2, j, :], ps_tr[:, j % 2, 0:128]),
                             (trk,), ('qst4%d' % u2,))
                    tgt2 = QT0 if cc2 < 4 else KT0
                    h0 = (cc2 % 4) * 4
                    k.dma('Y', 'yO' if tt2 % 2 == 0 else 'yP',
                          tgt2[h0:h0 + 4, :, tt2 * 128:(tt2 + 1) * 128].rearrange("h d t -> d h t"),
                          qst4[:, u2], reads=('qst4%d' % u2,))
                del pend_tr[:]

            for cc in range(12):
                wsl = cc % 2
                k.dma('G', 'gA' if wsl == 0 else 'gB', wqkvb[:, wsl],
                      w_qkv.rearrange("(a p) c -> p a c", p=128)[:, :, cc * 512:(cc + 1) * 512],
                      writes=('wqkv%d' % wsl,))
                is_v = cc >= 8
                for tt in range(NT):
                    pk = 'ps_mm%d' % (tt % 2)
                    for et in range(16):
                        k.mm(XT(et, tt * 128, (tt + 1) * 128), wqkvb[:, wsl, et, :],
                             ps_mm[:, tt % 2, :], ('XT', 'wqkv%d' % wsl), pk,
                             start=(et == 0), stop=(et == 15))
                    flush_tr()
                    if is_v:
                        u = tt % 2
                        vst = qkbf[:, u].rearrange("p a b -> p (a b)")
                        k.op('V', lambda v, u=u, vst=vst: v.tensor_copy(vst, ps_mm[:, tt % 2, :]),
                             (pk, 'qkbf%d' % u), ('qkbf%d' % u,))
                        for j in range(4):
                            h = (cc - 8) * 4 + j
                            k.dma('Y', 'yO' if j % 2 == 0 else 'yP',
                                  V0.rearrange("h (a p) d -> h p a d", p=128)[h, :, tt, :],
                                  vst[:, j * 128:(j + 1) * 128], reads=('qkbf%d' % u,))
                    else:
                        u = tt % 2
                        rc4 = ropec[:, tt, :].rearrange("p (j f) -> p j f", f=64)
                        rs4 = ropes[:, tt, :].rearrange("p (j f) -> p j f", f=64)
                        xkey = 'xev%d' % u
                        k.op('S', lambda s, u=u: s.activation(xev[:, u, :], ps_mm[:, tt % 2, :],
                                                              AF.Identity), (pk,), (xkey,))
                        pmm = xev[:, u, :]
                        q1v = pmm.rearrange("p (j f) -> p j f", f=128)[:, :, 0:64]
                        q2v = pmm.rearrange("p (j f) -> p j f", f=128)[:, :, 64:128]
                        rb = lambda i: ropeb[:, u, i % 2, :, :]
                        lniu = lnin[:, u]
                        k.op('V', lambda v: v.tensor_tensor(rb(0), q1v, rc4, op=ALU.mult),
                             (xkey, 'ropec'), ('rb0%d' % u,))
                        k.op('V', lambda v: v.tensor_tensor(rb(1), q2v, rs4, op=ALU.mult),
                             (xkey, 'ropes'), ('rb1%d' % u,))
                        k.op('V', lambda v: v.tensor_tensor(lniu[:, :, 0:64], rb(0), rb(1),
                                                            op=ALU.subtract),
                             ('rb0%d' % u, 'rb1%d' % u), ('lnin%d' % u,))
                        k.op('V', lambda v: v.tensor_tensor(rb(2), q1v, rs4, op=ALU.mult),
                             (xkey, 'ropes'), ('rb0%d' % u,))
                        k.op('V', lambda v: v.tensor_tensor(rb(3), q2v, rc4, op=ALU.mult),
                             (xkey, 'ropec'), ('rb1%d' % u,))
                        k.op('V', lambda v: v.tensor_tensor(lniu[:, :, 64:128], rb(2), rb(3),
                                                            op=ALU.add),
                             ('rb0%d' % u, 'rb1%d' % u), ('lnin%d' % u,))
                        for j in range(4):
                            k.op('S', lambda s, j=j, u=u: s.activation(
                                    a1[:, u, j, :], lniu[:, j, :], AF.Square,
                                    accum_out=lsc['s2'][:, u, j:j + 1]),
                                 ('lnin%d' % u, 'a1%d' % u), ('ls2%d' % u, 'a1%d' % u))
                        k.op('V', lambda v, u=u: v.reduce_sum(lsc['s1'][:, u], lniu[:],
                                                              axis=AX.X), ('lnin%d' % u,), ('ls1%d' % u,))
                        k.op('V', lambda v, u=u: v.tensor_scalar(lsc['m'][:, u], lsc['s1'][:, u],
                                                                 scalar1=1.0 / D0, scalar2=None,
                                                                 op0=ALU.mult), ('ls1%d' % u,), ('lm%d' % u,))
                        k.op('V', lambda v, u=u: v.tensor_scalar(lsc['t2'][:, u], lsc['s2'][:, u],
                                                                 scalar1=1.0 / D0, scalar2=None,
                                                                 op0=ALU.mult), ('ls2%d' % u,), ('lt2%d' % u,))
                        k.op('V', lambda v, u=u: v.tensor_tensor(lsc['var'][:, u], lsc['m'][:, u],
                                                                 lsc['m'][:, u], op=ALU.mult),
                             ('lm%d' % u,), ('lvar%d' % u,))
                        k.op('V', lambda v, u=u: v.tensor_tensor(lsc['var'][:, u], lsc['t2'][:, u],
                                                                 lsc['var'][:, u], op=ALU.subtract),
                             ('lt2%d' % u, 'lvar%d' % u), ('lvar%d' % u,))
                        k.op('V', lambda v, u=u: v.tensor_scalar(lsc['var'][:, u], lsc['var'][:, u],
                                                                 scalar1=LN_EPS, scalar2=None,
                                                                 op0=ALU.add), ('lvar%d' % u,), ('lvar%d' % u,))
                        k.op('S', lambda s, u=u: s.activation(lsc['sd'][:, u], lsc['var'][:, u],
                                                              AF.Sqrt), ('lvar%d' % u,), ('lsd%d' % u,))
                        k.op('V', lambda v, u=u: v.reciprocal(lsc['rstd'][:, u], lsc['sd'][:, u]),
                             ('lsd%d' % u,), ('lrstd%d' % u,))
                        k.op('V', lambda v, u=u: v.scalar_tensor_tensor(lsc['nm'][:, u], lsc['m'][:, u],
                                                                        -1.0, lsc['rstd'][:, u],
                                                                        op0=ALU.mult, op1=ALU.mult),
                             ('lm%d' % u, 'lrstd%d' % u), ('lnm%d' % u,))
                        for j in range(4):
                            k.op('S', lambda s, j=j, u=u: s.activation(
                                    a1[:, u, j, :], lniu[:, j, :], AF.Identity,
                                    bias=lsc['nm'][:, u, j:j + 1],
                                    scale=lsc['rstd'][:, u, j:j + 1]),
                                 ('lnin%d' % u, 'lnm%d' % u, 'lrstd%d' % u), ('a1%d' % u,))
                        a1v = a1[:, u].rearrange("p j f -> p (j f)")
                        qkv_ = qkbf[:, u].rearrange("p j f -> p (j f)")
                        k.op('V', lambda v, a1v=a1v: v.tensor_tensor(a1v, a1v, ln0sb[:], op=ALU.mult),
                             ('a1%d' % u, 'ln0sb'), ('a1%d' % u,))
                        k.op('V', lambda v, a1v=a1v, qkv_=qkv_: v.tensor_tensor(qkv_, a1v, ln0bb[:],
                                                                                op=ALU.add),
                             ('a1%d' % u, 'ln0bb', 'qkbf%d' % u), ('qkbf%d' % u,))
                        pend_tr.append((cc, tt, u))
                flush_tr()
            k.barrier()

        early_stack.close()

        with ExitStack() as ph:
            psb = lambda n, sh, dt=F32: ph.enter_context(nc.psum_tensor(_un(n), sh, dt))
            ps_s = psb("ps_s", [128, 4, 512])
            ps_ctx = psb("ps_ctx", [128, 2, 512])
            ps_den = psb("ps_den", [128, 512])
            ps_bc = psb("ps_bc", [128, 512])
            mask0 = ph.enter_context(nc.sbuf_tensor(_un("mask0"), [128, 4, 512], BF16))
            k.dma('Y', 'yM', mask0[:].rearrange("p a b -> p (a b)"), mask0_d[:, :], writes=('mask0',))
            qth = ph.enter_context(nc.sbuf_tensor(_un("qth"), [128, 2, 2, T], BF16))
            kth = ph.enter_context(nc.sbuf_tensor(_un("kth"), [128, 2, 2, T], BF16))
            vth = ph.enter_context(nc.sbuf_tensor(_un("vth"), [128, 2, 2, NT, 128], BF16))
            sqb = ph.enter_context(nc.sbuf_tensor(_un("sqb"), [128, 4, 512], F32))
            pbuf = ph.enter_context(nc.sbuf_tensor(_un("pbuf"), [128, 4, 512], BF16))
            recb = ph.enter_context(nc.sbuf_tensor(_un("recb"), [1, 2, 512], F32))
            bcb = ph.enter_context(nc.sbuf_tensor(_un("bcb"), [128, 2, 512], F32))
            for hp in range(H0 // 2):
                sl = hp % 2
                for hh in range(2):
                    h = 2 * hp + hh
                    k.dma('Y', ('yA', 'yB', 'yC', 'yD')[2 * sl + hh], qth[:, sl, hh, :],
                          QT0[h, :, :], writes=('qth%d%d' % (sl, hh),))
                    k.dma('Y', ('yC', 'yD', 'yA', 'yB')[2 * sl + hh], kth[:, sl, hh, :],
                          KT0[h, :, :], writes=('kth%d%d' % (sl, hh),))
                    k.dma('Y', 'yE' if sl == 0 else 'yF', vth[:, sl, hh],
                          V0.rearrange("h (a p) d -> h p a d", p=128)[h],
                          writes=('vth%d%d' % (sl, hh),))

                def mm1(hh, qc, kt):
                    sl_ = hp % 2
                    k.mm(kth[:, sl_, hh, kt * 128:(kt + 1) * 128],
                         qth[:, sl_, hh, qc * 512:(qc + 1) * 512],
                         ps_s[:, 2 * hh + kt % 2, :],
                         ('kth%d%d' % (sl_, hh), 'qth%d%d' % (sl_, hh)),
                         'ps_s%d' % (2 * hh + kt % 2), True, True)

                def chain(hh, qc, kt, nkt):
                    sl_ = hp % 2
                    pslot = 2 * hh + kt % 2
                    skey = 'ps_s%d' % pslot
                    pkk = 'pb%d' % pslot
                    k.op('S', lambda s, pslot=pslot: s.activation(
                            sqb[:, pslot, :], ps_s[:, pslot, :], AF.Square),
                         (skey,), ('sqb%d' % pslot,))
                    k.op('V', lambda v, pslot=pslot: v.tensor_tensor(
                            pbuf[:, pslot, :], sqb[:, pslot, :], sqb[:, pslot, :],
                            op=ALU.mult), ('sqb%d' % pslot,), (pkk,))
                    if kt >= 4 * qc:
                        k.op('V', lambda v, kt=kt, qc=qc, pslot=pslot: v.tensor_tensor(
                                pbuf[:, pslot, :], pbuf[:, pslot, :],
                                mask0[:, kt - 4 * qc, :], op=ALU.mult),
                             (pkk, 'mask0'), (pkk,))
                    k.mm(vth[:, sl_, hh, kt, :], pbuf[:, pslot, :], ps_ctx[:, hh, :],
                         ('vth%d%d' % (sl_, hh), pkk), 'ps_ctx%d' % hh,
                         start=(kt == 0), stop=(kt == nkt - 1), force_inc=True)
                    k.mm(onescol[:], pbuf[:, pslot, :], ps_den[64 * hh:64 * hh + 1, :],
                         ('onescol', pkk), 'ps_den%d' % hh,
                         start=(kt == 0), stop=(kt == nkt - 1), force_inc=True)

                for qc in range(NC):
                    nkt = 4 * qc + 4
                    for hh in range(2):
                        mm1(hh, qc, 0)
                    for kt in range(nkt):
                        for hh in range(2):
                            if kt + 1 < nkt:
                                mm1(hh, qc, kt + 1)
                            chain(hh, qc, kt, nkt)
                    for hh in range(2):
                        h = 2 * hp + hh
                        ck = 'ps_ctx%d' % hh
                        k.op('V', lambda v, hh=hh: v.reciprocal(recb[:, hh, :],
                                                                ps_den[64 * hh:64 * hh + 1, :]),
                             ('ps_den0', 'ps_den1'), ('recb%d' % hh,))
                        k.op('T', lambda t, hh=hh: t.matmul(ps_bc[:, :], ones1f[:],
                                                            recb[:, hh, :], start=True, stop=True),
                             ('ones1f', 'recb%d' % hh), ('ps_bc',))
                        k.op('S', lambda s, hh=hh: s.copy(bcb[:, hh, :], ps_bc[:, :]),
                             ('ps_bc',), ('bcb%d' % hh,))
                        k.op('V', lambda v, h=h, hh=hh, qc=qc: v.tensor_tensor(
                                CTXT(0, 128, h, qc * 512, (qc + 1) * 512),
                                ps_ctx[:, hh, :], bcb[:, hh, :], op=ALU.mult),
                             (ck, 'bcb%d' % hh), ('CTXT',))
            k.barrier()

        def outproj(wsrc, out_dram, wname):
            with ExitStack() as ph:
                psb = lambda n, sh, dt=F32: ph.enter_context(nc.psum_tensor(_un(n), sh, dt))
                ps_mm = psb("ps_mmo", [128, 2, 4, 512])
                est2 = ph.enter_context(nc.sbuf_tensor(_un("esto"), [128, 2, 4, 512], F32))
                k.dma('G', 'gM', BIG1[:, 0:16 * E].rearrange("p (a b) -> p a b", b=E),
                      wsrc.rearrange("(a p) c -> p a c", p=128), writes=('woB',))
                for tt in range(NT):
                    for ct in range(16):
                        for ec in range(4):
                            k.mm(CTXT(0, 128, ct, tt * 128, (tt + 1) * 128),
                                 BIG1[:, ct * E + ec * 512: ct * E + (ec + 1) * 512],
                                 ps_mm[:, tt % 2, ec, :], ('CTXT', 'woB'),
                                 'ps_mmo%d_%d' % (tt % 2, ec), start=(ct == 0), stop=(ct == 15))
                    for ec in range(4):
                        pk = 'ps_mmo%d_%d' % (tt % 2, ec)
                        ekey = 'esto%d_%d' % (tt % 2, ec)
                        k.op('V', lambda v, tt=tt, ec=ec: v.tensor_copy(
                                est2[:, tt % 2, ec, :], ps_mm[:, tt % 2, ec, :]), (pk,), (ekey,))
                        k.dma('Y', 'yO' if ec % 2 == 0 else 'yP',
                              out_dram[tt * 128:(tt + 1) * 128, ec * 512:(ec + 1) * 512],
                              est2[:, tt % 2, ec, :], reads=(ekey,))
                k.barrier()

        outproj(w_out0, pre1, 'wo0')

        def boundary(res_in, pre_in, res_out, pname, final=False):
            with ExitStack() as ph:
                psb = lambda n, sh, dt=F32: ph.enter_context(nc.psum_tensor(_un(n), sh, dt))
                ps_tr = psb("ps_trb", [128, 2, 1024], BF16)
                bz = ph.enter_context(nc.sbuf_tensor(_un("bz"), [128, 2, E], F32))
                br = ph.enter_context(nc.sbuf_tensor(_un("br"), [128, 2, E], F32))
                bzf = ph.enter_context(nc.sbuf_tensor(_un("bzf"), [128, E], F32))
                bzb = ph.enter_context(nc.sbuf_tensor(_un("bzb"), [128, E], BF16))
                sc = mk_lnsc(ph, "bln", E)
                for tt in range(NT):
                    sl = tt % 2
                    k.dma('Y', 'yA' if sl == 0 else 'yB', br[:, sl, :],
                          res_in[tt * 128:(tt + 1) * 128, :], writes=('br%d' % sl,))
                    k.dma('Y', 'yC' if sl == 0 else 'yD', bz[:, sl, :],
                          pre_in[tt * 128:(tt + 1) * 128, :], writes=('bz%d' % sl,))
                    zkey = 'bz%d' % sl
                    k.op('V', lambda v, sl=sl: v.tensor_tensor(bz[:, sl, :], bz[:, sl, :],
                                                               br[:, sl, :], op=ALU.add),
                         (zkey, 'br%d' % sl), (zkey,))
                    emit_ln(bz[:, sl, :], zkey, E, [(bzf[:], 'bzf')], sc, pname)
                    k.dma('Y', 'yO', res_out[tt * 128:(tt + 1) * 128, :], bzf[:], reads=('bzf',))
                    if not final:
                        k.op('V', lambda v: v.tensor_copy(bzb[:], bzf[:]), ('bzf',), ('bzb',))
                        for et in range(16):
                            trk = 'ps_trb%d' % (et % 2)
                            k.op('T', lambda t: t.transpose(ps_tr[:, et % 2, 0:128],
                                                            bzb[:, et * 128:(et + 1) * 128], idbf[:]),
                                 ('bzb', 'idbf'), (trk,))
                            k.op('V', lambda v: v.tensor_copy(XT(et, tt * 128, (tt + 1) * 128),
                                                              ps_tr[:, et % 2, 0:128]), (trk,), ('XT',))
                k.barrier()

        boundary(x_in, pre1, res1, 'b1.')

        def swiglu_A(wg_src, wu_src, n_it, G_dram):
            with ExitStack() as ph:
                psb = lambda n, sh, dt=F32: ph.enter_context(nc.psum_tensor(_un(n), sh, dt))
                ps_g = psb("ps_g", [128, NC, 512])
                ps_u = psb("ps_u", [128, NC, 512])
                gwb = ph.enter_context(nc.sbuf_tensor(_un("gwb"), [128, 2, 16, 128], BF16))
                uwb = ph.enter_context(nc.sbuf_tensor(_un("uwb"), [128, 2, 16, 128], BF16))
                sgb = ph.enter_context(nc.sbuf_tensor(_un("sgb"), [128, NC, 512], F32))
                gst = ph.enter_context(nc.sbuf_tensor(_un("gst"), [128, NC, 512], BF16))
                for it in range(n_it):
                    wsl = it % 2
                    k.dma('G', 'gA' if wsl == 0 else 'gB', gwb[:, wsl],
                          wg_src.rearrange("(a p) c -> p a c", p=128)[:, :, it * 128:(it + 1) * 128],
                          writes=('gw%d' % wsl,))
                    k.dma('G', 'gC' if wsl == 0 else 'gD', uwb[:, wsl],
                          wu_src.rearrange("(a p) c -> p a c", p=128)[:, :, it * 128:(it + 1) * 128],
                          writes=('uw%d' % wsl,))
                    for et in range(16):
                        for tc in range(NC):
                            k.mm(gwb[:, wsl, et, :], XT(et, tc * 512, (tc + 1) * 512),
                                 ps_g[:, tc, :], ('gw%d' % wsl, 'XT'), 'ps_g%d' % tc,
                                 start=(et == 0), stop=(et == 15))
                        for tc in range(NC):
                            k.mm(uwb[:, wsl, et, :], XT(et, tc * 512, (tc + 1) * 512),
                                 ps_u[:, tc, :], ('uw%d' % wsl, 'XT'), 'ps_u%d' % tc,
                                 start=(et == 0), stop=(et == 15))
                    for tc in range(NC):
                        k.op('S', lambda s, tc=tc: s.activation(sgb[:, tc, :], ps_g[:, tc, :], AF.Silu),
                             ('ps_g%d' % tc,), ('sgb%d' % tc,))
                        k.op('V', lambda v, tc=tc: v.tensor_tensor(gst[:, tc, :], sgb[:, tc, :],
                                                                   ps_u[:, tc, :], op=ALU.mult),
                             ('sgb%d' % tc, 'ps_u%d' % tc), ('gst',))
                    Gv = G_dram.rearrange("tt p i w -> p tt i w")
                    for tc in range(NC):
                        k.dma('Y', 'yO' if (it + tc) % 2 == 0 else 'yP',
                              Gv[:, tc * 4:(tc + 1) * 4, it, :],
                              gst[:, tc, :].rearrange("p (a b) -> p a b", b=128),
                              reads=('gst',))
                k.barrier()

        def down_B(G_dram, n_it, wd_src, out_dram, scale_col=None, accum=False):
            with ExitStack() as ph:
                psb = lambda n, sh, dt=F32: ph.enter_context(nc.psum_tensor(_un(n), sh, dt))
                ps_mm = psb("ps_mmd", [128, 2, 2, 512])
                gtb = ph.enter_context(nc.sbuf_tensor(_un("gtb"), [128, 2, n_it, 128], BF16))
                est2 = ph.enter_context(nc.sbuf_tensor(_un("est2"), [128, 2, 2, 512], F32))
                for pas in range(2):
                    for eco in range(2):
                        dst = BIG1 if eco == 0 else BIG2
                        k.dma('G', 'gA' if eco == 0 else 'gB',
                              dst[:, 0:n_it * 512].rearrange("p (a b) -> p a b", b=512),
                              wd_src.rearrange("(a p) c -> p a c", p=128)[:, :, (2 * pas + eco) * 512:(2 * pas + eco + 1) * 512],
                              writes=('dW%d' % eco,))
                    for tt in range(NT):
                        tsl = tt % 2
                        k.dma('Y', 'yA' if tsl == 0 else 'yB', gtb[:, tsl],
                              G_dram[tt], writes=('gtb%d' % tsl,))
                        for it in range(n_it):
                            for eco in range(2):
                                dst = BIG1 if eco == 0 else BIG2
                                k.mm(gtb[:, tsl, it, :], dst[:, it * 512:(it + 1) * 512],
                                     ps_mm[:, tt % 2, eco, :], ('gtb%d' % tsl, 'dW%d' % eco),
                                     'ps_mmd%d_%d' % (tt % 2, eco),
                                     start=(it == 0), stop=(it == n_it - 1))
                        for eco in range(2):
                            ec = 2 * pas + eco
                            pk = 'ps_mmd%d_%d' % (tt % 2, eco)
                            ekey = 'est2_%d_%d' % (tt % 2, eco)
                            if scale_col is None:
                                k.op('V', lambda v, tt=tt, eco=eco: v.tensor_copy(
                                        est2[:, tt % 2, eco, :], ps_mm[:, tt % 2, eco, :]),
                                     (pk,), (ekey,))
                            else:
                                k.op('V', lambda v, tt=tt, eco=eco: v.tensor_scalar(
                                        est2[:, tt % 2, eco, :], ps_mm[:, tt % 2, eco, :],
                                        scalar1=scale_col(tt), scalar2=None, op0=ALU.mult),
                                     (pk, 'sgate'), (ekey,))
                            if accum:
                                k.dma('G', 'gR', out_dram[tt * 128:(tt + 1) * 128, ec * 512:(ec + 1) * 512],
                                      est2[:, tt % 2, eco, :], reads=(ekey,), accum_op=ALU.add)
                            else:
                                k.dma('Y', 'yO' if eco == 0 else 'yP',
                                      out_dram[tt * 128:(tt + 1) * 128, ec * 512:(ec + 1) * 512],
                                      est2[:, tt % 2, eco, :], reads=(ekey,))
                k.barrier()

        swiglu_A(wgT0, wuT0, FF0 // 128, G0)
        down_B(G0, FF0 // 128, wdT0, pre2)
        boundary(res1, pre2, res2, 'b2.')

        with ExitStack() as ph:
            psb = lambda n, sh, dt=F32: ph.enter_context(nc.psum_tensor(_un(n), sh, dt))
            ps_mm = psb("ps_mmp", [128, 2, max(NC, 4), 512])
            wpb = ph.enter_context(nc.sbuf_tensor(_un("wpb"), [128, 2, 16, 128], BF16))
            pstage = ph.enter_context(nc.sbuf_tensor(_un("pstage"), [128, 2, NC, 512], BF16))
            for (wsrc, tgt, nm) in ((wqT1, QT1, 'q'), (wkT1, KT1, 'k')):
                for ct in range(16):
                    wsl = ct % 2
                    wk = 'wpb%d' % wsl
                    k.dma('G', 'gA' if wsl == 0 else 'gB', wpb[:, wsl],
                          wsrc.rearrange("(a p) c -> p a c", p=128)[:, :, ct * 128:(ct + 1) * 128],
                          writes=(wk,))
                    for et in range(16):
                        for tc in range(NC):
                            k.mm(wpb[:, wsl, et, :], XT(et, tc * 512, (tc + 1) * 512),
                                 ps_mm[:, ct % 2, tc, :], (wk, 'XT'), 'ps_mmp%d_%d' % (ct % 2, tc),
                                 start=(et == 0), stop=(et == 15))
                    for tc in range(NC):
                        k.op('V', lambda v, ct=ct, tc=tc: v.tensor_copy(
                                pstage[:, ct % 2, tc, :], ps_mm[:, ct % 2, tc, :]),
                             ('ps_mmp%d_%d' % (ct % 2, tc),), ('pstage%d' % (ct % 2),))
                    k.dma('Y', 'yO' if ct % 2 == 0 else 'yP', tgt[ct * 128:(ct + 1) * 128, :],
                          pstage[:, ct % 2].rearrange("p a b -> p (a b)"),
                          reads=('pstage%d' % (ct % 2),))
            k.dma('G', 'gM', BIG2[:, 0:16 * E].rearrange("p (a b) -> p a b", b=E),
                  wvT1.rearrange("(a p) c -> p a c", p=128), writes=('wvB',))
            for tt in range(NT):
                for et in range(16):
                    for cc in range(4):
                        k.mm(XT(et, tt * 128, (tt + 1) * 128),
                             BIG2[:, et * E + cc * 512: et * E + (cc + 1) * 512],
                             ps_mm[:, tt % 2, cc, :], ('XT', 'wvB'), 'ps_mmp%d_%d' % (tt % 2, cc),
                             start=(et == 0), stop=(et == 15))
                for cc in range(4):
                    pk = 'ps_mmp%d_%d' % (tt % 2, cc)
                    k.op('V', lambda v, tt=tt, cc=cc: v.tensor_copy(
                            pstage[:, tt % 2, 0, :] if NC == 1 else pstage[:, tt % 2, cc % NC, :],
                            ps_mm[:, tt % 2, cc, :]), (pk,), ('pstage%d' % (tt % 2),))
                    k.dma('Y', 'yO' if cc % 2 == 0 else 'yP',
                          V1[tt * 128:(tt + 1) * 128, cc * 512:(cc + 1) * 512],
                          pstage[:, tt % 2, 0, :] if NC == 1 else pstage[:, tt % 2, cc % NC, :],
                          reads=('pstage%d' % (tt % 2),))
            k.barrier()

        with ExitStack() as ph:
            psb = lambda n, sh, dt=F32: ph.enter_context(nc.psum_tensor(_un(n), sh, dt))
            ps_s = psb("ps_s1", [128, 4, 512])
            ps_ctx = psb("ps_ctx1", [128, 2, 512])
            ps_bc = psb("ps_bc1", [128, 512])
            expBc = ph.enter_context(nc.sbuf_tensor(_un("expBc"), [128, 15 * 128], BF16))
            k.dma('Y', 'yM', expBc[:], expB_d[:, :], writes=('expBc',))
            qb1 = ph.enter_context(nc.sbuf_tensor(_un("qb1"), [128, 2, T], BF16))
            kb1 = ph.enter_context(nc.sbuf_tensor(_un("kb1"), [128, 2, T], BF16))
            vx1 = ph.enter_context(nc.sbuf_tensor(_un("vx1"), [128, 2, 2, NT, 65], BF16))
            pe1 = ph.enter_context(nc.sbuf_tensor(_un("pe1"), [128, 4, 512], BF16))
            pb1 = ph.enter_context(nc.sbuf_tensor(_un("pb1"), [128, 4, 512], BF16))
            rec1 = ph.enter_context(nc.sbuf_tensor(_un("rec1"), [1, 2, 512], F32))
            bcb1 = ph.enter_context(nc.sbuf_tensor(_un("bcb1"), [64, 2, 512], F32))
            k.op('G', lambda g: g.memset(vx1[:, :, :, :, 64:65], 1.0), (), ('vxones',))
            for hp in range(H1 // 2):
                sl = hp % 2
                for hh in range(2):
                    h = 2 * hp + hh
                    p0 = 64 * hh
                    k.dma('Y', ('yA', 'yB', 'yC', 'yD')[2 * sl + hh], qb1[p0:p0 + 64, sl, :],
                          QT1.rearrange("(h d) t -> h d t", d=64)[h], writes=('qb1_%d%d' % (sl, hh),))
                    k.dma('Y', ('yC', 'yD', 'yA', 'yB')[2 * sl + hh], kb1[p0:p0 + 64, sl, :],
                          KT1.rearrange("(h d) t -> h d t", d=64)[h], writes=('kb1_%d%d' % (sl, hh),))
                    k.dma('Y', 'yE' if sl == 0 else 'yF', vx1[:, sl, hh, :, 0:64],
                          V1.rearrange("(a p) c -> p a c", p=128)[:, :, h * 64:(h + 1) * 64],
                          writes=('vx1_%d%d' % (sl, hh),))

                def mm1(hh, qc, kt, sl=None):
                    sl_ = hp % 2
                    p0 = 64 * hh
                    k.mm(kb1[p0:p0 + 64, sl_, kt * 128:(kt + 1) * 128],
                         qb1[p0:p0 + 64, sl_, qc * 512:(qc + 1) * 512],
                         ps_s[:, 2 * hh + kt % 2, :],
                         ('kb1_%d%d' % (sl_, hh), 'qb1_%d%d' % (sl_, hh)),
                         'ps_s1%d' % (2 * hh + kt % 2), True, True)

                def chain(hh, qc, kt):
                    sl_ = hp % 2
                    skey = 'ps_s1%d' % (2 * hh + kt % 2)
                    dc = kt - 4 * qc
                    pslot = 2 * hh + kt % 2
                    pkk = 'pb1%d' % pslot
                    if -4 <= dc <= 7:
                        k.op('S', lambda s, pslot=pslot: s.activation(
                                pe1[:, pslot, :], ps_s[:, pslot, :], AF.Exp, scale=0.125),
                             (skey,), ('pe1%d' % pslot,))
                        k.op('V', lambda v, pslot=pslot, dc=dc: v.tensor_tensor(
                                pb1[:, pslot, :], pe1[:, pslot, :],
                                expBc[:, (7 - dc) * 128:(7 - dc) * 128 + 512], op=ALU.mult),
                             ('pe1%d' % pslot, 'expBc'), (pkk,))
                    else:
                        ccol, ckn = (cHigh, 'cHigh') if dc > 7 else (cLow, 'cLow')
                        k.op('S', lambda s, ccol=ccol, pslot=pslot: s.activation(
                                pb1[:, pslot, :], ps_s[:, pslot, :], AF.Exp, scale=0.125,
                                bias=ccol[:]), (skey, ckn), (pkk,))
                    k.mm(vx1[:, sl_, hh, kt, :], pb1[:, pslot, :], ps_ctx[0:65, hh, :],
                         ('vx1_%d%d' % (sl_, hh), 'vxones', pkk), 'ps_ctx1%d' % hh,
                         start=(kt == 0), stop=(kt == NT - 1), force_inc=True)

                for qc in range(NC):
                    for hh in range(2):
                        mm1(hh, qc, 0)
                    for kt in range(NT):
                        for hh in range(2):
                            if kt + 1 < NT:
                                mm1(hh, qc, kt + 1)
                            chain(hh, qc, kt)
                    for hh in range(2):
                        h = 2 * hp + hh
                        ck = 'ps_ctx1%d' % hh
                        k.op('V', lambda v, hh=hh: v.reciprocal(rec1[:, hh, :],
                                                                ps_ctx[64:65, hh, :]),
                             (ck,), ('rec1%d' % hh,))
                        k.op('T', lambda t, hh=hh: t.matmul(ps_bc[0:64, :], ones1f[0:1, 0:64],
                                                            rec1[:, hh, :], start=True, stop=True),
                             ('ones1f', 'rec1%d' % hh), ('ps_bc1',))
                        k.op('S', lambda s, hh=hh: s.copy(bcb1[:, hh, :], ps_bc[0:64, :]),
                             ('ps_bc1',), ('bcb1%d' % hh,))
                        k.op('V', lambda v, h=h, hh=hh, qc=qc: v.tensor_tensor(
                                CTXT(64 * (h % 2), 64 * (h % 2) + 64, h // 2, qc * 512,
                                     (qc + 1) * 512),
                                ps_ctx[0:64, hh, :], bcb1[:, hh, :], op=ALU.mult),
                             (ck, 'bcb1%d' % hh), ('CTXT',))
            k.barrier()

        mid_stack.close()
        outproj(woT1, pre3, 'wo1')
        boundary(res2, pre3, res3, 'b3.')

        with ExitStack() as ph:
            psb = lambda n, sh, dt=F32: ph.enter_context(nc.psum_tensor(_un(n), sh, dt))
            ps_r = psb("ps_r", [128, 512])
            ps_sg = psb("ps_sg", [128, 512])
            rwb = ph.enter_context(nc.sbuf_tensor(_un("rwb"), [128, 16, NEXP], F32))
            xr32 = ph.enter_context(nc.sbuf_tensor(_un("xr32"), [128, 2, E], F32))
            xt32 = ph.enter_context(nc.sbuf_tensor(_un("xt32"), [128, 16, 128], F32))
            sgwb = ph.enter_context(nc.sbuf_tensor(_un("sgwb"), [128, 16, 1], BF16))
            lgb = ph.enter_context(nc.sbuf_tensor(_un("lgb"), [128, NEXP], F32))
            mskb = ph.enter_context(nc.sbuf_tensor(_un("mskb"), [128, NEXP], F32))
            eq1b = ph.enter_context(nc.sbuf_tensor(_un("eq1b"), [128, NEXP], F32))
            eq2b = ph.enter_context(nc.sbuf_tensor(_un("eq2b"), [128, NEXP], F32))
            scs = {n: ph.enter_context(nc.sbuf_tensor(_un("rt" + n), [128, 1], F32))[:]
                   for n in ('m1', 'm2', 'dl', 'w1', 'w2')}
            k.dma('Y', 'yM', rwb[:], rwT.rearrange("(a p) c -> p a c", p=128), writes=('rwb',))
            for et in range(16):
                k.dma('G', 'gM', sgwb[:, et, :], sgw[et * 128:(et + 1) * 128, :], writes=('sgwb',))
            ps_rt = psb("ps_rt", [128, 512])
            for tt in range(NT):
                sl = tt % 2
                k.dma('Y', 'yA' if sl == 0 else 'yB', xr32[:, sl, :],
                      res3[tt * 128:(tt + 1) * 128, :], writes=('xr32%d' % sl,))
                for et in range(16):
                    k.op('T', lambda t, sl=sl, et=et: t.transpose(
                            ps_rt[:, 0:128], xr32[:, sl, et * 128:(et + 1) * 128], idf32[:]),
                         ('xr32%d' % sl, 'idf32'), ('ps_rt',))
                    k.op('V', lambda v, et=et: v.tensor_copy(xt32[:, et, :], ps_rt[:, 0:128]),
                         ('ps_rt',), ('xt32',))
                for et in range(16):
                    k.mm(xt32[:, et, :], rwb[:, et, :], ps_r[:, 0:NEXP],
                         ('xt32', 'rwb'), 'ps_r', start=(et == 0), stop=(et == 15))
                for et in range(16):
                    k.mm(XT(et, tt * 128, (tt + 1) * 128), sgwb[:, et, :], ps_sg[:, 0:1],
                         ('XT', 'sgwb'), 'ps_sg', start=(et == 0), stop=(et == 15))
                k.op('V', lambda v: v.tensor_copy(lgb[:], ps_r[:, 0:NEXP]), ('ps_r',), ('lgb',))
                k.op('S', lambda s, tt=tt: s.activation(sgate[:, tt, :], ps_sg[:, 0:1], AF.Sigmoid),
                     ('ps_sg',), ('sgate',))
                k.op('V', lambda v: v.reduce_max(scs['m1'], lgb[:], axis=AX.X), ('lgb',), ('m1',))
                k.op('V', lambda v: v.tensor_scalar(eq1b[:], lgb[:], scalar1=scs['m1'], scalar2=None,
                                                    op0=ALU.is_equal), ('lgb', 'm1'), ('eq1b',))
                k.op('V', lambda v: v.scalar_tensor_tensor(mskb[:], eq1b[:], -1e30, lgb[:],
                                                           op0=ALU.mult, op1=ALU.add),
                     ('eq1b', 'lgb'), ('mskb',))
                k.op('V', lambda v: v.reduce_max(scs['m2'], mskb[:], axis=AX.X), ('mskb',), ('m2',))
                k.op('V', lambda v: v.tensor_scalar(eq2b[:], mskb[:], scalar1=scs['m2'], scalar2=None,
                                                    op0=ALU.is_equal), ('mskb', 'm2'), ('eq2b',))
                k.op('V', lambda v: v.tensor_tensor(scs['dl'], scs['m1'], scs['m2'], op=ALU.subtract),
                     ('m1', 'm2'), ('dl',))
                k.op('S', lambda s: s.activation(scs['w1'], scs['dl'], AF.Sigmoid), ('dl',), ('w1',))
                k.op('V', lambda v: v.tensor_scalar(scs['w2'], scs['w1'], scalar1=-1.0, scalar2=1.0,
                                                    op0=ALU.mult, op1=ALU.add), ('w1',), ('w2',))
                k.op('V', lambda v: v.tensor_scalar(eq1b[:], eq1b[:], scalar1=scs['w1'], scalar2=None,
                                                    op0=ALU.mult), ('eq1b', 'w1'), ('eq1b',))
                k.op('V', lambda v, tt=tt: v.scalar_tensor_tensor(combine[:, tt, :], eq2b[:],
                                                                  scs['w2'], eq1b[:],
                                                                  op0=ALU.mult, op1=ALU.add),
                     ('eq2b', 'w2', 'eq1b'), ('combine',))
            k.barrier()

        with ExitStack() as ph:
            psb = lambda n, sh, dt=F32: ph.enter_context(nc.psum_tensor(_un(n), sh, dt))
            ps_g = psb("ps_ge", [128, max(NC, 4), 512])
            ps_u = psb("ps_ue", [128, max(NC, 4), 512])
            gwb = ph.enter_context(nc.sbuf_tensor(_un("gwbe"), [128, 2, 16, 128], BF16))
            uwb = ph.enter_context(nc.sbuf_tensor(_un("uwbe"), [128, 2, 16, 128], BF16))
            sgbe2 = ph.enter_context(nc.sbuf_tensor(_un("sgbe"), [128, NC, 512], F32))
            este = ph.enter_context(nc.sbuf_tensor(_un("este"), [128, 2, 512], F32))
            GeB = ph.enter_context(nc.sbuf_tensor(_un("GeB"), [128, 11 * T], BF16))
            for ex in range(NEXP):
                k.dma('G', 'gW', BIG2[:, 0:11 * E].rearrange("p (a b) -> p a b", b=E),
                      wd1.rearrange("e (a p) c -> e p a c", p=128)[ex], writes=('ewd',))
                for it in range(11):
                    wsl = it % 2
                    k.dma('G', 'gA' if wsl == 0 else 'gB', gwb[:, wsl],
                          wg1.rearrange("e (a p) c -> e p a c", p=128)[ex][:, :, it * 128:(it + 1) * 128],
                          writes=('gw%d' % wsl,))
                    k.dma('G', 'gC' if wsl == 0 else 'gD', uwb[:, wsl],
                          wu1.rearrange("e (a p) c -> e p a c", p=128)[ex][:, :, it * 128:(it + 1) * 128],
                          writes=('uw%d' % wsl,))
                    for et in range(16):
                        for tc in range(NC):
                            k.mm(gwb[:, wsl, et, :], XT(et, tc * 512, (tc + 1) * 512),
                                 ps_g[:, tc, :], ('gw%d' % wsl, 'XT'), 'ps_ge%d' % tc,
                                 start=(et == 0), stop=(et == 15))
                        for tc in range(NC):
                            k.mm(uwb[:, wsl, et, :], XT(et, tc * 512, (tc + 1) * 512),
                                 ps_u[:, tc, :], ('uw%d' % wsl, 'XT'), 'ps_ue%d' % tc,
                                 start=(et == 0), stop=(et == 15))
                    for tc in range(NC):
                        k.op('S', lambda s, tc=tc: s.activation(sgbe2[:, tc, :], ps_g[:, tc, :], AF.Silu),
                             ('ps_ge%d' % tc,), ('sgb%d' % tc,))
                        k.op('V', lambda v, it=it, tc=tc: v.tensor_tensor(
                                GeB[:, it * T + tc * 512: it * T + (tc + 1) * 512], sgbe2[:, tc, :],
                                ps_u[:, tc, :], op=ALU.mult), ('sgb%d' % tc, 'ps_ue%d' % tc), ('Ge',))
                for tt in range(NT):
                    for it in range(11):
                        for hc in range(4):
                            k.mm(GeB[:, it * T + tt * 128: it * T + (tt + 1) * 128],
                                 BIG2[:, it * E + hc * 512: it * E + (hc + 1) * 512],
                                 ps_g[:, hc, :] if tt % 2 == 0 else ps_u[:, hc, :],
                                 ('Ge', 'ewd'), 'ps_%se%d' % ('g' if tt % 2 == 0 else 'u', hc),
                                 start=(it == 0), stop=(it == 10))
                    for hp in range(2):
                        for hh in range(2):
                            hc = hp * 2 + hh
                            psrc = ps_g[:, hc, :] if tt % 2 == 0 else ps_u[:, hc, :]
                            pk = 'ps_%se%d' % ('g' if tt % 2 == 0 else 'u', hc)
                            k.op('V', lambda v, tt=tt, ex=ex, hh=hh, psrc=psrc: v.tensor_scalar(
                                    este[:, hh, :], psrc,
                                    scalar1=combine[:, tt, ex:ex + 1], scalar2=None, op0=ALU.mult),
                                 (pk, 'combine'), ('este%d' % hh,))
                        sem = 'gR' if hp == 0 else 'gM'
                        if ex == 0:
                            k.dma('G', sem, racc[tt * 128:(tt + 1) * 128, hp * 1024:(hp + 1) * 1024],
                                  este[:].rearrange("p a b -> p (a b)"), reads=('este0', 'este1'))
                        else:
                            k.dma('G', sem, racc[tt * 128:(tt + 1) * 128, hp * 1024:(hp + 1) * 1024],
                                  este[:].rearrange("p a b -> p (a b)"), reads=('este0', 'este1'),
                                  accum_op=ALU.add)
            k.barrier()

        swiglu_A(swg, swu, FSH // 128, G1)
        down_B(G1, FSH // 128, swd, racc, scale_col=lambda tt: sgate[:, tt, :], accum=True)

        boundary(res3, racc, y_out, 'bf.', final=True)

    with nc.Block() as block:
        @block.sync
        def _(e):
            for c in k.q['Y']:
                c(e)

        @block.gpsimd
        def _(e):
            for c in k.q['G']:
                c(e)

        @block.tensor
        def _(e):
            for c in k.q['T']:
                c(e)

        @block.vector
        def _(e):
            for c in k.q['V']:
                c(e)

        @block.scalar
        def _(e):
            for c in k.q['S']:
                c(e)

    k.stack.close()
    return nc


def _host_consts(T):
    inv = 1.0 / (10000.0 ** (np.arange(0, D0, 2, dtype=np.float32) / D0))
    t = np.arange(T, dtype=np.float32)[:, None] * inv[None, :].astype(np.float32)
    ropec = np.ascontiguousarray(np.tile(np.cos(t).astype(np.float32), (1, 4))).astype(NPBF)
    ropes = np.ascontiguousarray(np.tile(np.sin(t).astype(np.float32), (1, 4))).astype(NPBF)
    qq = np.arange(512)
    pp = np.arange(128)[:, None]
    mask = np.zeros((128, 4 * 512), np.float32)
    for j in range(4):
        mask[:, j * 512:(j + 1) * 512] = (qq[None, :] >= 128 * j + pp).astype(np.float32)
    mask0 = mask.astype(NPBF)
    bidx = np.zeros((128, 15), np.int32)
    for b in range(15):
        delta = 7 - b
        bidx[:, b] = 128 * delta + 2047 - np.arange(128)
    return ropec, ropes, mask0, bidx


def host_inputs(inp, T=2048):
    f32 = lambda a: np.ascontiguousarray(np.asarray(a, dtype=np.float32))
    ropec, ropes, mask0, bidx = _host_consts(T)
    return {
        "w_qkv": f32(inp["L0_attn_w_qkv"]),
        "w_out0": f32(inp["L0_attn_w_out"]),
        "ln0_s": np.ascontiguousarray(np.tile(f32(inp["L0_attn_ln_scale"]).reshape(1, D0), (1, 4))),
        "ln0_b": np.ascontiguousarray(np.tile(f32(inp["L0_attn_ln_bias"]).reshape(1, D0), (1, 4))),
        "wgT0": np.ascontiguousarray(f32(inp["L0_ffn_w_gate"]).T),
        "wuT0": np.ascontiguousarray(f32(inp["L0_ffn_w_up"]).T),
        "wdT0": np.ascontiguousarray(f32(inp["L0_ffn_w_down"]).T),
        "wqT1": np.ascontiguousarray(f32(inp["L1_attn_wq"]).T),
        "wkT1": np.ascontiguousarray(f32(inp["L1_attn_wk"]).T),
        "wvT1": np.ascontiguousarray(f32(inp["L1_attn_wv"]).T),
        "woT1": np.ascontiguousarray(f32(inp["L1_attn_wo"]).T),
        "relpos": f32(inp["L1_attn_rel_pos"]),
        "rwT": np.ascontiguousarray(f32(inp["L1_ffn_router_w"]).T),
        "sgw": f32(inp["L1_ffn_shared_gate_w"]).reshape(E, 1),
        "wg1": f32(inp["L1_ffn_w_gate"]),
        "wu1": f32(inp["L1_ffn_w_up"]),
        "wd1": f32(inp["L1_ffn_w_down"]),
        "swg": f32(inp["L1_ffn_sw_gate"]),
        "swu": f32(inp["L1_ffn_sw_up"]),
        "swd": f32(inp["L1_ffn_sw_down"]),
        "ropec_d": ropec, "ropes_d": ropes, "mask0_d": mask0, "bidx_d": bidx,
    }


def kernel(**inp):
    T = 2048
    x = np.ascontiguousarray(np.asarray(inp["L0_attn_x"], dtype=np.float32))
    common = host_inputs(inp, T)
    nc = build_program(T=T)
    in_maps = [dict(common, x=np.ascontiguousarray(x[b])) for b in range(8)]
    res = run_bass_kernel_spmd(nc, in_maps, core_ids=list(range(8)))
    return np.stack([np.asarray(r["y"], dtype=np.float32) for r in res.results], axis=0)


# version 09
# speedup vs baseline: 1.0698x; 1.0276x over previous
import numpy as np
import ml_dtypes
from contextlib import ExitStack
import concourse.bass as bass
import concourse.mybir as mybir
from concourse.bass import AP, IndirectOffsetOnAxis
from concourse.bass_utils import run_bass_kernel_spmd

F32 = mybir.dt.float32
BF16 = mybir.dt.bfloat16
I32 = mybir.dt.int32
AF = mybir.ActivationFunctionType
ALU = mybir.AluOpType
AX = mybir.AxisListType
NPBF = ml_dtypes.bfloat16

E = 2048
H0, D0 = 16, 128
H1, D1 = 32, 64
FF0 = 8192
NEXP, FI, FSH = 8, 1408, 5632
LN_EPS = 1e-5


class _Rec:

    def __init__(self):
        self.calls = []

    def __getattr__(self, name):
        def meth(*a, **kw):
            self.calls.append((name, a, kw))
            return None
        return meth


class Ctx:

    def __init__(self, nc, T):
        self.nc = nc
        self.T = T
        self.NT = T // 128
        self.NC = T // 512
        self.stack = ExitStack()
        self.cnt = {}
        self.sems = {}
        self.q = {'Y': [], 'G': [], 'T': [], 'V': [], 'S': []}
        self.engsem = {'T': 'sT', 'V': 'sV', 'S': 'sS', 'G': 'sG'}
        self.hw = {}
        self.bufs = {}

    def sem(self, name):
        self.sems[name] = self.stack.enter_context(self.nc.semaphore(name))
        self.cnt[name] = 0

    DMA_SEMS = frozenset(('yA', 'yB', 'yC', 'yD', 'yE', 'yF', 'yO', 'yP', 'yM',
                          'gA', 'gB', 'gC', 'gD', 'gM', 'gR', 'gW'))

    def wait(self, ek, sname, v):
        if sname in self.DMA_SEMS:
            v = self.cnt[sname]
        if v <= 0 or self.hw.get((ek, sname), 0) >= v:
            return
        sem = self.sems[sname]
        self.q[ek].append(lambda eng, v=v: eng.wait_ge(sem, v))
        self.hw[(ek, sname)] = v

    def dep(self, ek, reads, writes):
        best = {}

        def add(tok):
            if tok is not None and best.get(tok[0], 0) < tok[1]:
                best[tok[0]] = tok[1]

        for b in reads:
            st = self.bufs.get(b)
            if st:
                add(st['w'])
        for b in writes:
            st = self.bufs.get(b)
            if st:
                add(st['w'])
                for s, v in st['r'].items():
                    add((s, v))
        for s, v in best.items():
            self.wait(ek, s, v)

    def note(self, tok, reads, writes):
        for b in writes:
            self.bufs[b] = {'w': tok, 'r': {}}
        if tok is not None:
            s, v = tok
            for b in reads:
                st = self.bufs.setdefault(b, {'w': None, 'r': {}})
                if st['r'].get(s, 0) < v:
                    st['r'][s] = v

    def op(self, ek, fn, reads=(), writes=(), inc=True):
        self.dep(ek, reads, writes)
        rec = _Rec()
        fn(rec)
        assert len(rec.calls) == 1, "op() lambda must make exactly one engine call"
        name, a, kw = rec.calls[0]
        if inc:
            sname = self.engsem[ek]
            self.cnt[sname] += 1
            sem = self.sems[sname]
            self.q[ek].append(lambda eng, name=name, a=a, kw=kw:
                              getattr(eng, name)(*a, **kw).then_inc(sem, 1))
            tok = (sname, self.cnt[sname])
        else:
            self.q[ek].append(lambda eng, name=name, a=a, kw=kw: getattr(eng, name)(*a, **kw))
            tok = None
        self.note(tok, reads, writes)
        return tok

    def mm(self, lhsT, rhs, out, reads, psum_key, start, stop, force_inc=False):
        self.dep('T', reads, (psum_key,) if start else ())
        if stop or force_inc:
            self.cnt['sT'] += 1
            sem = self.sems['sT']
            self.q['T'].append(lambda eng: eng.matmul(out, lhsT, rhs, start=start, stop=stop,
                                                      skip_group_check=True).then_inc(sem, 1))
            tok = ('sT', self.cnt['sT'])
            self.note(tok, reads, (psum_key,) if stop else ())
            return tok
        self.q['T'].append(lambda eng: eng.matmul(out, lhsT, rhs, start=start, stop=stop,
                                                  skip_group_check=True))
        self.note(None, reads, ())
        return None

    def dma(self, ek, sname, out, in_, reads=(), writes=(), accum_op=None, indirect=None):
        self.dep(ek, reads, writes)
        self.wait(ek, sname, self.cnt[sname])
        sem = self.sems[sname]

        def closure(eng, out=out, in_=in_, accum_op=accum_op, indirect=indirect):
            if indirect is not None:
                instr = eng.indirect_dma_start(out=out, out_offset=None, in_=in_,
                                               in_offset=indirect)
            elif accum_op is not None:
                instr = eng.dma_start(out=out, in_=in_, accum_op=accum_op)
            else:
                instr = eng.dma_start(out=out, in_=in_)
            instr.then_inc(sem, 16)
        self.q[ek].append(closure)
        self.cnt[sname] += 16
        tok = (sname, self.cnt[sname])
        self.note(tok, reads, writes)
        return tok

    def barrier(self):
        snap = list(self.cnt.items())
        for ek in ('Y', 'G', 'T', 'V', 'S'):
            for sname, v in snap:
                self.wait(ek, sname, v)
        self.bufs = {}


def build_program(T=2048, taps=()):
    nc = bass.Bass("TRN2", target_bir_lowering=False, debug=False)
    k = Ctx(nc, T)
    NT, NC = k.NT, k.NC
    _uid = [0]

    def _un(n):
        _uid[0] += 1
        return "%s_u%d" % (n, _uid[0])
    taps = set(taps)

    def dram(name, shape, dtype):
        kind = "ExternalOutput" if name in taps else "Internal"
        return nc.dram_tensor(name, shape, dtype, kind=kind).ap()

    def inp(name, shape, dtype):
        return nc.dram_tensor(name, shape, dtype, kind="ExternalInput").ap()

    x_in = inp("x", [T, E], F32)
    w_qkv = inp("w_qkv", [E, 3 * E], F32)
    w_out0 = inp("w_out0", [E, E], F32)
    ln0_s = inp("ln0_s", [1, 512], F32)
    ln0_b = inp("ln0_b", [1, 512], F32)
    wgT0 = inp("wgT0", [E, FF0], F32)
    wuT0 = inp("wuT0", [E, FF0], F32)
    wdT0 = inp("wdT0", [FF0, E], F32)
    wqT1 = inp("wqT1", [E, E], F32)
    wkT1 = inp("wkT1", [E, E], F32)
    wvT1 = inp("wvT1", [E, E], F32)
    woT1 = inp("woT1", [E, E], F32)
    relpos = inp("relpos", [1023, 64], F32)
    rwT = inp("rwT", [E, NEXP], F32)
    sgw = inp("sgw", [E, 1], F32)
    wg1 = inp("wg1", [NEXP, E, FI], F32)
    wu1 = inp("wu1", [NEXP, E, FI], F32)
    wd1 = inp("wd1", [NEXP, FI, E], F32)
    swg = inp("swg", [E, FSH], F32)
    swu = inp("swu", [E, FSH], F32)
    swd = inp("swd", [FSH, E], F32)
    ropec_d = inp("ropec_d", [T, 256], BF16)
    ropes_d = inp("ropes_d", [T, 256], BF16)
    mask0_d = inp("mask0_d", [128, 4 * 512], BF16)
    bidx_d = inp("bidx_d", [128, 15], I32)

    y_out = nc.dram_tensor("y", [T, E], F32, kind="ExternalOutput").ap()
    res1 = dram("res1", [T, E], F32)
    res2 = dram("res2", [T, E], F32)
    res3 = dram("res3", [T, E], F32)
    pre1 = dram("pre1", [T, E], F32)
    pre2 = dram("pre2", [T, E], F32)
    pre3 = dram("pre3", [T, E], F32)
    QT0 = dram("QT0", [H0, D0, T], BF16)
    KT0 = dram("KT0", [H0, D0, T], BF16)
    V0 = dram("V0", [H0, T, D0], BF16)
    G0 = dram("G0", [T // 128, 128, FF0 // 128, 128], BF16)
    QT1 = dram("QT1", [E, T], BF16)
    KT1 = dram("KT1", [E, T], BF16)
    V1 = dram("V1", [T, E], BF16)
    G1 = dram("G1", [T // 128, 128, FSH // 128, 128], BF16)
    racc = dram("racc", [T, E], F32)
    relm = dram("relm", [1023, 1], F32)
    expB_d = dram("expB_d", [128, 15 * 128], BF16)
    ext_d = nc.dram_tensor("ext", [4095], F32, kind="Internal")
    ext = ext_d.ap()
    ext2d = ext.rearrange("(a b) -> a b", b=1)

    for s in ('sT', 'sV', 'sS', 'sG', 'yA', 'yB', 'yC', 'yD', 'yE', 'yF', 'yO', 'yP',
              'yM', 'gA', 'gB', 'gC', 'gD', 'gM', 'gR', 'gW'):
        k.sem(s)

    st = k.stack
    sb = lambda name, shape, dt: st.enter_context(nc.sbuf_tensor(name, shape, dt))

    BIG1 = sb("BIG1", [128, 32768], BF16)
    BIG2 = sb("BIG2", [128, 32768], BF16)

    ln0sb = sb("ln0sb", [128, 512], F32)
    ln0bb = sb("ln0bb", [128, 512], F32)
    idbf = sb("idbf", [128, 128], BF16)
    onescol = sb("onescol", [128, 1], BF16)
    ones1f = sb("ones1f", [1, 128], F32)
    combine = sb("combine", [128, NT, NEXP], F32)
    sgate = sb("sgate", [128, NT, 1], F32)
    idf32 = sb("idf32", [128, 128], F32)
    mid_stack = ExitStack()
    cLow = mid_stack.enter_context(nc.sbuf_tensor("cLow", [128, 1], F32))
    cHigh = mid_stack.enter_context(nc.sbuf_tensor("cHigh", [128, 1], F32))
    early_stack = ExitStack()
    ropec = early_stack.enter_context(nc.sbuf_tensor("ropec", [128, NT, 256], BF16))
    ropes = early_stack.enter_context(nc.sbuf_tensor("ropes", [128, NT, 256], BF16))

    def XT(et, t0, t1):
        return BIG1[:, et * T + t0: et * T + t1]

    def CTXT(p0, p1, ct, t0, t1):
        return BIG2[p0:p1, ct * T + t0: ct * T + t1]

    if True:
        def emit_ln(zap, zkey, W, outs, sc, p):
            k.op('V', lambda v: v.reduce_sum(sc['s1'], zap, axis=AX.X), (zkey,), (p + 's1',))
            emit_ln_post(zap, zkey, W, outs, sc, p)

        def emit_ln_post(zap, zkey, W, outs, sc, p):
            k.op('S', lambda s: s.activation(sc['sq'], zap, AF.Square, accum_out=sc['s2']),
                 (zkey,), (p + 'sq', p + 's2'))
            k.op('V', lambda v: v.tensor_scalar(sc['m'], sc['s1'], scalar1=1.0 / W, scalar2=None,
                                                op0=ALU.mult), (p + 's1',), (p + 'm',))
            k.op('V', lambda v: v.tensor_scalar(sc['t2'], sc['s2'], scalar1=1.0 / W, scalar2=None,
                                                op0=ALU.mult), (p + 's2',), (p + 't2',))
            k.op('V', lambda v: v.tensor_tensor(sc['var'], sc['m'], sc['m'], op=ALU.mult),
                 (p + 'm',), (p + 'var',))
            k.op('V', lambda v: v.tensor_tensor(sc['var'], sc['t2'], sc['var'], op=ALU.subtract),
                 (p + 't2', p + 'var'), (p + 'var',))
            k.op('V', lambda v: v.tensor_scalar(sc['var'], sc['var'], scalar1=LN_EPS, scalar2=None,
                                                op0=ALU.add), (p + 'var',), (p + 'var',))
            k.op('S', lambda s: s.activation(sc['sd'], sc['var'], AF.Sqrt),
                 (p + 'var',), (p + 'sd',))
            k.op('V', lambda v: v.reciprocal(sc['rstd'], sc['sd']), (p + 'sd',), (p + 'rstd',))
            k.op('V', lambda v: v.scalar_tensor_tensor(sc['nm'], sc['m'], -1.0, sc['rstd'],
                                                       op0=ALU.mult, op1=ALU.mult),
                 (p + 'm', p + 'rstd'), (p + 'nm',))
            for oap, okey in outs:
                k.op('S', lambda s, oap=oap: s.activation(oap, zap, AF.Identity, bias=sc['nm'],
                                                          scale=sc['rstd']),
                     (zkey, p + 'nm', p + 'rstd'), (okey,))

        def mk_lnsc(ph, name, W):
            d = {n: ph.enter_context(nc.sbuf_tensor(_un(name + n), [128, 1], F32))[:]
                 for n in ('s1', 's2', 'm', 't2', 'var', 'sd', 'rstd', 'nm')}
            d['sq'] = ph.enter_context(nc.sbuf_tensor(_un(name + 'sq'), [128, W], F32))[:]
            return d

        with ExitStack() as ph:
            psb = lambda n, sh, dt=F32: ph.enter_context(nc.psum_tensor(_un(n), sh, dt))
            ps_tr = psb("ps_tr", [128, 2, 1024], BF16)
            ps_trf = psb("ps_trf", [128, 512])
            k.dma('Y', 'yM', ropec[:], ropec_d.rearrange("(a p) f -> p a f", p=128), writes=('ropec',))
            k.dma('Y', 'yM', ropes[:], ropes_d.rearrange("(a p) f -> p a f", p=128), writes=('ropes',))
            k.dma('Y', 'yM', ln0sb[:], ln0_s.broadcast_to([128, 512]), writes=('ln0sb',))
            k.dma('Y', 'yM', ln0bb[:], ln0_b.broadcast_to([128, 512]), writes=('ln0bb',))
            k.op('G', lambda g: g.memset(idbf[:], 0.0), (), ('idbf',))
            k.op('G', lambda g: g.affine_select(out=idbf[:], in_=idbf[:], compare_op=ALU.not_equal,
                                                fill=1.0, base=0, pattern=[[-1, 128]],
                                                channel_multiplier=1), ('idbf',), ('idbf',))
            k.op('G', lambda g: g.memset(onescol[:], 1.0), (), ('onescol',))
            k.op('G', lambda g: g.memset(ones1f[:], 1.0), (), ('ones1f',))

            xstage = ph.enter_context(nc.sbuf_tensor(_un("xstage"), [128, 2, E], BF16))
            for tt in range(NT):
                sl = tt % 2
                k.dma('G', 'gA' if sl == 0 else 'gB', xstage[:, sl, :],
                      x_in[tt * 128:(tt + 1) * 128, :], writes=('xs%d' % sl,))
                for et in range(16):
                    pk = 'ps_tr%d' % (et % 2)
                    k.op('T', lambda t: t.transpose(ps_tr[:, et % 2, 0:128],
                                                    xstage[:, sl, et * 128:(et + 1) * 128], idbf[:]),
                         ('xs%d' % sl, 'idbf'), (pk,))
                    k.op('V', lambda v: v.tensor_copy(XT(et, tt * 128, (tt + 1) * 128),
                                                      ps_tr[:, et % 2, 0:128]), (pk,), ('XT',))

            rpbuf = ph.enter_context(nc.sbuf_tensor(_un("rpbuf"), [128, 64], F32))
            rmcol = ph.enter_context(nc.sbuf_tensor(_un("rmcol"), [128, 1], F32))
            relm_toks = []
            for i in range(8):
                rows = 128 if i < 7 else 127
                k.dma('Y', 'yM', rpbuf[0:rows, :], relpos[i * 128:i * 128 + rows, :], writes=('rpbuf',))
                k.op('V', lambda v: v.reduce_sum(rmcol[0:rows, :], rpbuf[0:rows, :], axis=AX.X),
                     ('rpbuf',), ('rmcol',))
                k.op('V', lambda v: v.tensor_scalar(rmcol[0:rows, :], rmcol[0:rows, :], scalar1=1.0 / 64,
                                                    scalar2=None, op0=ALU.mult), ('rmcol',), ('rmcol',))
                k.dma('Y', 'yM', relm[i * 128:i * 128 + rows, :], rmcol[0:rows, :],
                      reads=('rmcol',), writes=('relm',))
            k.dma('Y', 'yM', ext2d[1536:2559, :], relm[:, :], reads=('relm',), writes=('extm',))
            padt = ph.enter_context(nc.sbuf_tensor(_un("padt"), [128, 12], F32))
            k.dma('Y', 'yM', cLow[:], relm[0:1, :].broadcast_to([128, 1]), reads=('relm',), writes=('cLow',))
            k.dma('Y', 'yM', cHigh[:], relm[1022:1023, :].broadcast_to([128, 1]), reads=('relm',),
                  writes=('cHigh',))
            k.op('G', lambda g: g.memset(padt[:], 0.0), (), ('padt',))
            k.op('V', lambda v: v.tensor_scalar(padt[:], padt[:], scalar1=cLow[:], scalar2=None,
                                                op0=ALU.add), ('padt', 'cLow'), ('padt',))
            k.dma('Y', 'yM', AP(ext_d, 0, [[12, 128], [1, 12]]), padt[:],
                  reads=('padt',), writes=('extl',))
            k.op('G', lambda g: g.memset(padt[:], 0.0), (), ('padt',))
            k.op('V', lambda v: v.tensor_scalar(padt[:], padt[:], scalar1=cHigh[:], scalar2=None,
                                                op0=ALU.add), ('padt', 'cHigh'), ('padt',))
            k.dma('Y', 'yM', AP(ext_d, 2559, [[12, 128], [1, 12]]), padt[:],
                  reads=('padt',), writes=('extr',))
            natb = ph.enter_context(nc.sbuf_tensor(_un("natb"), [128, 128], F32))
            btmp = ph.enter_context(nc.sbuf_tensor(_un("btmp"), [128, 128], F32))
            bidx = ph.enter_context(nc.sbuf_tensor(_un("bidx"), [128, 15], I32))
            expstg = ph.enter_context(nc.sbuf_tensor(_un("expstg"), [128, 128], BF16))
            k.dma('Y', 'yM', bidx[:], bidx_d[:, :], writes=('bidx',))
            k.op('G', lambda g: g.memset(idf32[:], 0.0), (), ('idf32',))
            k.op('G', lambda g: g.affine_select(out=idf32[:], in_=idf32[:], compare_op=ALU.not_equal,
                                                fill=1.0, base=0, pattern=[[-1, 128]],
                                                channel_multiplier=1), ('idf32',), ('idf32',))
            for d in range(15):
                k.dma('G', 'gM', natb[:], ext2d[:, :], reads=('bidx', 'extm', 'extl', 'extr'),
                      writes=('natb',), indirect=IndirectOffsetOnAxis(bidx[:, d:d + 1], 0))
                k.op('T', lambda t: t.transpose(ps_trf[:, 0:128], natb[:], idf32[:]),
                     ('natb', 'idf32'), ('ps_trf',))
                k.op('V', lambda v: v.tensor_copy(btmp[:], ps_trf[:, 0:128]), ('ps_trf',), ('btmp',))
                k.op('S', lambda s, d=d: s.activation(expstg[:], btmp[:], AF.Exp),
                     ('btmp', 'expstg'), ('expstg',))
                k.dma('Y', 'yM', expB_d[:, d * 128:(d + 1) * 128], expstg[:], reads=('expstg',))
            k.barrier()

        with ExitStack() as ph:
            psb = lambda n, sh, dt=F32: ph.enter_context(nc.psum_tensor(_un(n), sh, dt))
            ps_mm = psb("ps_mm", [128, 2, 512])
            ps_tr = psb("ps_tr", [128, 2, 1024], BF16)
            wqkvb = ph.enter_context(nc.sbuf_tensor(_un("wqkvb"), [128, 2, 16, 512], BF16))
            ropeb = ph.enter_context(nc.sbuf_tensor(_un("ropeb"), [128, 2, 2, 4, 64], F32))
            lnin = ph.enter_context(nc.sbuf_tensor(_un("lnin"), [128, 2, 4, 128], F32))
            a1 = ph.enter_context(nc.sbuf_tensor(_un("a1"), [128, 2, 4, 128], F32))
            qkbf = ph.enter_context(nc.sbuf_tensor(_un("qkbf"), [128, 2, 4, 128], BF16))
            qst4 = ph.enter_context(nc.sbuf_tensor(_un("qst4"), [128, 2, 4, 128], BF16))
            xev = ph.enter_context(nc.sbuf_tensor(_un("xev"), [128, 2, 512], F32))
            lsc = {n: ph.enter_context(nc.sbuf_tensor(_un("p1l" + n), [128, 2, 4], F32))[:]
                   for n in ('s1', 's2', 'm', 't2', 'var', 'sd', 'rstd', 'nm')}
            pend_tr = []

            def flush_tr():
                for (cc2, tt2, u2) in pend_tr:
                    for j in range(4):
                        trk = 'ps_tr%d' % (j % 2)
                        k.op('T', lambda t, u2=u2, j=j: t.transpose(
                                ps_tr[:, j % 2, 0:128], qkbf[:, u2, j, :], idbf[:]),
                             ('qkbf%d' % u2, 'idbf'), (trk,))
                        k.op('V', lambda v, u2=u2, j=j: v.tensor_copy(
                                qst4[:, u2, j, :], ps_tr[:, j % 2, 0:128]),
                             (trk,), ('qst4%d' % u2,))
                    tgt2 = QT0 if cc2 < 4 else KT0
                    h0 = (cc2 % 4) * 4
                    k.dma('Y', 'yO' if tt2 % 2 == 0 else 'yP',
                          tgt2[h0:h0 + 4, :, tt2 * 128:(tt2 + 1) * 128].rearrange("h d t -> d h t"),
                          qst4[:, u2], reads=('qst4%d' % u2,))
                del pend_tr[:]

            for cc in range(12):
                wsl = cc % 2
                k.dma('G', 'gA' if wsl == 0 else 'gB', wqkvb[:, wsl],
                      w_qkv.rearrange("(a p) c -> p a c", p=128)[:, :, cc * 512:(cc + 1) * 512],
                      writes=('wqkv%d' % wsl,))
                is_v = cc >= 8
                for tt in range(NT):
                    pk = 'ps_mm%d' % (tt % 2)
                    for et in range(16):
                        k.mm(XT(et, tt * 128, (tt + 1) * 128), wqkvb[:, wsl, et, :],
                             ps_mm[:, tt % 2, :], ('XT', 'wqkv%d' % wsl), pk,
                             start=(et == 0), stop=(et == 15))
                    flush_tr()
                    if is_v:
                        u = tt % 2
                        vst = qkbf[:, u].rearrange("p a b -> p (a b)")
                        k.op('V', lambda v, u=u, vst=vst: v.tensor_copy(vst, ps_mm[:, tt % 2, :]),
                             (pk, 'qkbf%d' % u), ('qkbf%d' % u,))
                        for j in range(4):
                            h = (cc - 8) * 4 + j
                            k.dma('Y', 'yO' if j % 2 == 0 else 'yP',
                                  V0.rearrange("h (a p) d -> h p a d", p=128)[h, :, tt, :],
                                  vst[:, j * 128:(j + 1) * 128], reads=('qkbf%d' % u,))
                    else:
                        u = tt % 2
                        rc4 = ropec[:, tt, :].rearrange("p (j f) -> p j f", f=64)
                        rs4 = ropes[:, tt, :].rearrange("p (j f) -> p j f", f=64)
                        xkey = 'xev%d' % u
                        k.op('S', lambda s, u=u: s.activation(xev[:, u, :], ps_mm[:, tt % 2, :],
                                                              AF.Identity), (pk,), (xkey,))
                        pmm = xev[:, u, :]
                        q1v = pmm.rearrange("p (j f) -> p j f", f=128)[:, :, 0:64]
                        q2v = pmm.rearrange("p (j f) -> p j f", f=128)[:, :, 64:128]
                        rb = lambda i: ropeb[:, u, i % 2, :, :]
                        lniu = lnin[:, u]
                        k.op('V', lambda v: v.tensor_tensor(rb(0), q1v, rc4, op=ALU.mult),
                             (xkey, 'ropec'), ('rb0%d' % u,))
                        k.op('V', lambda v: v.tensor_tensor(rb(1), q2v, rs4, op=ALU.mult),
                             (xkey, 'ropes'), ('rb1%d' % u,))
                        k.op('V', lambda v: v.tensor_tensor(lniu[:, :, 0:64], rb(0), rb(1),
                                                            op=ALU.subtract),
                             ('rb0%d' % u, 'rb1%d' % u), ('lnin%d' % u,))
                        k.op('V', lambda v: v.tensor_tensor(rb(2), q1v, rs4, op=ALU.mult),
                             (xkey, 'ropes'), ('rb0%d' % u,))
                        k.op('V', lambda v: v.tensor_tensor(rb(3), q2v, rc4, op=ALU.mult),
                             (xkey, 'ropec'), ('rb1%d' % u,))
                        k.op('V', lambda v: v.tensor_tensor(lniu[:, :, 64:128], rb(2), rb(3),
                                                            op=ALU.add),
                             ('rb0%d' % u, 'rb1%d' % u), ('lnin%d' % u,))
                        for j in range(4):
                            k.op('S', lambda s, j=j, u=u: s.activation(
                                    a1[:, u, j, :], lniu[:, j, :], AF.Square,
                                    accum_out=lsc['s2'][:, u, j:j + 1]),
                                 ('lnin%d' % u, 'a1%d' % u), ('ls2%d' % u, 'a1%d' % u))
                        k.op('V', lambda v, u=u: v.reduce_sum(lsc['s1'][:, u], lniu[:],
                                                              axis=AX.X), ('lnin%d' % u,), ('ls1%d' % u,))
                        k.op('V', lambda v, u=u: v.tensor_scalar(lsc['m'][:, u], lsc['s1'][:, u],
                                                                 scalar1=1.0 / D0, scalar2=None,
                                                                 op0=ALU.mult), ('ls1%d' % u,), ('lm%d' % u,))
                        k.op('V', lambda v, u=u: v.tensor_scalar(lsc['t2'][:, u], lsc['s2'][:, u],
                                                                 scalar1=1.0 / D0, scalar2=None,
                                                                 op0=ALU.mult), ('ls2%d' % u,), ('lt2%d' % u,))
                        k.op('V', lambda v, u=u: v.tensor_tensor(lsc['var'][:, u], lsc['m'][:, u],
                                                                 lsc['m'][:, u], op=ALU.mult),
                             ('lm%d' % u,), ('lvar%d' % u,))
                        k.op('V', lambda v, u=u: v.tensor_tensor(lsc['var'][:, u], lsc['t2'][:, u],
                                                                 lsc['var'][:, u], op=ALU.subtract),
                             ('lt2%d' % u, 'lvar%d' % u), ('lvar%d' % u,))
                        k.op('V', lambda v, u=u: v.tensor_scalar(lsc['var'][:, u], lsc['var'][:, u],
                                                                 scalar1=LN_EPS, scalar2=None,
                                                                 op0=ALU.add), ('lvar%d' % u,), ('lvar%d' % u,))
                        k.op('S', lambda s, u=u: s.activation(lsc['sd'][:, u], lsc['var'][:, u],
                                                              AF.Sqrt), ('lvar%d' % u,), ('lsd%d' % u,))
                        k.op('V', lambda v, u=u: v.reciprocal(lsc['rstd'][:, u], lsc['sd'][:, u]),
                             ('lsd%d' % u,), ('lrstd%d' % u,))
                        k.op('V', lambda v, u=u: v.scalar_tensor_tensor(lsc['nm'][:, u], lsc['m'][:, u],
                                                                        -1.0, lsc['rstd'][:, u],
                                                                        op0=ALU.mult, op1=ALU.mult),
                             ('lm%d' % u, 'lrstd%d' % u), ('lnm%d' % u,))
                        for j in range(4):
                            k.op('S', lambda s, j=j, u=u: s.activation(
                                    a1[:, u, j, :], lniu[:, j, :], AF.Identity,
                                    bias=lsc['nm'][:, u, j:j + 1],
                                    scale=lsc['rstd'][:, u, j:j + 1]),
                                 ('lnin%d' % u, 'lnm%d' % u, 'lrstd%d' % u), ('a1%d' % u,))
                        a1v = a1[:, u].rearrange("p j f -> p (j f)")
                        qkv_ = qkbf[:, u].rearrange("p j f -> p (j f)")
                        k.op('V', lambda v, a1v=a1v: v.tensor_tensor(a1v, a1v, ln0sb[:], op=ALU.mult),
                             ('a1%d' % u, 'ln0sb'), ('a1%d' % u,))
                        k.op('V', lambda v, a1v=a1v, qkv_=qkv_: v.tensor_tensor(qkv_, a1v, ln0bb[:],
                                                                                op=ALU.add),
                             ('a1%d' % u, 'ln0bb', 'qkbf%d' % u), ('qkbf%d' % u,))
                        pend_tr.append((cc, tt, u))
                flush_tr()
            k.barrier()

        early_stack.close()

        with ExitStack() as ph:
            psb = lambda n, sh, dt=F32: ph.enter_context(nc.psum_tensor(_un(n), sh, dt))
            ps_s = psb("ps_s", [128, 4, 512])
            ps_ctx = psb("ps_ctx", [128, 2, 512])
            ps_den = psb("ps_den", [128, 512])
            ps_bc = psb("ps_bc", [128, 512])
            mask0 = ph.enter_context(nc.sbuf_tensor(_un("mask0"), [128, 4, 512], BF16))
            k.dma('Y', 'yM', mask0[:].rearrange("p a b -> p (a b)"), mask0_d[:, :], writes=('mask0',))
            qth = ph.enter_context(nc.sbuf_tensor(_un("qth"), [128, 2, 2, T], BF16))
            kth = ph.enter_context(nc.sbuf_tensor(_un("kth"), [128, 2, 2, T], BF16))
            vth = ph.enter_context(nc.sbuf_tensor(_un("vth"), [128, 2, 2, NT, 128], BF16))
            sqb = ph.enter_context(nc.sbuf_tensor(_un("sqb"), [128, 4, 512], F32))
            pbuf = ph.enter_context(nc.sbuf_tensor(_un("pbuf"), [128, 4, 512], BF16))
            recb = ph.enter_context(nc.sbuf_tensor(_un("recb"), [1, 2, 512], F32))
            bcb = ph.enter_context(nc.sbuf_tensor(_un("bcb"), [128, 2, 512], F32))
            for hp in range(H0 // 2):
                sl = hp % 2
                for hh in range(2):
                    h = 2 * hp + hh
                    k.dma('Y', ('yA', 'yB', 'yC', 'yD')[2 * sl + hh], qth[:, sl, hh, :],
                          QT0[h, :, :], writes=('qth%d%d' % (sl, hh),))
                    k.dma('Y', ('yC', 'yD', 'yA', 'yB')[2 * sl + hh], kth[:, sl, hh, :],
                          KT0[h, :, :], writes=('kth%d%d' % (sl, hh),))
                    k.dma('Y', 'yE' if sl == 0 else 'yF', vth[:, sl, hh],
                          V0.rearrange("h (a p) d -> h p a d", p=128)[h],
                          writes=('vth%d%d' % (sl, hh),))

                def mm1(hh, qc, kt):
                    sl_ = hp % 2
                    k.mm(kth[:, sl_, hh, kt * 128:(kt + 1) * 128],
                         qth[:, sl_, hh, qc * 512:(qc + 1) * 512],
                         ps_s[:, 2 * hh + kt % 2, :],
                         ('kth%d%d' % (sl_, hh), 'qth%d%d' % (sl_, hh)),
                         'ps_s%d' % (2 * hh + kt % 2), True, True)

                def chain(hh, qc, kt, nkt):
                    sl_ = hp % 2
                    pslot = 2 * hh + kt % 2
                    skey = 'ps_s%d' % pslot
                    pkk = 'pb%d' % pslot
                    k.op('S', lambda s, pslot=pslot: s.activation(
                            sqb[:, pslot, :], ps_s[:, pslot, :], AF.Square),
                         (skey,), ('sqb%d' % pslot,))
                    k.op('V', lambda v, pslot=pslot: v.tensor_tensor(
                            pbuf[:, pslot, :], sqb[:, pslot, :], sqb[:, pslot, :],
                            op=ALU.mult), ('sqb%d' % pslot,), (pkk,))
                    if kt >= 4 * qc:
                        k.op('V', lambda v, kt=kt, qc=qc, pslot=pslot: v.tensor_tensor(
                                pbuf[:, pslot, :], pbuf[:, pslot, :],
                                mask0[:, kt - 4 * qc, :], op=ALU.mult),
                             (pkk, 'mask0'), (pkk,))
                    k.mm(vth[:, sl_, hh, kt, :], pbuf[:, pslot, :], ps_ctx[:, hh, :],
                         ('vth%d%d' % (sl_, hh), pkk), 'ps_ctx%d' % hh,
                         start=(kt == 0), stop=(kt == nkt - 1), force_inc=True)
                    k.mm(onescol[:], pbuf[:, pslot, :], ps_den[64 * hh:64 * hh + 1, :],
                         ('onescol', pkk), 'ps_den%d' % hh,
                         start=(kt == 0), stop=(kt == nkt - 1), force_inc=True)

                def epilogue0(qc):
                    for hh in range(2):
                        h = 2 * hp + hh
                        ck = 'ps_ctx%d' % hh
                        k.op('V', lambda v, hh=hh: v.reciprocal(recb[:, hh, :],
                                                                ps_den[64 * hh:64 * hh + 1, :]),
                             ('ps_den0', 'ps_den1'), ('recb%d' % hh,))
                        k.op('T', lambda t, hh=hh: t.matmul(ps_bc[:, :], ones1f[:],
                                                            recb[:, hh, :], start=True, stop=True),
                             ('ones1f', 'recb%d' % hh), ('ps_bc',))
                        k.op('S', lambda s, hh=hh: s.copy(bcb[:, hh, :], ps_bc[:, :]),
                             ('ps_bc',), ('bcb%d' % hh,))
                        k.op('V', lambda v, h=h, hh=hh, qc=qc: v.tensor_tensor(
                                CTXT(0, 128, h, qc * 512, (qc + 1) * 512),
                                ps_ctx[:, hh, :], bcb[:, hh, :], op=ALU.mult),
                             (ck, 'bcb%d' % hh), ('CTXT',))

                for qc in range(NC):
                    nkt = 4 * qc + 4
                    for hh in range(2):
                        mm1(hh, qc, 0)
                    if qc > 0:
                        epilogue0(qc - 1)
                    for kt in range(nkt):
                        for hh in range(2):
                            if kt + 1 < nkt:
                                mm1(hh, qc, kt + 1)
                            chain(hh, qc, kt, nkt)
                epilogue0(NC - 1)
            k.barrier()

        def outproj(wsrc, out_dram, wname):
            with ExitStack() as ph:
                psb = lambda n, sh, dt=F32: ph.enter_context(nc.psum_tensor(_un(n), sh, dt))
                ps_mm = psb("ps_mmo", [128, 2, 4, 512])
                est2 = ph.enter_context(nc.sbuf_tensor(_un("esto"), [128, 2, 4, 512], F32))
                k.dma('G', 'gM', BIG1[:, 0:16 * E].rearrange("p (a b) -> p a b", b=E),
                      wsrc.rearrange("(a p) c -> p a c", p=128), writes=('woB',))
                for tt in range(NT):
                    for ct in range(16):
                        for ec in range(4):
                            k.mm(CTXT(0, 128, ct, tt * 128, (tt + 1) * 128),
                                 BIG1[:, ct * E + ec * 512: ct * E + (ec + 1) * 512],
                                 ps_mm[:, tt % 2, ec, :], ('CTXT', 'woB'),
                                 'ps_mmo%d_%d' % (tt % 2, ec), start=(ct == 0), stop=(ct == 15))
                    for ec in range(4):
                        pk = 'ps_mmo%d_%d' % (tt % 2, ec)
                        ekey = 'esto%d_%d' % (tt % 2, ec)
                        k.op('V', lambda v, tt=tt, ec=ec: v.tensor_copy(
                                est2[:, tt % 2, ec, :], ps_mm[:, tt % 2, ec, :]), (pk,), (ekey,))
                        k.dma('Y', 'yO' if ec % 2 == 0 else 'yP',
                              out_dram[tt * 128:(tt + 1) * 128, ec * 512:(ec + 1) * 512],
                              est2[:, tt % 2, ec, :], reads=(ekey,))
                k.barrier()

        outproj(w_out0, pre1, 'wo0')

        def boundary(res_in, pre_in, res_out, pname, final=False):
            with ExitStack() as ph:
                psb = lambda n, sh, dt=F32: ph.enter_context(nc.psum_tensor(_un(n), sh, dt))
                ps_tr = psb("ps_trb", [128, 2, 1024], BF16)
                bz = ph.enter_context(nc.sbuf_tensor(_un("bz"), [128, 2, E], F32))
                br = ph.enter_context(nc.sbuf_tensor(_un("br"), [128, 2, E], F32))
                bzf = ph.enter_context(nc.sbuf_tensor(_un("bzf"), [128, 2, E], F32))
                bzb = ph.enter_context(nc.sbuf_tensor(_un("bzb"), [128, 2, E], BF16))
                bsq = ph.enter_context(nc.sbuf_tensor(_un("bsq"), [128, E], F32))
                sc2 = [{n: ph.enter_context(nc.sbuf_tensor(_un("bln%d" % u + n), [128, 1], F32))[:]
                        for n in ('s1', 's2', 'm', 't2', 'var', 'sd', 'rstd', 'nm')}
                       for u in range(2)]
                pend = []

                def flush(nmax):
                    while len(pend) > nmax:
                        tt2, u2 = pend.pop(0)
                        k.op('V', lambda v, u2=u2: v.tensor_copy(bzb[:, u2, :], bzf[:, u2, :]),
                             ('bzf%d' % u2,), ('bzb%d' % u2,))
                        for et in range(16):
                            trk = 'ps_trb%d' % (et % 2)
                            k.op('T', lambda t, u2=u2, et=et: t.transpose(
                                    ps_tr[:, et % 2, 0:128], bzb[:, u2, et * 128:(et + 1) * 128],
                                    idbf[:]), ('bzb%d' % u2, 'idbf'), (trk,))
                            k.op('V', lambda v, et=et, tt2=tt2: v.tensor_copy(
                                    XT(et, tt2 * 128, (tt2 + 1) * 128), ps_tr[:, et % 2, 0:128]),
                                 (trk,), ('XT',))

                for tt in range(NT):
                    u = tt % 2
                    sc = sc2[u]
                    k.dma('Y', 'yA' if u == 0 else 'yB', br[:, u, :],
                          res_in[tt * 128:(tt + 1) * 128, :], writes=('br%d' % u,))
                    k.dma('Y', 'yC' if u == 0 else 'yD', bz[:, u, :],
                          pre_in[tt * 128:(tt + 1) * 128, :], writes=('bz%d' % u,))
                    zkey = 'bz%d' % u
                    k.op('V', lambda v, u=u: v.tensor_tensor(bz[:, u, :], bz[:, u, :],
                                                             br[:, u, :], op=ALU.add),
                         (zkey, 'br%d' % u), (zkey,))
                    k.op('V', lambda v, u=u: v.reduce_sum(sc['s1'], bz[:, u, :], axis=AX.X),
                         (zkey,), (pname + 's1%d' % u,))
                    k.op('S', lambda s, u=u: s.activation(bsq[:], bz[:, u, :], AF.Square,
                                                          accum_out=sc['s2']),
                         (zkey, 'bsq'), (pname + 's2%d' % u, 'bsq'))
                    k.op('V', lambda v, sc=sc: v.tensor_scalar(sc['m'], sc['s1'], scalar1=1.0 / E,
                                                               scalar2=None, op0=ALU.mult),
                         (pname + 's1%d' % u,), (pname + 'm%d' % u,))
                    k.op('V', lambda v, sc=sc: v.tensor_scalar(sc['t2'], sc['s2'], scalar1=1.0 / E,
                                                               scalar2=None, op0=ALU.mult),
                         (pname + 's2%d' % u,), (pname + 't2%d' % u,))
                    k.op('V', lambda v, sc=sc: v.tensor_tensor(sc['var'], sc['m'], sc['m'],
                                                               op=ALU.mult),
                         (pname + 'm%d' % u,), (pname + 'var%d' % u,))
                    k.op('V', lambda v, sc=sc: v.tensor_tensor(sc['var'], sc['t2'], sc['var'],
                                                               op=ALU.subtract),
                         (pname + 't2%d' % u, pname + 'var%d' % u), (pname + 'var%d' % u,))
                    k.op('V', lambda v, sc=sc: v.tensor_scalar(sc['var'], sc['var'], scalar1=LN_EPS,
                                                               scalar2=None, op0=ALU.add),
                         (pname + 'var%d' % u,), (pname + 'var%d' % u,))
                    k.op('S', lambda s, sc=sc: s.activation(sc['sd'], sc['var'], AF.Sqrt),
                         (pname + 'var%d' % u,), (pname + 'sd%d' % u,))
                    k.op('V', lambda v, sc=sc: v.reciprocal(sc['rstd'], sc['sd']),
                         (pname + 'sd%d' % u,), (pname + 'rstd%d' % u,))
                    k.op('V', lambda v, sc=sc: v.scalar_tensor_tensor(sc['nm'], sc['m'], -1.0,
                                                                      sc['rstd'], op0=ALU.mult,
                                                                      op1=ALU.mult),
                         (pname + 'm%d' % u, pname + 'rstd%d' % u), (pname + 'nm%d' % u,))
                    k.op('S', lambda s, sc=sc, u=u: s.activation(bzf[:, u, :], bz[:, u, :],
                                                                 AF.Identity, bias=sc['nm'],
                                                                 scale=sc['rstd']),
                         (zkey, pname + 'nm%d' % u, pname + 'rstd%d' % u), ('bzf%d' % u,))
                    k.dma('Y', 'yO' if u == 0 else 'yP', res_out[tt * 128:(tt + 1) * 128, :],
                          bzf[:, u, :], reads=('bzf%d' % u,))
                    if not final:
                        pend.append((tt, u))
                        flush(1)
                if not final:
                    flush(0)
                k.barrier()

        boundary(x_in, pre1, res1, 'b1.')

        def swiglu_A(wg_src, wu_src, n_it, G_dram):
            with ExitStack() as ph:
                psb = lambda n, sh, dt=F32: ph.enter_context(nc.psum_tensor(_un(n), sh, dt))
                ps_g = psb("ps_g", [128, NC, 512])
                ps_u = psb("ps_u", [128, NC, 512])
                gwb = ph.enter_context(nc.sbuf_tensor(_un("gwb"), [128, 2, 16, 128], BF16))
                uwb = ph.enter_context(nc.sbuf_tensor(_un("uwb"), [128, 2, 16, 128], BF16))
                sgb = ph.enter_context(nc.sbuf_tensor(_un("sgb"), [128, NC, 512], F32))
                gst = ph.enter_context(nc.sbuf_tensor(_un("gst"), [128, NC, 512], BF16))
                for it in range(n_it):
                    wsl = it % 2
                    k.dma('G', 'gA' if wsl == 0 else 'gB', gwb[:, wsl],
                          wg_src.rearrange("(a p) c -> p a c", p=128)[:, :, it * 128:(it + 1) * 128],
                          writes=('gw%d' % wsl,))
                    k.dma('G', 'gC' if wsl == 0 else 'gD', uwb[:, wsl],
                          wu_src.rearrange("(a p) c -> p a c", p=128)[:, :, it * 128:(it + 1) * 128],
                          writes=('uw%d' % wsl,))
                    for et in range(16):
                        for tc in range(NC):
                            k.mm(gwb[:, wsl, et, :], XT(et, tc * 512, (tc + 1) * 512),
                                 ps_g[:, tc, :], ('gw%d' % wsl, 'XT'), 'ps_g%d' % tc,
                                 start=(et == 0), stop=(et == 15))
                        for tc in range(NC):
                            k.mm(uwb[:, wsl, et, :], XT(et, tc * 512, (tc + 1) * 512),
                                 ps_u[:, tc, :], ('uw%d' % wsl, 'XT'), 'ps_u%d' % tc,
                                 start=(et == 0), stop=(et == 15))
                    for tc in range(NC):
                        k.op('S', lambda s, tc=tc: s.activation(sgb[:, tc, :], ps_g[:, tc, :], AF.Silu),
                             ('ps_g%d' % tc,), ('sgb%d' % tc,))
                        k.op('V', lambda v, tc=tc: v.tensor_tensor(gst[:, tc, :], sgb[:, tc, :],
                                                                   ps_u[:, tc, :], op=ALU.mult),
                             ('sgb%d' % tc, 'ps_u%d' % tc), ('gst',))
                    Gv = G_dram.rearrange("tt p i w -> p tt i w")
                    for tc in range(NC):
                        k.dma('Y', 'yO' if (it + tc) % 2 == 0 else 'yP',
                              Gv[:, tc * 4:(tc + 1) * 4, it, :],
                              gst[:, tc, :].rearrange("p (a b) -> p a b", b=128),
                              reads=('gst',))
                k.barrier()

        def down_B(G_dram, n_it, wd_src, out_dram, scale_col=None, accum=False):
            with ExitStack() as ph:
                psb = lambda n, sh, dt=F32: ph.enter_context(nc.psum_tensor(_un(n), sh, dt))
                ps_mm = psb("ps_mmd", [128, 2, 2, 512])
                gtb = ph.enter_context(nc.sbuf_tensor(_un("gtb"), [128, 3, n_it, 128], BF16))
                est2 = ph.enter_context(nc.sbuf_tensor(_un("est2"), [128, 2, 2, 512], F32))
                for pas in range(2):
                    for eco in range(2):
                        dst = BIG1 if eco == 0 else BIG2
                        k.dma('G', 'gA' if eco == 0 else 'gB',
                              dst[:, 0:n_it * 512].rearrange("p (a b) -> p a b", b=512),
                              wd_src.rearrange("(a p) c -> p a c", p=128)[:, :, (2 * pas + eco) * 512:(2 * pas + eco + 1) * 512],
                              writes=('dW%d' % eco,))
                    for pf in range(min(2, NT)):
                        k.dma('Y', ('yA', 'yB', 'yC')[pf % 3], gtb[:, pf % 3],
                              G_dram[pf], writes=('gtb%d' % (pf % 3),))
                    for tt in range(NT):
                        tsl = tt % 3
                        if tt + 2 < NT:
                            k.dma('Y', ('yA', 'yB', 'yC')[(tt + 2) % 3], gtb[:, (tt + 2) % 3],
                                  G_dram[tt + 2], writes=('gtb%d' % ((tt + 2) % 3),))
                        for it in range(n_it):
                            for eco in range(2):
                                dst = BIG1 if eco == 0 else BIG2
                                k.mm(gtb[:, tsl, it, :], dst[:, it * 512:(it + 1) * 512],
                                     ps_mm[:, tt % 2, eco, :], ('gtb%d' % tsl, 'dW%d' % eco),
                                     'ps_mmd%d_%d' % (tt % 2, eco),
                                     start=(it == 0), stop=(it == n_it - 1))
                        for eco in range(2):
                            ec = 2 * pas + eco
                            pk = 'ps_mmd%d_%d' % (tt % 2, eco)
                            ekey = 'est2_%d_%d' % (tt % 2, eco)
                            if scale_col is None:
                                k.op('V', lambda v, tt=tt, eco=eco: v.tensor_copy(
                                        est2[:, tt % 2, eco, :], ps_mm[:, tt % 2, eco, :]),
                                     (pk,), (ekey,))
                            else:
                                k.op('V', lambda v, tt=tt, eco=eco: v.tensor_scalar(
                                        est2[:, tt % 2, eco, :], ps_mm[:, tt % 2, eco, :],
                                        scalar1=scale_col(tt), scalar2=None, op0=ALU.mult),
                                     (pk, 'sgate'), (ekey,))
                            if accum:
                                k.dma('G', 'gR', out_dram[tt * 128:(tt + 1) * 128, ec * 512:(ec + 1) * 512],
                                      est2[:, tt % 2, eco, :], reads=(ekey,), accum_op=ALU.add)
                            else:
                                k.dma('Y', 'yO' if eco == 0 else 'yP',
                                      out_dram[tt * 128:(tt + 1) * 128, ec * 512:(ec + 1) * 512],
                                      est2[:, tt % 2, eco, :], reads=(ekey,))
                k.barrier()

        swiglu_A(wgT0, wuT0, FF0 // 128, G0)
        down_B(G0, FF0 // 128, wdT0, pre2)
        boundary(res1, pre2, res2, 'b2.')

        with ExitStack() as ph:
            psb = lambda n, sh, dt=F32: ph.enter_context(nc.psum_tensor(_un(n), sh, dt))
            ps_mm = psb("ps_mmp", [128, 2, max(NC, 4), 512])
            wpb = ph.enter_context(nc.sbuf_tensor(_un("wpb"), [128, 2, 16, 128], BF16))
            pstage = ph.enter_context(nc.sbuf_tensor(_un("pstage"), [128, 2, NC, 512], BF16))
            for (wsrc, tgt, nm) in ((wqT1, QT1, 'q'), (wkT1, KT1, 'k')):
                for ct in range(16):
                    wsl = ct % 2
                    wk = 'wpb%d' % wsl
                    k.dma('G', 'gA' if wsl == 0 else 'gB', wpb[:, wsl],
                          wsrc.rearrange("(a p) c -> p a c", p=128)[:, :, ct * 128:(ct + 1) * 128],
                          writes=(wk,))
                    for et in range(16):
                        for tc in range(NC):
                            k.mm(wpb[:, wsl, et, :], XT(et, tc * 512, (tc + 1) * 512),
                                 ps_mm[:, ct % 2, tc, :], (wk, 'XT'), 'ps_mmp%d_%d' % (ct % 2, tc),
                                 start=(et == 0), stop=(et == 15))
                    for tc in range(NC):
                        k.op('V', lambda v, ct=ct, tc=tc: v.tensor_copy(
                                pstage[:, ct % 2, tc, :], ps_mm[:, ct % 2, tc, :]),
                             ('ps_mmp%d_%d' % (ct % 2, tc),), ('pstage%d' % (ct % 2),))
                    k.dma('Y', 'yO' if ct % 2 == 0 else 'yP', tgt[ct * 128:(ct + 1) * 128, :],
                          pstage[:, ct % 2].rearrange("p a b -> p (a b)"),
                          reads=('pstage%d' % (ct % 2),))
            k.dma('G', 'gM', BIG2[:, 0:16 * E].rearrange("p (a b) -> p a b", b=E),
                  wvT1.rearrange("(a p) c -> p a c", p=128), writes=('wvB',))
            for tt in range(NT):
                for et in range(16):
                    for cc in range(4):
                        k.mm(XT(et, tt * 128, (tt + 1) * 128),
                             BIG2[:, et * E + cc * 512: et * E + (cc + 1) * 512],
                             ps_mm[:, tt % 2, cc, :], ('XT', 'wvB'), 'ps_mmp%d_%d' % (tt % 2, cc),
                             start=(et == 0), stop=(et == 15))
                for cc in range(4):
                    pk = 'ps_mmp%d_%d' % (tt % 2, cc)
                    k.op('V', lambda v, tt=tt, cc=cc: v.tensor_copy(
                            pstage[:, tt % 2, 0, :] if NC == 1 else pstage[:, tt % 2, cc % NC, :],
                            ps_mm[:, tt % 2, cc, :]), (pk,), ('pstage%d' % (tt % 2),))
                    k.dma('Y', 'yO' if cc % 2 == 0 else 'yP',
                          V1[tt * 128:(tt + 1) * 128, cc * 512:(cc + 1) * 512],
                          pstage[:, tt % 2, 0, :] if NC == 1 else pstage[:, tt % 2, cc % NC, :],
                          reads=('pstage%d' % (tt % 2),))
            k.barrier()

        with ExitStack() as ph:
            psb = lambda n, sh, dt=F32: ph.enter_context(nc.psum_tensor(_un(n), sh, dt))
            ps_s = psb("ps_s1", [128, 4, 512])
            ps_ctx = psb("ps_ctx1", [128, 2, 512])
            ps_bc = psb("ps_bc1", [128, 512])
            expBc = ph.enter_context(nc.sbuf_tensor(_un("expBc"), [128, 15 * 128], BF16))
            k.dma('Y', 'yM', expBc[:], expB_d[:, :], writes=('expBc',))
            qb1 = ph.enter_context(nc.sbuf_tensor(_un("qb1"), [128, 2, T], BF16))
            kb1 = ph.enter_context(nc.sbuf_tensor(_un("kb1"), [128, 2, T], BF16))
            vx1 = ph.enter_context(nc.sbuf_tensor(_un("vx1"), [128, 2, 2, NT, 65], BF16))
            pe1 = ph.enter_context(nc.sbuf_tensor(_un("pe1"), [128, 4, 512], BF16))
            pb1 = ph.enter_context(nc.sbuf_tensor(_un("pb1"), [128, 4, 512], BF16))
            rec1 = ph.enter_context(nc.sbuf_tensor(_un("rec1"), [1, 2, 512], F32))
            bcb1 = ph.enter_context(nc.sbuf_tensor(_un("bcb1"), [64, 2, 512], F32))
            k.op('G', lambda g: g.memset(vx1[:, :, :, :, 64:65], 1.0), (), ('vxones',))
            for hp in range(H1 // 2):
                sl = hp % 2
                for hh in range(2):
                    h = 2 * hp + hh
                    p0 = 64 * hh
                    k.dma('Y', ('yA', 'yB', 'yC', 'yD')[2 * sl + hh], qb1[p0:p0 + 64, sl, :],
                          QT1.rearrange("(h d) t -> h d t", d=64)[h], writes=('qb1_%d%d' % (sl, hh),))
                    k.dma('Y', ('yC', 'yD', 'yA', 'yB')[2 * sl + hh], kb1[p0:p0 + 64, sl, :],
                          KT1.rearrange("(h d) t -> h d t", d=64)[h], writes=('kb1_%d%d' % (sl, hh),))
                    k.dma('Y', 'yE' if sl == 0 else 'yF', vx1[:, sl, hh, :, 0:64],
                          V1.rearrange("(a p) c -> p a c", p=128)[:, :, h * 64:(h + 1) * 64],
                          writes=('vx1_%d%d' % (sl, hh),))

                def mm1(hh, qc, kt, sl=None):
                    sl_ = hp % 2
                    p0 = 64 * hh
                    k.mm(kb1[p0:p0 + 64, sl_, kt * 128:(kt + 1) * 128],
                         qb1[p0:p0 + 64, sl_, qc * 512:(qc + 1) * 512],
                         ps_s[:, 2 * hh + kt % 2, :],
                         ('kb1_%d%d' % (sl_, hh), 'qb1_%d%d' % (sl_, hh)),
                         'ps_s1%d' % (2 * hh + kt % 2), True, True)

                def chain(hh, qc, kt):
                    sl_ = hp % 2
                    skey = 'ps_s1%d' % (2 * hh + kt % 2)
                    dc = kt - 4 * qc
                    pslot = 2 * hh + kt % 2
                    pkk = 'pb1%d' % pslot
                    if -4 <= dc <= 7:
                        k.op('S', lambda s, pslot=pslot: s.activation(
                                pe1[:, pslot, :], ps_s[:, pslot, :], AF.Exp, scale=0.125),
                             (skey,), ('pe1%d' % pslot,))
                        k.op('V', lambda v, pslot=pslot, dc=dc: v.tensor_tensor(
                                pb1[:, pslot, :], pe1[:, pslot, :],
                                expBc[:, (7 - dc) * 128:(7 - dc) * 128 + 512], op=ALU.mult),
                             ('pe1%d' % pslot, 'expBc'), (pkk,))
                    else:
                        ccol, ckn = (cHigh, 'cHigh') if dc > 7 else (cLow, 'cLow')
                        k.op('S', lambda s, ccol=ccol, pslot=pslot: s.activation(
                                pb1[:, pslot, :], ps_s[:, pslot, :], AF.Exp, scale=0.125,
                                bias=ccol[:]), (skey, ckn), (pkk,))
                    k.mm(vx1[:, sl_, hh, kt, :], pb1[:, pslot, :], ps_ctx[0:65, hh, :],
                         ('vx1_%d%d' % (sl_, hh), 'vxones', pkk), 'ps_ctx1%d' % hh,
                         start=(kt == 0), stop=(kt == NT - 1), force_inc=True)

                def epilogue(qc):
                    for hh in range(2):
                        h = 2 * hp + hh
                        ck = 'ps_ctx1%d' % hh
                        k.op('V', lambda v, hh=hh: v.reciprocal(rec1[:, hh, :],
                                                                ps_ctx[64:65, hh, :]),
                             (ck,), ('rec1%d' % hh,))
                        k.op('T', lambda t, hh=hh: t.matmul(ps_bc[0:64, :], ones1f[0:1, 0:64],
                                                            rec1[:, hh, :], start=True, stop=True),
                             ('ones1f', 'rec1%d' % hh), ('ps_bc1',))
                        k.op('S', lambda s, hh=hh: s.copy(bcb1[:, hh, :], ps_bc[0:64, :]),
                             ('ps_bc1',), ('bcb1%d' % hh,))
                        k.op('V', lambda v, h=h, hh=hh, qc=qc: v.tensor_tensor(
                                CTXT(64 * (h % 2), 64 * (h % 2) + 64, h // 2, qc * 512,
                                     (qc + 1) * 512),
                                ps_ctx[0:64, hh, :], bcb1[:, hh, :], op=ALU.mult),
                             (ck, 'bcb1%d' % hh), ('CTXT',))

                for qc in range(NC):
                    for hh in range(2):
                        mm1(hh, qc, 0)
                    if qc > 0:
                        epilogue(qc - 1)
                    for kt in range(NT):
                        for hh in range(2):
                            if kt + 1 < NT:
                                mm1(hh, qc, kt + 1)
                            chain(hh, qc, kt)
                epilogue(NC - 1)
            k.barrier()

        mid_stack.close()
        outproj(woT1, pre3, 'wo1')
        boundary(res2, pre3, res3, 'b3.')

        with ExitStack() as ph:
            psb = lambda n, sh, dt=F32: ph.enter_context(nc.psum_tensor(_un(n), sh, dt))
            ps_r = psb("ps_r", [128, 512])
            ps_sg = psb("ps_sg", [128, 512])
            rwb = ph.enter_context(nc.sbuf_tensor(_un("rwb"), [128, 16, NEXP], F32))
            xr32 = ph.enter_context(nc.sbuf_tensor(_un("xr32"), [128, 2, E], F32))
            xt32 = ph.enter_context(nc.sbuf_tensor(_un("xt32"), [128, 16, 128], F32))
            sgwb = ph.enter_context(nc.sbuf_tensor(_un("sgwb"), [128, 16, 1], BF16))
            lgb = ph.enter_context(nc.sbuf_tensor(_un("lgb"), [128, NEXP], F32))
            mskb = ph.enter_context(nc.sbuf_tensor(_un("mskb"), [128, NEXP], F32))
            eq1b = ph.enter_context(nc.sbuf_tensor(_un("eq1b"), [128, NEXP], F32))
            eq2b = ph.enter_context(nc.sbuf_tensor(_un("eq2b"), [128, NEXP], F32))
            scs = {n: ph.enter_context(nc.sbuf_tensor(_un("rt" + n), [128, 1], F32))[:]
                   for n in ('m1', 'm2', 'dl', 'w1', 'w2')}
            k.dma('Y', 'yM', rwb[:], rwT.rearrange("(a p) c -> p a c", p=128), writes=('rwb',))
            for et in range(16):
                k.dma('G', 'gM', sgwb[:, et, :], sgw[et * 128:(et + 1) * 128, :], writes=('sgwb',))
            ps_rt = psb("ps_rt", [128, 512])
            for tt in range(NT):
                sl = tt % 2
                k.dma('Y', 'yA' if sl == 0 else 'yB', xr32[:, sl, :],
                      res3[tt * 128:(tt + 1) * 128, :], writes=('xr32%d' % sl,))
                for et in range(16):
                    k.op('T', lambda t, sl=sl, et=et: t.transpose(
                            ps_rt[:, 0:128], xr32[:, sl, et * 128:(et + 1) * 128], idf32[:]),
                         ('xr32%d' % sl, 'idf32'), ('ps_rt',))
                    k.op('V', lambda v, et=et: v.tensor_copy(xt32[:, et, :], ps_rt[:, 0:128]),
                         ('ps_rt',), ('xt32',))
                for et in range(16):
                    k.mm(xt32[:, et, :], rwb[:, et, :], ps_r[:, 0:NEXP],
                         ('xt32', 'rwb'), 'ps_r', start=(et == 0), stop=(et == 15))
                for et in range(16):
                    k.mm(XT(et, tt * 128, (tt + 1) * 128), sgwb[:, et, :], ps_sg[:, 0:1],
                         ('XT', 'sgwb'), 'ps_sg', start=(et == 0), stop=(et == 15))
                k.op('V', lambda v: v.tensor_copy(lgb[:], ps_r[:, 0:NEXP]), ('ps_r',), ('lgb',))
                k.op('S', lambda s, tt=tt: s.activation(sgate[:, tt, :], ps_sg[:, 0:1], AF.Sigmoid),
                     ('ps_sg',), ('sgate',))
                k.op('V', lambda v: v.reduce_max(scs['m1'], lgb[:], axis=AX.X), ('lgb',), ('m1',))
                k.op('V', lambda v: v.tensor_scalar(eq1b[:], lgb[:], scalar1=scs['m1'], scalar2=None,
                                                    op0=ALU.is_equal), ('lgb', 'm1'), ('eq1b',))
                k.op('V', lambda v: v.scalar_tensor_tensor(mskb[:], eq1b[:], -1e30, lgb[:],
                                                           op0=ALU.mult, op1=ALU.add),
                     ('eq1b', 'lgb'), ('mskb',))
                k.op('V', lambda v: v.reduce_max(scs['m2'], mskb[:], axis=AX.X), ('mskb',), ('m2',))
                k.op('V', lambda v: v.tensor_scalar(eq2b[:], mskb[:], scalar1=scs['m2'], scalar2=None,
                                                    op0=ALU.is_equal), ('mskb', 'm2'), ('eq2b',))
                k.op('V', lambda v: v.tensor_tensor(scs['dl'], scs['m1'], scs['m2'], op=ALU.subtract),
                     ('m1', 'm2'), ('dl',))
                k.op('S', lambda s: s.activation(scs['w1'], scs['dl'], AF.Sigmoid), ('dl',), ('w1',))
                k.op('V', lambda v: v.tensor_scalar(scs['w2'], scs['w1'], scalar1=-1.0, scalar2=1.0,
                                                    op0=ALU.mult, op1=ALU.add), ('w1',), ('w2',))
                k.op('V', lambda v: v.tensor_scalar(eq1b[:], eq1b[:], scalar1=scs['w1'], scalar2=None,
                                                    op0=ALU.mult), ('eq1b', 'w1'), ('eq1b',))
                k.op('V', lambda v, tt=tt: v.scalar_tensor_tensor(combine[:, tt, :], eq2b[:],
                                                                  scs['w2'], eq1b[:],
                                                                  op0=ALU.mult, op1=ALU.add),
                     ('eq2b', 'w2', 'eq1b'), ('combine',))
            k.barrier()

        with ExitStack() as ph:
            psb = lambda n, sh, dt=F32: ph.enter_context(nc.psum_tensor(_un(n), sh, dt))
            ps_g = psb("ps_ge", [128, max(NC, 4), 512])
            ps_u = psb("ps_ue", [128, max(NC, 4), 512])
            gwb = ph.enter_context(nc.sbuf_tensor(_un("gwbe"), [128, 2, 16, 128], BF16))
            uwb = ph.enter_context(nc.sbuf_tensor(_un("uwbe"), [128, 2, 16, 128], BF16))
            sgbe2 = ph.enter_context(nc.sbuf_tensor(_un("sgbe"), [128, NC, 512], F32))
            este = ph.enter_context(nc.sbuf_tensor(_un("este"), [128, 2, 512], F32))
            GeB = ph.enter_context(nc.sbuf_tensor(_un("GeB"), [128, 11 * T], BF16))
            for ex in range(NEXP):
                k.dma('G', 'gW', BIG2[:, 0:11 * E].rearrange("p (a b) -> p a b", b=E),
                      wd1.rearrange("e (a p) c -> e p a c", p=128)[ex], writes=('ewd',))
                for it in range(11):
                    wsl = it % 2
                    k.dma('G', 'gA' if wsl == 0 else 'gB', gwb[:, wsl],
                          wg1.rearrange("e (a p) c -> e p a c", p=128)[ex][:, :, it * 128:(it + 1) * 128],
                          writes=('gw%d' % wsl,))
                    k.dma('G', 'gC' if wsl == 0 else 'gD', uwb[:, wsl],
                          wu1.rearrange("e (a p) c -> e p a c", p=128)[ex][:, :, it * 128:(it + 1) * 128],
                          writes=('uw%d' % wsl,))
                    for et in range(16):
                        for tc in range(NC):
                            k.mm(gwb[:, wsl, et, :], XT(et, tc * 512, (tc + 1) * 512),
                                 ps_g[:, tc, :], ('gw%d' % wsl, 'XT'), 'ps_ge%d' % tc,
                                 start=(et == 0), stop=(et == 15))
                        for tc in range(NC):
                            k.mm(uwb[:, wsl, et, :], XT(et, tc * 512, (tc + 1) * 512),
                                 ps_u[:, tc, :], ('uw%d' % wsl, 'XT'), 'ps_ue%d' % tc,
                                 start=(et == 0), stop=(et == 15))
                    for tc in range(NC):
                        k.op('S', lambda s, tc=tc: s.activation(sgbe2[:, tc, :], ps_g[:, tc, :], AF.Silu),
                             ('ps_ge%d' % tc,), ('sgb%d' % tc,))
                        k.op('V', lambda v, it=it, tc=tc: v.tensor_tensor(
                                GeB[:, it * T + tc * 512: it * T + (tc + 1) * 512], sgbe2[:, tc, :],
                                ps_u[:, tc, :], op=ALU.mult), ('sgb%d' % tc, 'ps_ue%d' % tc), ('Ge',))
                for tt in range(NT):
                    for it in range(11):
                        for hc in range(4):
                            k.mm(GeB[:, it * T + tt * 128: it * T + (tt + 1) * 128],
                                 BIG2[:, it * E + hc * 512: it * E + (hc + 1) * 512],
                                 ps_g[:, hc, :] if tt % 2 == 0 else ps_u[:, hc, :],
                                 ('Ge', 'ewd'), 'ps_%se%d' % ('g' if tt % 2 == 0 else 'u', hc),
                                 start=(it == 0), stop=(it == 10))
                    for hp in range(2):
                        for hh in range(2):
                            hc = hp * 2 + hh
                            psrc = ps_g[:, hc, :] if tt % 2 == 0 else ps_u[:, hc, :]
                            pk = 'ps_%se%d' % ('g' if tt % 2 == 0 else 'u', hc)
                            k.op('V', lambda v, tt=tt, ex=ex, hh=hh, psrc=psrc: v.tensor_scalar(
                                    este[:, hh, :], psrc,
                                    scalar1=combine[:, tt, ex:ex + 1], scalar2=None, op0=ALU.mult),
                                 (pk, 'combine'), ('este%d' % hh,))
                        sem = 'gR' if hp == 0 else 'gM'
                        if ex == 0:
                            k.dma('G', sem, racc[tt * 128:(tt + 1) * 128, hp * 1024:(hp + 1) * 1024],
                                  este[:].rearrange("p a b -> p (a b)"), reads=('este0', 'este1'))
                        else:
                            k.dma('G', sem, racc[tt * 128:(tt + 1) * 128, hp * 1024:(hp + 1) * 1024],
                                  este[:].rearrange("p a b -> p (a b)"), reads=('este0', 'este1'),
                                  accum_op=ALU.add)
            k.barrier()

        swiglu_A(swg, swu, FSH // 128, G1)
        down_B(G1, FSH // 128, swd, racc, scale_col=lambda tt: sgate[:, tt, :], accum=True)

        boundary(res3, racc, y_out, 'bf.', final=True)

    with nc.Block() as block:
        @block.sync
        def _(e):
            for c in k.q['Y']:
                c(e)

        @block.gpsimd
        def _(e):
            for c in k.q['G']:
                c(e)

        @block.tensor
        def _(e):
            for c in k.q['T']:
                c(e)

        @block.vector
        def _(e):
            for c in k.q['V']:
                c(e)

        @block.scalar
        def _(e):
            for c in k.q['S']:
                c(e)

    k.stack.close()
    return nc


def _host_consts(T):
    inv = 1.0 / (10000.0 ** (np.arange(0, D0, 2, dtype=np.float32) / D0))
    t = np.arange(T, dtype=np.float32)[:, None] * inv[None, :].astype(np.float32)
    ropec = np.ascontiguousarray(np.tile(np.cos(t).astype(np.float32), (1, 4))).astype(NPBF)
    ropes = np.ascontiguousarray(np.tile(np.sin(t).astype(np.float32), (1, 4))).astype(NPBF)
    qq = np.arange(512)
    pp = np.arange(128)[:, None]
    mask = np.zeros((128, 4 * 512), np.float32)
    for j in range(4):
        mask[:, j * 512:(j + 1) * 512] = (qq[None, :] >= 128 * j + pp).astype(np.float32)
    mask0 = mask.astype(NPBF)
    bidx = np.zeros((128, 15), np.int32)
    for b in range(15):
        delta = 7 - b
        bidx[:, b] = 128 * delta + 2047 - np.arange(128)
    return ropec, ropes, mask0, bidx


def host_inputs(inp, T=2048):
    f32 = lambda a: np.ascontiguousarray(np.asarray(a, dtype=np.float32))
    ropec, ropes, mask0, bidx = _host_consts(T)
    return {
        "w_qkv": f32(inp["L0_attn_w_qkv"]),
        "w_out0": f32(inp["L0_attn_w_out"]),
        "ln0_s": np.ascontiguousarray(np.tile(f32(inp["L0_attn_ln_scale"]).reshape(1, D0), (1, 4))),
        "ln0_b": np.ascontiguousarray(np.tile(f32(inp["L0_attn_ln_bias"]).reshape(1, D0), (1, 4))),
        "wgT0": np.ascontiguousarray(f32(inp["L0_ffn_w_gate"]).T),
        "wuT0": np.ascontiguousarray(f32(inp["L0_ffn_w_up"]).T),
        "wdT0": np.ascontiguousarray(f32(inp["L0_ffn_w_down"]).T),
        "wqT1": np.ascontiguousarray(f32(inp["L1_attn_wq"]).T),
        "wkT1": np.ascontiguousarray(f32(inp["L1_attn_wk"]).T),
        "wvT1": np.ascontiguousarray(f32(inp["L1_attn_wv"]).T),
        "woT1": np.ascontiguousarray(f32(inp["L1_attn_wo"]).T),
        "relpos": f32(inp["L1_attn_rel_pos"]),
        "rwT": np.ascontiguousarray(f32(inp["L1_ffn_router_w"]).T),
        "sgw": f32(inp["L1_ffn_shared_gate_w"]).reshape(E, 1),
        "wg1": f32(inp["L1_ffn_w_gate"]),
        "wu1": f32(inp["L1_ffn_w_up"]),
        "wd1": f32(inp["L1_ffn_w_down"]),
        "swg": f32(inp["L1_ffn_sw_gate"]),
        "swu": f32(inp["L1_ffn_sw_up"]),
        "swd": f32(inp["L1_ffn_sw_down"]),
        "ropec_d": ropec, "ropes_d": ropes, "mask0_d": mask0, "bidx_d": bidx,
    }


def kernel(**inp):
    T = 2048
    x = np.ascontiguousarray(np.asarray(inp["L0_attn_x"], dtype=np.float32))
    common = host_inputs(inp, T)
    nc = build_program(T=T)
    in_maps = [dict(common, x=np.ascontiguousarray(x[b])) for b in range(8)]
    res = run_bass_kernel_spmd(nc, in_maps, core_ids=list(range(8)))
    return np.stack([np.asarray(r["y"], dtype=np.float32) for r in res.results], axis=0)


# version 11 (final)
# speedup vs baseline: 1.0754x; 1.0053x over previous
import numpy as np
import ml_dtypes
from contextlib import ExitStack
import concourse.bass as bass
import concourse.mybir as mybir
from concourse.bass import AP, IndirectOffsetOnAxis
from concourse.bass_utils import run_bass_kernel_spmd

F32 = mybir.dt.float32
BF16 = mybir.dt.bfloat16
I32 = mybir.dt.int32
AF = mybir.ActivationFunctionType
ALU = mybir.AluOpType
AX = mybir.AxisListType
NPBF = ml_dtypes.bfloat16

E = 2048
H0, D0 = 16, 128
H1, D1 = 32, 64
FF0 = 8192
NEXP, FI, FSH = 8, 1408, 5632
LN_EPS = 1e-5


class _Rec:

    def __init__(self):
        self.calls = []

    def __getattr__(self, name):
        def meth(*a, **kw):
            self.calls.append((name, a, kw))
            return None
        return meth


class Ctx:

    def __init__(self, nc, T):
        self.nc = nc
        self.T = T
        self.NT = T // 128
        self.NC = T // 512
        self.stack = ExitStack()
        self.cnt = {}
        self.sems = {}
        self.q = {'Y': [], 'G': [], 'T': [], 'V': [], 'S': []}
        self.engsem = {'T': 'sT', 'V': 'sV', 'S': 'sS', 'G': 'sG'}
        self.hw = {}
        self.bufs = {}

    def sem(self, name):
        self.sems[name] = self.stack.enter_context(self.nc.semaphore(name))
        self.cnt[name] = 0

    DMA_SEMS = frozenset(('yA', 'yB', 'yC', 'yD', 'yE', 'yF', 'yO', 'yP', 'yM',
                          'gA', 'gB', 'gC', 'gD', 'gM', 'gR', 'gW'))

    def wait(self, ek, sname, v):
        if sname in self.DMA_SEMS:
            v = self.cnt[sname]
        if v <= 0 or self.hw.get((ek, sname), 0) >= v:
            return
        sem = self.sems[sname]
        self.q[ek].append(lambda eng, v=v: eng.wait_ge(sem, v))
        self.hw[(ek, sname)] = v

    def dep(self, ek, reads, writes):
        best = {}

        def add(tok):
            if tok is not None and best.get(tok[0], 0) < tok[1]:
                best[tok[0]] = tok[1]

        for b in reads:
            st = self.bufs.get(b)
            if st:
                add(st['w'])
        for b in writes:
            st = self.bufs.get(b)
            if st:
                add(st['w'])
                for s, v in st['r'].items():
                    add((s, v))
        for s, v in best.items():
            self.wait(ek, s, v)

    def note(self, tok, reads, writes):
        for b in writes:
            self.bufs[b] = {'w': tok, 'r': {}}
        if tok is not None:
            s, v = tok
            for b in reads:
                st = self.bufs.setdefault(b, {'w': None, 'r': {}})
                if st['r'].get(s, 0) < v:
                    st['r'][s] = v

    def op(self, ek, fn, reads=(), writes=(), inc=True):
        self.dep(ek, reads, writes)
        rec = _Rec()
        fn(rec)
        assert len(rec.calls) == 1, "op() lambda must make exactly one engine call"
        name, a, kw = rec.calls[0]
        if inc:
            sname = self.engsem[ek]
            self.cnt[sname] += 1
            sem = self.sems[sname]
            self.q[ek].append(lambda eng, name=name, a=a, kw=kw:
                              getattr(eng, name)(*a, **kw).then_inc(sem, 1))
            tok = (sname, self.cnt[sname])
        else:
            self.q[ek].append(lambda eng, name=name, a=a, kw=kw: getattr(eng, name)(*a, **kw))
            tok = None
        self.note(tok, reads, writes)
        return tok

    def mm(self, lhsT, rhs, out, reads, psum_key, start, stop, force_inc=False):
        self.dep('T', reads, (psum_key,) if start else ())
        if stop or force_inc:
            self.cnt['sT'] += 1
            sem = self.sems['sT']
            self.q['T'].append(lambda eng: eng.matmul(out, lhsT, rhs, start=start, stop=stop,
                                                      skip_group_check=True).then_inc(sem, 1))
            tok = ('sT', self.cnt['sT'])
            self.note(tok, reads, (psum_key,) if stop else ())
            return tok
        self.q['T'].append(lambda eng: eng.matmul(out, lhsT, rhs, start=start, stop=stop,
                                                  skip_group_check=True))
        self.note(None, reads, ())
        return None

    def dma(self, ek, sname, out, in_, reads=(), writes=(), accum_op=None, indirect=None):
        self.dep(ek, reads, writes)
        self.wait(ek, sname, self.cnt[sname])
        sem = self.sems[sname]

        def closure(eng, out=out, in_=in_, accum_op=accum_op, indirect=indirect):
            if indirect is not None:
                instr = eng.indirect_dma_start(out=out, out_offset=None, in_=in_,
                                               in_offset=indirect)
            elif accum_op is not None:
                instr = eng.dma_start(out=out, in_=in_, accum_op=accum_op)
            else:
                instr = eng.dma_start(out=out, in_=in_)
            instr.then_inc(sem, 16)
        self.q[ek].append(closure)
        self.cnt[sname] += 16
        tok = (sname, self.cnt[sname])
        self.note(tok, reads, writes)
        return tok

    def barrier(self):
        snap = list(self.cnt.items())
        for ek in ('Y', 'G', 'T', 'V', 'S'):
            for sname, v in snap:
                self.wait(ek, sname, v)
        self.bufs = {}


def build_program(T=2048, taps=()):
    nc = bass.Bass("TRN2", target_bir_lowering=False, debug=False)
    k = Ctx(nc, T)
    NT, NC = k.NT, k.NC
    _uid = [0]

    def _un(n):
        _uid[0] += 1
        return "%s_u%d" % (n, _uid[0])
    taps = set(taps)

    def dram(name, shape, dtype):
        kind = "ExternalOutput" if name in taps else "Internal"
        return nc.dram_tensor(name, shape, dtype, kind=kind).ap()

    def inp(name, shape, dtype):
        return nc.dram_tensor(name, shape, dtype, kind="ExternalInput").ap()

    x_in = inp("x", [T, E], F32)
    w_qkv = inp("w_qkv", [E, 3 * E], F32)
    w_out0 = inp("w_out0", [E, E], F32)
    ln0_s = inp("ln0_s", [1, 512], F32)
    ln0_b = inp("ln0_b", [1, 512], F32)
    wgT0 = inp("wgT0", [E, FF0], F32)
    wuT0 = inp("wuT0", [E, FF0], F32)
    wdT0 = inp("wdT0", [FF0, E], F32)
    wqT1 = inp("wqT1", [E, E], F32)
    wkT1 = inp("wkT1", [E, E], F32)
    wvT1 = inp("wvT1", [E, E], F32)
    woT1 = inp("woT1", [E, E], F32)
    relpos = inp("relpos", [1023, 64], F32)
    rwT = inp("rwT", [E, NEXP], F32)
    sgw = inp("sgw", [E, 1], F32)
    wg1 = inp("wg1", [NEXP, E, FI], F32)
    wu1 = inp("wu1", [NEXP, E, FI], F32)
    wd1 = inp("wd1", [NEXP, FI, E], F32)
    swg = inp("swg", [E, FSH], F32)
    swu = inp("swu", [E, FSH], F32)
    swd = inp("swd", [FSH, E], F32)
    ropec_d = inp("ropec_d", [T, 256], BF16)
    ropes_d = inp("ropes_d", [T, 256], BF16)
    mask0_d = inp("mask0_d", [128, 4 * 512], BF16)
    bidx_d = inp("bidx_d", [128, 15], I32)

    y_out = nc.dram_tensor("y", [T, E], F32, kind="ExternalOutput").ap()
    res1 = dram("res1", [T, E], F32)
    res2 = dram("res2", [T, E], F32)
    res3 = dram("res3", [T, E], F32)
    pre1 = dram("pre1", [T, E], F32)
    pre2 = dram("pre2", [T, E], F32)
    pre3 = dram("pre3", [T, E], F32)
    QT0 = dram("QT0", [H0, D0, T], BF16)
    KT0 = dram("KT0", [H0, D0, T], BF16)
    V0 = dram("V0", [H0, T, D0], BF16)
    G0 = dram("G0", [T // 128, 128, FF0 // 128, 128], BF16)
    QT1 = dram("QT1", [E, T], BF16)
    KT1 = dram("KT1", [E, T], BF16)
    V1 = dram("V1", [T, E], BF16)
    G1 = dram("G1", [T // 128, 128, FSH // 128, 128], BF16)
    racc = dram("racc", [T, E], F32)
    relm = dram("relm", [1023, 1], F32)
    expB_d = dram("expB_d", [128, 15 * 128], BF16)
    ext_d = nc.dram_tensor("ext", [4095], F32, kind="Internal")
    ext = ext_d.ap()
    ext2d = ext.rearrange("(a b) -> a b", b=1)

    for s in ('sT', 'sV', 'sS', 'sG', 'yA', 'yB', 'yC', 'yD', 'yE', 'yF', 'yO', 'yP',
              'yM', 'gA', 'gB', 'gC', 'gD', 'gM', 'gR', 'gW'):
        k.sem(s)

    st = k.stack
    sb = lambda name, shape, dt: st.enter_context(nc.sbuf_tensor(name, shape, dt))

    BIG1 = sb("BIG1", [128, 32768], BF16)
    BIG2 = sb("BIG2", [128, 32768], BF16)

    ln0sb = sb("ln0sb", [128, 512], F32)
    ln0bb = sb("ln0bb", [128, 512], F32)
    idbf = sb("idbf", [128, 128], BF16)
    onescol = sb("onescol", [128, 1], BF16)
    ones1f = sb("ones1f", [1, 128], F32)
    combine = sb("combine", [128, NT, NEXP], F32)
    sgate = sb("sgate", [128, NT, 1], F32)
    idf32 = sb("idf32", [128, 128], F32)
    mid_stack = ExitStack()
    cLow = mid_stack.enter_context(nc.sbuf_tensor("cLow", [128, 1], F32))
    cHigh = mid_stack.enter_context(nc.sbuf_tensor("cHigh", [128, 1], F32))
    early_stack = ExitStack()
    ropec = early_stack.enter_context(nc.sbuf_tensor("ropec", [128, NT, 256], BF16))
    ropes = early_stack.enter_context(nc.sbuf_tensor("ropes", [128, NT, 256], BF16))

    def XT(et, t0, t1):
        return BIG1[:, et * T + t0: et * T + t1]

    def CTXT(p0, p1, ct, t0, t1):
        return BIG2[p0:p1, ct * T + t0: ct * T + t1]

    if True:
        def emit_ln(zap, zkey, W, outs, sc, p):
            k.op('V', lambda v: v.reduce_sum(sc['s1'], zap, axis=AX.X), (zkey,), (p + 's1',))
            emit_ln_post(zap, zkey, W, outs, sc, p)

        def emit_ln_post(zap, zkey, W, outs, sc, p):
            k.op('S', lambda s: s.activation(sc['sq'], zap, AF.Square, accum_out=sc['s2']),
                 (zkey,), (p + 'sq', p + 's2'))
            k.op('V', lambda v: v.tensor_scalar(sc['m'], sc['s1'], scalar1=1.0 / W, scalar2=None,
                                                op0=ALU.mult), (p + 's1',), (p + 'm',))
            k.op('V', lambda v: v.tensor_scalar(sc['t2'], sc['s2'], scalar1=1.0 / W, scalar2=None,
                                                op0=ALU.mult), (p + 's2',), (p + 't2',))
            k.op('V', lambda v: v.tensor_tensor(sc['var'], sc['m'], sc['m'], op=ALU.mult),
                 (p + 'm',), (p + 'var',))
            k.op('V', lambda v: v.tensor_tensor(sc['var'], sc['t2'], sc['var'], op=ALU.subtract),
                 (p + 't2', p + 'var'), (p + 'var',))
            k.op('V', lambda v: v.tensor_scalar(sc['var'], sc['var'], scalar1=LN_EPS, scalar2=None,
                                                op0=ALU.add), (p + 'var',), (p + 'var',))
            k.op('S', lambda s: s.activation(sc['sd'], sc['var'], AF.Sqrt),
                 (p + 'var',), (p + 'sd',))
            k.op('V', lambda v: v.reciprocal(sc['rstd'], sc['sd']), (p + 'sd',), (p + 'rstd',))
            k.op('V', lambda v: v.scalar_tensor_tensor(sc['nm'], sc['m'], -1.0, sc['rstd'],
                                                       op0=ALU.mult, op1=ALU.mult),
                 (p + 'm', p + 'rstd'), (p + 'nm',))
            for oap, okey in outs:
                k.op('S', lambda s, oap=oap: s.activation(oap, zap, AF.Identity, bias=sc['nm'],
                                                          scale=sc['rstd']),
                     (zkey, p + 'nm', p + 'rstd'), (okey,))

        def mk_lnsc(ph, name, W):
            d = {n: ph.enter_context(nc.sbuf_tensor(_un(name + n), [128, 1], F32))[:]
                 for n in ('s1', 's2', 'm', 't2', 'var', 'sd', 'rstd', 'nm')}
            d['sq'] = ph.enter_context(nc.sbuf_tensor(_un(name + 'sq'), [128, W], F32))[:]
            return d

        with ExitStack() as ph:
            psb = lambda n, sh, dt=F32: ph.enter_context(nc.psum_tensor(_un(n), sh, dt))
            ps_tr = psb("ps_tr", [128, 2, 1024], BF16)
            ps_trf = psb("ps_trf", [128, 512])
            k.dma('Y', 'yM', ropec[:], ropec_d.rearrange("(a p) f -> p a f", p=128), writes=('ropec',))
            k.dma('Y', 'yM', ropes[:], ropes_d.rearrange("(a p) f -> p a f", p=128), writes=('ropes',))
            k.dma('Y', 'yM', ln0sb[:], ln0_s.broadcast_to([128, 512]), writes=('ln0sb',))
            k.dma('Y', 'yM', ln0bb[:], ln0_b.broadcast_to([128, 512]), writes=('ln0bb',))
            k.op('G', lambda g: g.memset(idbf[:], 0.0), (), ('idbf',))
            k.op('G', lambda g: g.affine_select(out=idbf[:], in_=idbf[:], compare_op=ALU.not_equal,
                                                fill=1.0, base=0, pattern=[[-1, 128]],
                                                channel_multiplier=1), ('idbf',), ('idbf',))
            k.op('G', lambda g: g.memset(onescol[:], 1.0), (), ('onescol',))
            k.op('G', lambda g: g.memset(ones1f[:], 1.0), (), ('ones1f',))

            xstage = ph.enter_context(nc.sbuf_tensor(_un("xstage"), [128, 2, E], BF16))
            for tt in range(NT):
                sl = tt % 2
                k.dma('G', 'gA' if sl == 0 else 'gB', xstage[:, sl, :],
                      x_in[tt * 128:(tt + 1) * 128, :], writes=('xs%d' % sl,))
                for et in range(16):
                    pk = 'ps_tr%d' % (et % 2)
                    k.op('T', lambda t: t.transpose(ps_tr[:, et % 2, 0:128],
                                                    xstage[:, sl, et * 128:(et + 1) * 128], idbf[:]),
                         ('xs%d' % sl, 'idbf'), (pk,))
                    k.op('V', lambda v: v.tensor_copy(XT(et, tt * 128, (tt + 1) * 128),
                                                      ps_tr[:, et % 2, 0:128]), (pk,), ('XT',))

            rpbuf = ph.enter_context(nc.sbuf_tensor(_un("rpbuf"), [128, 64], F32))
            rmcol = ph.enter_context(nc.sbuf_tensor(_un("rmcol"), [128, 1], F32))
            relm_toks = []
            for i in range(8):
                rows = 128 if i < 7 else 127
                k.dma('Y', 'yM', rpbuf[0:rows, :], relpos[i * 128:i * 128 + rows, :], writes=('rpbuf',))
                k.op('V', lambda v: v.reduce_sum(rmcol[0:rows, :], rpbuf[0:rows, :], axis=AX.X),
                     ('rpbuf',), ('rmcol',))
                k.op('V', lambda v: v.tensor_scalar(rmcol[0:rows, :], rmcol[0:rows, :], scalar1=1.0 / 64,
                                                    scalar2=None, op0=ALU.mult), ('rmcol',), ('rmcol',))
                k.dma('Y', 'yM', relm[i * 128:i * 128 + rows, :], rmcol[0:rows, :],
                      reads=('rmcol',), writes=('relm',))
            k.dma('Y', 'yM', ext2d[1536:2559, :], relm[:, :], reads=('relm',), writes=('extm',))
            padt = ph.enter_context(nc.sbuf_tensor(_un("padt"), [128, 12], F32))
            k.dma('Y', 'yM', cLow[:], relm[0:1, :].broadcast_to([128, 1]), reads=('relm',), writes=('cLow',))
            k.dma('Y', 'yM', cHigh[:], relm[1022:1023, :].broadcast_to([128, 1]), reads=('relm',),
                  writes=('cHigh',))
            k.op('G', lambda g: g.memset(padt[:], 0.0), (), ('padt',))
            k.op('V', lambda v: v.tensor_scalar(padt[:], padt[:], scalar1=cLow[:], scalar2=None,
                                                op0=ALU.add), ('padt', 'cLow'), ('padt',))
            k.dma('Y', 'yM', AP(ext_d, 0, [[12, 128], [1, 12]]), padt[:],
                  reads=('padt',), writes=('extl',))
            k.op('G', lambda g: g.memset(padt[:], 0.0), (), ('padt',))
            k.op('V', lambda v: v.tensor_scalar(padt[:], padt[:], scalar1=cHigh[:], scalar2=None,
                                                op0=ALU.add), ('padt', 'cHigh'), ('padt',))
            k.dma('Y', 'yM', AP(ext_d, 2559, [[12, 128], [1, 12]]), padt[:],
                  reads=('padt',), writes=('extr',))
            natb = ph.enter_context(nc.sbuf_tensor(_un("natb"), [128, 128], F32))
            btmp = ph.enter_context(nc.sbuf_tensor(_un("btmp"), [128, 128], F32))
            bidx = ph.enter_context(nc.sbuf_tensor(_un("bidx"), [128, 15], I32))
            expstg = ph.enter_context(nc.sbuf_tensor(_un("expstg"), [128, 128], BF16))
            k.dma('Y', 'yM', bidx[:], bidx_d[:, :], writes=('bidx',))
            k.op('G', lambda g: g.memset(idf32[:], 0.0), (), ('idf32',))
            k.op('G', lambda g: g.affine_select(out=idf32[:], in_=idf32[:], compare_op=ALU.not_equal,
                                                fill=1.0, base=0, pattern=[[-1, 128]],
                                                channel_multiplier=1), ('idf32',), ('idf32',))
            for d in range(15):
                k.dma('G', 'gM', natb[:], ext2d[:, :], reads=('bidx', 'extm', 'extl', 'extr'),
                      writes=('natb',), indirect=IndirectOffsetOnAxis(bidx[:, d:d + 1], 0))
                k.op('T', lambda t: t.transpose(ps_trf[:, 0:128], natb[:], idf32[:]),
                     ('natb', 'idf32'), ('ps_trf',))
                k.op('V', lambda v: v.tensor_copy(btmp[:], ps_trf[:, 0:128]), ('ps_trf',), ('btmp',))
                k.op('S', lambda s, d=d: s.activation(expstg[:], btmp[:], AF.Exp),
                     ('btmp', 'expstg'), ('expstg',))
                k.dma('Y', 'yM', expB_d[:, d * 128:(d + 1) * 128], expstg[:], reads=('expstg',))
            k.barrier()

        with ExitStack() as ph:
            psb = lambda n, sh, dt=F32: ph.enter_context(nc.psum_tensor(_un(n), sh, dt))
            ps_mm = psb("ps_mm", [128, 2, 512])
            ps_tr = psb("ps_tr", [128, 2, 1024], BF16)
            wqkvb = ph.enter_context(nc.sbuf_tensor(_un("wqkvb"), [128, 2, 16, 512], BF16))
            ropeb = ph.enter_context(nc.sbuf_tensor(_un("ropeb"), [128, 2, 2, 4, 64], F32))
            lnin = ph.enter_context(nc.sbuf_tensor(_un("lnin"), [128, 2, 4, 128], F32))
            a1 = ph.enter_context(nc.sbuf_tensor(_un("a1"), [128, 2, 4, 128], F32))
            qkbf = ph.enter_context(nc.sbuf_tensor(_un("qkbf"), [128, 2, 4, 128], BF16))
            qst4 = ph.enter_context(nc.sbuf_tensor(_un("qst4"), [128, 2, 4, 128], BF16))
            xev = ph.enter_context(nc.sbuf_tensor(_un("xev"), [128, 2, 512], F32))
            lsc = {n: ph.enter_context(nc.sbuf_tensor(_un("p1l" + n), [128, 2, 4], F32))[:]
                   for n in ('s1', 's2', 'm', 't2', 'var', 'sd', 'rstd', 'nm')}
            pend_tr = []

            def flush_tr():
                for (cc2, tt2, u2) in pend_tr:
                    for j in range(4):
                        trk = 'ps_tr%d' % (j % 2)
                        k.op('T', lambda t, u2=u2, j=j: t.transpose(
                                ps_tr[:, j % 2, 0:128], qkbf[:, u2, j, :], idbf[:]),
                             ('qkbf%d' % u2, 'idbf'), (trk,))
                        k.op('V', lambda v, u2=u2, j=j: v.tensor_copy(
                                qst4[:, u2, j, :], ps_tr[:, j % 2, 0:128]),
                             (trk,), ('qst4%d' % u2,))
                    tgt2 = QT0 if cc2 < 4 else KT0
                    h0 = (cc2 % 4) * 4
                    k.dma('Y', 'yO' if tt2 % 2 == 0 else 'yP',
                          tgt2[h0:h0 + 4, :, tt2 * 128:(tt2 + 1) * 128].rearrange("h d t -> d h t"),
                          qst4[:, u2], reads=('qst4%d' % u2,))
                del pend_tr[:]

            for cc in range(12):
                wsl = cc % 2
                k.dma('G', 'gA' if wsl == 0 else 'gB', wqkvb[:, wsl],
                      w_qkv.rearrange("(a p) c -> p a c", p=128)[:, :, cc * 512:(cc + 1) * 512],
                      writes=('wqkv%d' % wsl,))
                is_v = cc >= 8
                for tt in range(NT):
                    pk = 'ps_mm%d' % (tt % 2)
                    for et in range(16):
                        k.mm(XT(et, tt * 128, (tt + 1) * 128), wqkvb[:, wsl, et, :],
                             ps_mm[:, tt % 2, :], ('XT', 'wqkv%d' % wsl), pk,
                             start=(et == 0), stop=(et == 15))
                    flush_tr()
                    if is_v:
                        u = tt % 2
                        vst = qkbf[:, u].rearrange("p a b -> p (a b)")
                        k.op('V', lambda v, u=u, vst=vst: v.tensor_copy(vst, ps_mm[:, tt % 2, :]),
                             (pk, 'qkbf%d' % u), ('qkbf%d' % u,))
                        for j in range(4):
                            h = (cc - 8) * 4 + j
                            k.dma('Y', 'yO' if j % 2 == 0 else 'yP',
                                  V0.rearrange("h (a p) d -> h p a d", p=128)[h, :, tt, :],
                                  vst[:, j * 128:(j + 1) * 128], reads=('qkbf%d' % u,))
                    else:
                        u = tt % 2
                        rc4 = ropec[:, tt, :].rearrange("p (j f) -> p j f", f=64)
                        rs4 = ropes[:, tt, :].rearrange("p (j f) -> p j f", f=64)
                        xkey = 'xev%d' % u
                        k.op('S', lambda s, u=u: s.activation(xev[:, u, :], ps_mm[:, tt % 2, :],
                                                              AF.Identity), (pk,), (xkey,))
                        pmm = xev[:, u, :]
                        q1v = pmm.rearrange("p (j f) -> p j f", f=128)[:, :, 0:64]
                        q2v = pmm.rearrange("p (j f) -> p j f", f=128)[:, :, 64:128]
                        rb = lambda i: ropeb[:, u, i % 2, :, :]
                        lniu = lnin[:, u]
                        k.op('V', lambda v: v.tensor_tensor(rb(0), q1v, rc4, op=ALU.mult),
                             (xkey, 'ropec'), ('rb0%d' % u,))
                        k.op('V', lambda v: v.tensor_tensor(rb(1), q2v, rs4, op=ALU.mult),
                             (xkey, 'ropes'), ('rb1%d' % u,))
                        k.op('V', lambda v: v.tensor_tensor(lniu[:, :, 0:64], rb(0), rb(1),
                                                            op=ALU.subtract),
                             ('rb0%d' % u, 'rb1%d' % u), ('lnin%d' % u,))
                        k.op('V', lambda v: v.tensor_tensor(rb(2), q1v, rs4, op=ALU.mult),
                             (xkey, 'ropes'), ('rb0%d' % u,))
                        k.op('V', lambda v: v.tensor_tensor(rb(3), q2v, rc4, op=ALU.mult),
                             (xkey, 'ropec'), ('rb1%d' % u,))
                        k.op('V', lambda v: v.tensor_tensor(lniu[:, :, 64:128], rb(2), rb(3),
                                                            op=ALU.add),
                             ('rb0%d' % u, 'rb1%d' % u), ('lnin%d' % u,))
                        for j in range(4):
                            k.op('S', lambda s, j=j, u=u: s.activation(
                                    a1[:, u, j, :], lniu[:, j, :], AF.Square,
                                    accum_out=lsc['s2'][:, u, j:j + 1]),
                                 ('lnin%d' % u, 'a1%d' % u), ('ls2%d' % u, 'a1%d' % u))
                        k.op('V', lambda v, u=u: v.reduce_sum(lsc['s1'][:, u], lniu[:],
                                                              axis=AX.X), ('lnin%d' % u,), ('ls1%d' % u,))
                        k.op('V', lambda v, u=u: v.tensor_scalar(lsc['m'][:, u], lsc['s1'][:, u],
                                                                 scalar1=1.0 / D0, scalar2=None,
                                                                 op0=ALU.mult), ('ls1%d' % u,), ('lm%d' % u,))
                        k.op('V', lambda v, u=u: v.tensor_scalar(lsc['t2'][:, u], lsc['s2'][:, u],
                                                                 scalar1=1.0 / D0, scalar2=None,
                                                                 op0=ALU.mult), ('ls2%d' % u,), ('lt2%d' % u,))
                        k.op('V', lambda v, u=u: v.tensor_tensor(lsc['var'][:, u], lsc['m'][:, u],
                                                                 lsc['m'][:, u], op=ALU.mult),
                             ('lm%d' % u,), ('lvar%d' % u,))
                        k.op('V', lambda v, u=u: v.tensor_tensor(lsc['var'][:, u], lsc['t2'][:, u],
                                                                 lsc['var'][:, u], op=ALU.subtract),
                             ('lt2%d' % u, 'lvar%d' % u), ('lvar%d' % u,))
                        k.op('V', lambda v, u=u: v.tensor_scalar(lsc['var'][:, u], lsc['var'][:, u],
                                                                 scalar1=LN_EPS, scalar2=None,
                                                                 op0=ALU.add), ('lvar%d' % u,), ('lvar%d' % u,))
                        k.op('S', lambda s, u=u: s.activation(lsc['sd'][:, u], lsc['var'][:, u],
                                                              AF.Sqrt), ('lvar%d' % u,), ('lsd%d' % u,))
                        k.op('V', lambda v, u=u: v.reciprocal(lsc['rstd'][:, u], lsc['sd'][:, u]),
                             ('lsd%d' % u,), ('lrstd%d' % u,))
                        k.op('V', lambda v, u=u: v.scalar_tensor_tensor(lsc['nm'][:, u], lsc['m'][:, u],
                                                                        -1.0, lsc['rstd'][:, u],
                                                                        op0=ALU.mult, op1=ALU.mult),
                             ('lm%d' % u, 'lrstd%d' % u), ('lnm%d' % u,))
                        for j in range(4):
                            k.op('S', lambda s, j=j, u=u: s.activation(
                                    a1[:, u, j, :], lniu[:, j, :], AF.Identity,
                                    bias=lsc['nm'][:, u, j:j + 1],
                                    scale=lsc['rstd'][:, u, j:j + 1]),
                                 ('lnin%d' % u, 'lnm%d' % u, 'lrstd%d' % u), ('a1%d' % u,))
                        a1v = a1[:, u].rearrange("p j f -> p (j f)")
                        qkv_ = qkbf[:, u].rearrange("p j f -> p (j f)")
                        k.op('V', lambda v, a1v=a1v: v.tensor_tensor(a1v, a1v, ln0sb[:], op=ALU.mult),
                             ('a1%d' % u, 'ln0sb'), ('a1%d' % u,))
                        k.op('V', lambda v, a1v=a1v, qkv_=qkv_: v.tensor_tensor(qkv_, a1v, ln0bb[:],
                                                                                op=ALU.add),
                             ('a1%d' % u, 'ln0bb', 'qkbf%d' % u), ('qkbf%d' % u,))
                        pend_tr.append((cc, tt, u))
                flush_tr()
            k.barrier()

        early_stack.close()

        with ExitStack() as ph:
            psb = lambda n, sh, dt=F32: ph.enter_context(nc.psum_tensor(_un(n), sh, dt))
            ps_s = psb("ps_s", [128, 4, 512])
            ps_ctx = psb("ps_ctx", [128, 2, 512])
            ps_den = psb("ps_den", [128, 512])
            ps_bc = psb("ps_bc", [128, 512])
            mask0 = ph.enter_context(nc.sbuf_tensor(_un("mask0"), [128, 4, 512], BF16))
            k.dma('Y', 'yM', mask0[:].rearrange("p a b -> p (a b)"), mask0_d[:, :], writes=('mask0',))
            qth = ph.enter_context(nc.sbuf_tensor(_un("qth"), [128, 2, 2, T], BF16))
            kth = ph.enter_context(nc.sbuf_tensor(_un("kth"), [128, 2, 2, T], BF16))
            vth = ph.enter_context(nc.sbuf_tensor(_un("vth"), [128, 2, 2, NT, 128], BF16))
            sqb = ph.enter_context(nc.sbuf_tensor(_un("sqb"), [128, 4, 512], F32))
            pbuf = ph.enter_context(nc.sbuf_tensor(_un("pbuf"), [128, 4, 512], BF16))
            recb = ph.enter_context(nc.sbuf_tensor(_un("recb"), [1, 2, 512], F32))
            bcb = ph.enter_context(nc.sbuf_tensor(_un("bcb"), [128, 2, 512], F32))
            for hp in range(H0 // 2):
                sl = hp % 2
                for hh in range(2):
                    h = 2 * hp + hh
                    k.dma('Y', ('yA', 'yB', 'yC', 'yD')[2 * sl + hh], qth[:, sl, hh, :],
                          QT0[h, :, :], writes=('qth%d%d' % (sl, hh),))
                    k.dma('Y', ('yC', 'yD', 'yA', 'yB')[2 * sl + hh], kth[:, sl, hh, :],
                          KT0[h, :, :], writes=('kth%d%d' % (sl, hh),))
                    k.dma('Y', 'yE' if sl == 0 else 'yF', vth[:, sl, hh],
                          V0.rearrange("h (a p) d -> h p a d", p=128)[h],
                          writes=('vth%d%d' % (sl, hh),))

                def mm1(hh, qc, kt):
                    sl_ = hp % 2
                    k.mm(kth[:, sl_, hh, kt * 128:(kt + 1) * 128],
                         qth[:, sl_, hh, qc * 512:(qc + 1) * 512],
                         ps_s[:, 2 * (kt % 2) + hh, :],
                         ('kth%d%d' % (sl_, hh), 'qth%d%d' % (sl_, hh)),
                         'ps_s%d' % (2 * (kt % 2) + hh), True, True)

                def chain2(qc, kt, nkt):
                    sl_ = hp % 2
                    s0 = 2 * (kt % 2)
                    skeys = ('ps_s%d' % s0, 'ps_s%d' % (s0 + 1))
                    pkk = ('pb%d' % s0, 'pb%d' % (s0 + 1))
                    k.op('S', lambda s, s0=s0: s.activation(
                            sqb[:, s0:s0 + 2, :], ps_s[:, s0:s0 + 2, :], AF.Square),
                         skeys, ('sqb%d' % s0,))
                    k.op('V', lambda v, s0=s0: v.tensor_tensor(
                            pbuf[:, s0:s0 + 2, :], sqb[:, s0:s0 + 2, :], sqb[:, s0:s0 + 2, :],
                            op=ALU.mult), ('sqb%d' % s0,), pkk)
                    if kt >= 4 * qc:
                        for hh in range(2):
                            k.op('V', lambda v, kt=kt, qc=qc, s0=s0, hh=hh: v.tensor_tensor(
                                    pbuf[:, s0 + hh, :], pbuf[:, s0 + hh, :],
                                    mask0[:, kt - 4 * qc, :], op=ALU.mult),
                                 (pkk[hh], 'mask0'), (pkk[hh],))
                    for hh in range(2):
                        k.mm(vth[:, sl_, hh, kt, :], pbuf[:, s0 + hh, :], ps_ctx[:, hh, :],
                             ('vth%d%d' % (sl_, hh), pkk[hh]), 'ps_ctx%d' % hh,
                             start=(kt == 0), stop=(kt == nkt - 1), force_inc=True)
                        k.mm(onescol[:], pbuf[:, s0 + hh, :], ps_den[64 * hh:64 * hh + 1, :],
                             ('onescol', pkk[hh]), 'ps_den%d' % hh,
                             start=(kt == 0), stop=(kt == nkt - 1), force_inc=True)

                def epilogue0(qc):
                    for hh in range(2):
                        h = 2 * hp + hh
                        ck = 'ps_ctx%d' % hh
                        k.op('V', lambda v, hh=hh: v.reciprocal(recb[:, hh, :],
                                                                ps_den[64 * hh:64 * hh + 1, :]),
                             ('ps_den0', 'ps_den1'), ('recb%d' % hh,))
                        k.op('T', lambda t, hh=hh: t.matmul(ps_bc[:, :], ones1f[:],
                                                            recb[:, hh, :], start=True, stop=True),
                             ('ones1f', 'recb%d' % hh), ('ps_bc',))
                        k.op('S', lambda s, hh=hh: s.copy(bcb[:, hh, :], ps_bc[:, :]),
                             ('ps_bc',), ('bcb%d' % hh,))
                        k.op('V', lambda v, h=h, hh=hh, qc=qc: v.tensor_tensor(
                                CTXT(0, 128, h, qc * 512, (qc + 1) * 512),
                                ps_ctx[:, hh, :], bcb[:, hh, :], op=ALU.mult),
                             (ck, 'bcb%d' % hh), ('CTXT',))

                for qc in range(NC):
                    nkt = 4 * qc + 4
                    for hh in range(2):
                        mm1(hh, qc, 0)
                    if qc > 0:
                        epilogue0(qc - 1)
                    for kt in range(nkt):
                        for hh in range(2):
                            if kt + 1 < nkt:
                                mm1(hh, qc, kt + 1)
                        chain2(qc, kt, nkt)
                epilogue0(NC - 1)
            k.barrier()

        def outproj(wsrc, out_dram, wname):
            with ExitStack() as ph:
                psb = lambda n, sh, dt=F32: ph.enter_context(nc.psum_tensor(_un(n), sh, dt))
                ps_mm = psb("ps_mmo", [128, 2, 4, 512])
                est2 = ph.enter_context(nc.sbuf_tensor(_un("esto"), [128, 2, 4, 512], F32))
                k.dma('G', 'gM', BIG1[:, 0:16 * E].rearrange("p (a b) -> p a b", b=E),
                      wsrc.rearrange("(a p) c -> p a c", p=128), writes=('woB',))
                for tt in range(NT):
                    for ct in range(16):
                        for ec in range(4):
                            k.mm(CTXT(0, 128, ct, tt * 128, (tt + 1) * 128),
                                 BIG1[:, ct * E + ec * 512: ct * E + (ec + 1) * 512],
                                 ps_mm[:, tt % 2, ec, :], ('CTXT', 'woB'),
                                 'ps_mmo%d_%d' % (tt % 2, ec), start=(ct == 0), stop=(ct == 15))
                    for ec in range(4):
                        pk = 'ps_mmo%d_%d' % (tt % 2, ec)
                        ekey = 'esto%d_%d' % (tt % 2, ec)
                        k.op('V', lambda v, tt=tt, ec=ec: v.tensor_copy(
                                est2[:, tt % 2, ec, :], ps_mm[:, tt % 2, ec, :]), (pk,), (ekey,))
                        k.dma('Y', 'yO' if ec % 2 == 0 else 'yP',
                              out_dram[tt * 128:(tt + 1) * 128, ec * 512:(ec + 1) * 512],
                              est2[:, tt % 2, ec, :], reads=(ekey,))
                k.barrier()

        outproj(w_out0, pre1, 'wo0')

        def boundary(res_in, pre_in, res_out, pname, final=False):
            with ExitStack() as ph:
                psb = lambda n, sh, dt=F32: ph.enter_context(nc.psum_tensor(_un(n), sh, dt))
                ps_tr = psb("ps_trb", [128, 2, 1024], BF16)
                bz = ph.enter_context(nc.sbuf_tensor(_un("bz"), [128, 2, E], F32))
                br = ph.enter_context(nc.sbuf_tensor(_un("br"), [128, 2, E], F32))
                bzf = ph.enter_context(nc.sbuf_tensor(_un("bzf"), [128, 2, E], F32))
                bzb = ph.enter_context(nc.sbuf_tensor(_un("bzb"), [128, 2, E], BF16))
                bsq = ph.enter_context(nc.sbuf_tensor(_un("bsq"), [128, E], F32))
                sc2 = [{n: ph.enter_context(nc.sbuf_tensor(_un("bln%d" % u + n), [128, 1], F32))[:]
                        for n in ('s1', 's2', 'm', 't2', 'var', 'sd', 'rstd', 'nm')}
                       for u in range(2)]
                pend = []

                def flush(nmax):
                    while len(pend) > nmax:
                        tt2, u2 = pend.pop(0)
                        k.op('V', lambda v, u2=u2: v.tensor_copy(bzb[:, u2, :], bzf[:, u2, :]),
                             ('bzf%d' % u2,), ('bzb%d' % u2,))
                        for et in range(16):
                            trk = 'ps_trb%d' % (et % 2)
                            k.op('T', lambda t, u2=u2, et=et: t.transpose(
                                    ps_tr[:, et % 2, 0:128], bzb[:, u2, et * 128:(et + 1) * 128],
                                    idbf[:]), ('bzb%d' % u2, 'idbf'), (trk,))
                            k.op('V', lambda v, et=et, tt2=tt2: v.tensor_copy(
                                    XT(et, tt2 * 128, (tt2 + 1) * 128), ps_tr[:, et % 2, 0:128]),
                                 (trk,), ('XT',))

                for tt in range(NT):
                    u = tt % 2
                    sc = sc2[u]
                    k.dma('Y', 'yA' if u == 0 else 'yB', br[:, u, :],
                          res_in[tt * 128:(tt + 1) * 128, :], writes=('br%d' % u,))
                    k.dma('Y', 'yC' if u == 0 else 'yD', bz[:, u, :],
                          pre_in[tt * 128:(tt + 1) * 128, :], writes=('bz%d' % u,))
                    zkey = 'bz%d' % u
                    k.op('V', lambda v, u=u: v.tensor_tensor(bz[:, u, :], bz[:, u, :],
                                                             br[:, u, :], op=ALU.add),
                         (zkey, 'br%d' % u), (zkey,))
                    k.op('V', lambda v, u=u: v.reduce_sum(sc['s1'], bz[:, u, :], axis=AX.X),
                         (zkey,), (pname + 's1%d' % u,))
                    k.op('S', lambda s, u=u: s.activation(bsq[:], bz[:, u, :], AF.Square,
                                                          accum_out=sc['s2']),
                         (zkey, 'bsq'), (pname + 's2%d' % u, 'bsq'))
                    k.op('V', lambda v, sc=sc: v.tensor_scalar(sc['m'], sc['s1'], scalar1=1.0 / E,
                                                               scalar2=None, op0=ALU.mult),
                         (pname + 's1%d' % u,), (pname + 'm%d' % u,))
                    k.op('V', lambda v, sc=sc: v.tensor_scalar(sc['t2'], sc['s2'], scalar1=1.0 / E,
                                                               scalar2=None, op0=ALU.mult),
                         (pname + 's2%d' % u,), (pname + 't2%d' % u,))
                    k.op('V', lambda v, sc=sc: v.tensor_tensor(sc['var'], sc['m'], sc['m'],
                                                               op=ALU.mult),
                         (pname + 'm%d' % u,), (pname + 'var%d' % u,))
                    k.op('V', lambda v, sc=sc: v.tensor_tensor(sc['var'], sc['t2'], sc['var'],
                                                               op=ALU.subtract),
                         (pname + 't2%d' % u, pname + 'var%d' % u), (pname + 'var%d' % u,))
                    k.op('V', lambda v, sc=sc: v.tensor_scalar(sc['var'], sc['var'], scalar1=LN_EPS,
                                                               scalar2=None, op0=ALU.add),
                         (pname + 'var%d' % u,), (pname + 'var%d' % u,))
                    k.op('S', lambda s, sc=sc: s.activation(sc['sd'], sc['var'], AF.Sqrt),
                         (pname + 'var%d' % u,), (pname + 'sd%d' % u,))
                    k.op('V', lambda v, sc=sc: v.reciprocal(sc['rstd'], sc['sd']),
                         (pname + 'sd%d' % u,), (pname + 'rstd%d' % u,))
                    k.op('V', lambda v, sc=sc: v.scalar_tensor_tensor(sc['nm'], sc['m'], -1.0,
                                                                      sc['rstd'], op0=ALU.mult,
                                                                      op1=ALU.mult),
                         (pname + 'm%d' % u, pname + 'rstd%d' % u), (pname + 'nm%d' % u,))
                    k.op('S', lambda s, sc=sc, u=u: s.activation(bzf[:, u, :], bz[:, u, :],
                                                                 AF.Identity, bias=sc['nm'],
                                                                 scale=sc['rstd']),
                         (zkey, pname + 'nm%d' % u, pname + 'rstd%d' % u), ('bzf%d' % u,))
                    k.dma('Y', 'yO' if u == 0 else 'yP', res_out[tt * 128:(tt + 1) * 128, :],
                          bzf[:, u, :], reads=('bzf%d' % u,))
                    if not final:
                        pend.append((tt, u))
                        flush(1)
                if not final:
                    flush(0)
                k.barrier()

        boundary(x_in, pre1, res1, 'b1.')

        def swiglu_A(wg_src, wu_src, n_it, G_dram):
            with ExitStack() as ph:
                psb = lambda n, sh, dt=F32: ph.enter_context(nc.psum_tensor(_un(n), sh, dt))
                ps_g = psb("ps_g", [128, NC, 512])
                ps_u = psb("ps_u", [128, NC, 512])
                gwb = ph.enter_context(nc.sbuf_tensor(_un("gwb"), [128, 2, 16, 128], BF16))
                uwb = ph.enter_context(nc.sbuf_tensor(_un("uwb"), [128, 2, 16, 128], BF16))
                sgb = ph.enter_context(nc.sbuf_tensor(_un("sgb"), [128, NC, 512], F32))
                gst = ph.enter_context(nc.sbuf_tensor(_un("gst"), [128, NC, 512], BF16))
                for it in range(n_it):
                    wsl = it % 2
                    k.dma('G', 'gA' if wsl == 0 else 'gB', gwb[:, wsl],
                          wg_src.rearrange("(a p) c -> p a c", p=128)[:, :, it * 128:(it + 1) * 128],
                          writes=('gw%d' % wsl,))
                    k.dma('G', 'gC' if wsl == 0 else 'gD', uwb[:, wsl],
                          wu_src.rearrange("(a p) c -> p a c", p=128)[:, :, it * 128:(it + 1) * 128],
                          writes=('uw%d' % wsl,))
                    for et in range(16):
                        for tc in range(NC):
                            k.mm(gwb[:, wsl, et, :], XT(et, tc * 512, (tc + 1) * 512),
                                 ps_g[:, tc, :], ('gw%d' % wsl, 'XT'), 'ps_g%d' % tc,
                                 start=(et == 0), stop=(et == 15))
                        for tc in range(NC):
                            k.mm(uwb[:, wsl, et, :], XT(et, tc * 512, (tc + 1) * 512),
                                 ps_u[:, tc, :], ('uw%d' % wsl, 'XT'), 'ps_u%d' % tc,
                                 start=(et == 0), stop=(et == 15))
                    for tc in range(NC):
                        k.op('S', lambda s, tc=tc: s.activation(sgb[:, tc, :], ps_g[:, tc, :], AF.Silu),
                             ('ps_g%d' % tc,), ('sgb%d' % tc,))
                        k.op('V', lambda v, tc=tc: v.tensor_tensor(gst[:, tc, :], sgb[:, tc, :],
                                                                   ps_u[:, tc, :], op=ALU.mult),
                             ('sgb%d' % tc, 'ps_u%d' % tc), ('gst',))
                    Gv = G_dram.rearrange("tt p i w -> p tt i w")
                    for tc in range(NC):
                        k.dma('Y', 'yO' if (it + tc) % 2 == 0 else 'yP',
                              Gv[:, tc * 4:(tc + 1) * 4, it, :],
                              gst[:, tc, :].rearrange("p (a b) -> p a b", b=128),
                              reads=('gst',))
                k.barrier()

        def down_B(G_dram, n_it, wd_src, out_dram, scale_col=None, accum=False):
            with ExitStack() as ph:
                psb = lambda n, sh, dt=F32: ph.enter_context(nc.psum_tensor(_un(n), sh, dt))
                ps_mm = psb("ps_mmd", [128, 2, 2, 512])
                gtb = ph.enter_context(nc.sbuf_tensor(_un("gtb"), [128, 3, n_it, 128], BF16))
                est2 = ph.enter_context(nc.sbuf_tensor(_un("est2"), [128, 2, 2, 512], F32))
                for pas in range(2):
                    for eco in range(2):
                        dst = BIG1 if eco == 0 else BIG2
                        k.dma('G', 'gA' if eco == 0 else 'gB',
                              dst[:, 0:n_it * 512].rearrange("p (a b) -> p a b", b=512),
                              wd_src.rearrange("(a p) c -> p a c", p=128)[:, :, (2 * pas + eco) * 512:(2 * pas + eco + 1) * 512],
                              writes=('dW%d' % eco,))
                    for pf in range(min(2, NT)):
                        k.dma('Y', ('yA', 'yB', 'yC')[pf % 3], gtb[:, pf % 3],
                              G_dram[pf], writes=('gtb%d' % (pf % 3),))
                    for tt in range(NT):
                        tsl = tt % 3
                        if tt + 2 < NT:
                            k.dma('Y', ('yA', 'yB', 'yC')[(tt + 2) % 3], gtb[:, (tt + 2) % 3],
                                  G_dram[tt + 2], writes=('gtb%d' % ((tt + 2) % 3),))
                        for it in range(n_it):
                            for eco in range(2):
                                dst = BIG1 if eco == 0 else BIG2
                                k.mm(gtb[:, tsl, it, :], dst[:, it * 512:(it + 1) * 512],
                                     ps_mm[:, tt % 2, eco, :], ('gtb%d' % tsl, 'dW%d' % eco),
                                     'ps_mmd%d_%d' % (tt % 2, eco),
                                     start=(it == 0), stop=(it == n_it - 1))
                        for eco in range(2):
                            ec = 2 * pas + eco
                            pk = 'ps_mmd%d_%d' % (tt % 2, eco)
                            ekey = 'est2_%d_%d' % (tt % 2, eco)
                            if scale_col is None:
                                k.op('V', lambda v, tt=tt, eco=eco: v.tensor_copy(
                                        est2[:, tt % 2, eco, :], ps_mm[:, tt % 2, eco, :]),
                                     (pk,), (ekey,))
                            else:
                                k.op('V', lambda v, tt=tt, eco=eco: v.tensor_scalar(
                                        est2[:, tt % 2, eco, :], ps_mm[:, tt % 2, eco, :],
                                        scalar1=scale_col(tt), scalar2=None, op0=ALU.mult),
                                     (pk, 'sgate'), (ekey,))
                            if accum:
                                k.dma('G', 'gR', out_dram[tt * 128:(tt + 1) * 128, ec * 512:(ec + 1) * 512],
                                      est2[:, tt % 2, eco, :], reads=(ekey,), accum_op=ALU.add)
                            else:
                                k.dma('Y', 'yO' if eco == 0 else 'yP',
                                      out_dram[tt * 128:(tt + 1) * 128, ec * 512:(ec + 1) * 512],
                                      est2[:, tt % 2, eco, :], reads=(ekey,))
                k.barrier()

        swiglu_A(wgT0, wuT0, FF0 // 128, G0)
        down_B(G0, FF0 // 128, wdT0, pre2)
        boundary(res1, pre2, res2, 'b2.')

        with ExitStack() as ph:
            psb = lambda n, sh, dt=F32: ph.enter_context(nc.psum_tensor(_un(n), sh, dt))
            ps_mm = psb("ps_mmp", [128, 2, max(NC, 4), 512])
            wpb = ph.enter_context(nc.sbuf_tensor(_un("wpb"), [128, 2, 16, 128], BF16))
            pstage = ph.enter_context(nc.sbuf_tensor(_un("pstage"), [128, 2, NC, 512], BF16))
            for (wsrc, tgt, nm) in ((wqT1, QT1, 'q'), (wkT1, KT1, 'k')):
                for ct in range(16):
                    wsl = ct % 2
                    wk = 'wpb%d' % wsl
                    k.dma('G', 'gA' if wsl == 0 else 'gB', wpb[:, wsl],
                          wsrc.rearrange("(a p) c -> p a c", p=128)[:, :, ct * 128:(ct + 1) * 128],
                          writes=(wk,))
                    for et in range(16):
                        for tc in range(NC):
                            k.mm(wpb[:, wsl, et, :], XT(et, tc * 512, (tc + 1) * 512),
                                 ps_mm[:, ct % 2, tc, :], (wk, 'XT'), 'ps_mmp%d_%d' % (ct % 2, tc),
                                 start=(et == 0), stop=(et == 15))
                    for tc in range(NC):
                        k.op('V', lambda v, ct=ct, tc=tc: v.tensor_copy(
                                pstage[:, ct % 2, tc, :], ps_mm[:, ct % 2, tc, :]),
                             ('ps_mmp%d_%d' % (ct % 2, tc),), ('pstage%d' % (ct % 2),))
                    k.dma('Y', 'yO' if ct % 2 == 0 else 'yP', tgt[ct * 128:(ct + 1) * 128, :],
                          pstage[:, ct % 2].rearrange("p a b -> p (a b)"),
                          reads=('pstage%d' % (ct % 2),))
            k.dma('G', 'gM', BIG2[:, 0:16 * E].rearrange("p (a b) -> p a b", b=E),
                  wvT1.rearrange("(a p) c -> p a c", p=128), writes=('wvB',))
            for tt in range(NT):
                for et in range(16):
                    for cc in range(4):
                        k.mm(XT(et, tt * 128, (tt + 1) * 128),
                             BIG2[:, et * E + cc * 512: et * E + (cc + 1) * 512],
                             ps_mm[:, tt % 2, cc, :], ('XT', 'wvB'), 'ps_mmp%d_%d' % (tt % 2, cc),
                             start=(et == 0), stop=(et == 15))
                for cc in range(4):
                    pk = 'ps_mmp%d_%d' % (tt % 2, cc)
                    k.op('V', lambda v, tt=tt, cc=cc: v.tensor_copy(
                            pstage[:, tt % 2, 0, :] if NC == 1 else pstage[:, tt % 2, cc % NC, :],
                            ps_mm[:, tt % 2, cc, :]), (pk,), ('pstage%d' % (tt % 2),))
                    k.dma('Y', 'yO' if cc % 2 == 0 else 'yP',
                          V1[tt * 128:(tt + 1) * 128, cc * 512:(cc + 1) * 512],
                          pstage[:, tt % 2, 0, :] if NC == 1 else pstage[:, tt % 2, cc % NC, :],
                          reads=('pstage%d' % (tt % 2),))
            k.barrier()

        with ExitStack() as ph:
            psb = lambda n, sh, dt=F32: ph.enter_context(nc.psum_tensor(_un(n), sh, dt))
            ps_s = psb("ps_s1", [128, 4, 512])
            ps_ctx = psb("ps_ctx1", [128, 2, 512])
            ps_bc = psb("ps_bc1", [128, 512])
            expBc = ph.enter_context(nc.sbuf_tensor(_un("expBc"), [128, 2, 15 * 128], BF16))
            k.dma('Y', 'yM', expBc[:, 0, :], expB_d[:, :], writes=('expBc',))
            k.dma('Y', 'yM', expBc[:, 1, :], expB_d[:, :], writes=('expBc',))
            qb1 = ph.enter_context(nc.sbuf_tensor(_un("qb1"), [128, 2, T], BF16))
            kb1 = ph.enter_context(nc.sbuf_tensor(_un("kb1"), [128, 2, T], BF16))
            vx1 = ph.enter_context(nc.sbuf_tensor(_un("vx1"), [128, 2, 2, NT, 65], BF16))
            pe1 = ph.enter_context(nc.sbuf_tensor(_un("pe1"), [128, 4, 512], BF16))
            pb1 = ph.enter_context(nc.sbuf_tensor(_un("pb1"), [128, 4, 512], BF16))
            rec1 = ph.enter_context(nc.sbuf_tensor(_un("rec1"), [1, 2, 512], F32))
            bcb1 = ph.enter_context(nc.sbuf_tensor(_un("bcb1"), [64, 2, 512], F32))
            k.op('G', lambda g: g.memset(vx1[:, :, :, :, 64:65], 1.0), (), ('vxones',))
            for hp in range(H1 // 2):
                sl = hp % 2
                for hh in range(2):
                    h = 2 * hp + hh
                    p0 = 64 * hh
                    k.dma('Y', ('yA', 'yB', 'yC', 'yD')[2 * sl + hh], qb1[p0:p0 + 64, sl, :],
                          QT1.rearrange("(h d) t -> h d t", d=64)[h], writes=('qb1_%d%d' % (sl, hh),))
                    k.dma('Y', ('yC', 'yD', 'yA', 'yB')[2 * sl + hh], kb1[p0:p0 + 64, sl, :],
                          KT1.rearrange("(h d) t -> h d t", d=64)[h], writes=('kb1_%d%d' % (sl, hh),))
                    k.dma('Y', 'yE' if sl == 0 else 'yF', vx1[:, sl, hh, :, 0:64],
                          V1.rearrange("(a p) c -> p a c", p=128)[:, :, h * 64:(h + 1) * 64],
                          writes=('vx1_%d%d' % (sl, hh),))

                def mm1(hh, qc, kt, sl=None):
                    sl_ = hp % 2
                    p0 = 64 * hh
                    k.mm(kb1[p0:p0 + 64, sl_, kt * 128:(kt + 1) * 128],
                         qb1[p0:p0 + 64, sl_, qc * 512:(qc + 1) * 512],
                         ps_s[:, 2 * (kt % 2) + hh, :],
                         ('kb1_%d%d' % (sl_, hh), 'qb1_%d%d' % (sl_, hh)),
                         'ps_s1%d' % (2 * (kt % 2) + hh), True, True)

                def chain2(qc, kt):
                    sl_ = hp % 2
                    s0 = 2 * (kt % 2)
                    dc = kt - 4 * qc
                    skeys = ('ps_s1%d' % s0, 'ps_s1%d' % (s0 + 1))
                    pkk = ('pb1%d' % s0, 'pb1%d' % (s0 + 1))
                    if -4 <= dc <= 7:
                        k.op('S', lambda s, s0=s0: s.activation(
                                pe1[:, s0:s0 + 2, :], ps_s[:, s0:s0 + 2, :], AF.Exp,
                                scale=0.125), skeys, ('pe1%d' % s0,))
                        k.op('V', lambda v, s0=s0, dc=dc: v.tensor_tensor(
                                pb1[:, s0:s0 + 2, :], pe1[:, s0:s0 + 2, :],
                                expBc[:, :, (7 - dc) * 128:(7 - dc) * 128 + 512], op=ALU.mult),
                             ('pe1%d' % s0, 'expBc'), pkk)
                    else:
                        ccol, ckn = (cHigh, 'cHigh') if dc > 7 else (cLow, 'cLow')
                        k.op('S', lambda s, ccol=ccol, s0=s0: s.activation(
                                pb1[:, s0:s0 + 2, :], ps_s[:, s0:s0 + 2, :], AF.Exp, scale=0.125,
                                bias=ccol[:]), skeys + (ckn,), pkk)
                    for hh in range(2):
                        k.mm(vx1[:, sl_, hh, kt, :], pb1[:, s0 + hh, :], ps_ctx[0:65, hh, :],
                             ('vx1_%d%d' % (sl_, hh), 'vxones', pkk[hh]), 'ps_ctx1%d' % hh,
                             start=(kt == 0), stop=(kt == NT - 1), force_inc=True)

                def epilogue(qc):
                    for hh in range(2):
                        h = 2 * hp + hh
                        ck = 'ps_ctx1%d' % hh
                        k.op('V', lambda v, hh=hh: v.reciprocal(rec1[:, hh, :],
                                                                ps_ctx[64:65, hh, :]),
                             (ck,), ('rec1%d' % hh,))
                        k.op('T', lambda t, hh=hh: t.matmul(ps_bc[0:64, :], ones1f[0:1, 0:64],
                                                            rec1[:, hh, :], start=True, stop=True),
                             ('ones1f', 'rec1%d' % hh), ('ps_bc1',))
                        k.op('S', lambda s, hh=hh: s.copy(bcb1[:, hh, :], ps_bc[0:64, :]),
                             ('ps_bc1',), ('bcb1%d' % hh,))
                        k.op('V', lambda v, h=h, hh=hh, qc=qc: v.tensor_tensor(
                                CTXT(64 * (h % 2), 64 * (h % 2) + 64, h // 2, qc * 512,
                                     (qc + 1) * 512),
                                ps_ctx[0:64, hh, :], bcb1[:, hh, :], op=ALU.mult),
                             (ck, 'bcb1%d' % hh), ('CTXT',))

                for qc in range(NC):
                    for hh in range(2):
                        mm1(hh, qc, 0)
                    if qc > 0:
                        epilogue(qc - 1)
                    for kt in range(NT):
                        for hh in range(2):
                            if kt + 1 < NT:
                                mm1(hh, qc, kt + 1)
                        chain2(qc, kt)
                epilogue(NC - 1)
            k.barrier()

        mid_stack.close()
        outproj(woT1, pre3, 'wo1')
        boundary(res2, pre3, res3, 'b3.')

        with ExitStack() as ph:
            psb = lambda n, sh, dt=F32: ph.enter_context(nc.psum_tensor(_un(n), sh, dt))
            ps_r = psb("ps_r", [128, 512])
            ps_sg = psb("ps_sg", [128, 512])
            rwb = ph.enter_context(nc.sbuf_tensor(_un("rwb"), [128, 16, NEXP], F32))
            xr32 = ph.enter_context(nc.sbuf_tensor(_un("xr32"), [128, 2, E], F32))
            xt32 = ph.enter_context(nc.sbuf_tensor(_un("xt32"), [128, 16, 128], F32))
            sgwb = ph.enter_context(nc.sbuf_tensor(_un("sgwb"), [128, 16, 1], BF16))
            lgb = ph.enter_context(nc.sbuf_tensor(_un("lgb"), [128, NEXP], F32))
            mskb = ph.enter_context(nc.sbuf_tensor(_un("mskb"), [128, NEXP], F32))
            eq1b = ph.enter_context(nc.sbuf_tensor(_un("eq1b"), [128, NEXP], F32))
            eq2b = ph.enter_context(nc.sbuf_tensor(_un("eq2b"), [128, NEXP], F32))
            scs = {n: ph.enter_context(nc.sbuf_tensor(_un("rt" + n), [128, 1], F32))[:]
                   for n in ('m1', 'm2', 'dl', 'w1', 'w2')}
            k.dma('Y', 'yM', rwb[:], rwT.rearrange("(a p) c -> p a c", p=128), writes=('rwb',))
            for et in range(16):
                k.dma('G', 'gM', sgwb[:, et, :], sgw[et * 128:(et + 1) * 128, :], writes=('sgwb',))
            ps_rt = psb("ps_rt", [128, 512])
            for tt in range(NT):
                sl = tt % 2
                k.dma('Y', 'yA' if sl == 0 else 'yB', xr32[:, sl, :],
                      res3[tt * 128:(tt + 1) * 128, :], writes=('xr32%d' % sl,))
                for et in range(16):
                    k.op('T', lambda t, sl=sl, et=et: t.transpose(
                            ps_rt[:, 0:128], xr32[:, sl, et * 128:(et + 1) * 128], idf32[:]),
                         ('xr32%d' % sl, 'idf32'), ('ps_rt',))
                    k.op('V', lambda v, et=et: v.tensor_copy(xt32[:, et, :], ps_rt[:, 0:128]),
                         ('ps_rt',), ('xt32',))
                for et in range(16):
                    k.mm(xt32[:, et, :], rwb[:, et, :], ps_r[:, 0:NEXP],
                         ('xt32', 'rwb'), 'ps_r', start=(et == 0), stop=(et == 15))
                for et in range(16):
                    k.mm(XT(et, tt * 128, (tt + 1) * 128), sgwb[:, et, :], ps_sg[:, 0:1],
                         ('XT', 'sgwb'), 'ps_sg', start=(et == 0), stop=(et == 15))
                k.op('V', lambda v: v.tensor_copy(lgb[:], ps_r[:, 0:NEXP]), ('ps_r',), ('lgb',))
                k.op('S', lambda s, tt=tt: s.activation(sgate[:, tt, :], ps_sg[:, 0:1], AF.Sigmoid),
                     ('ps_sg',), ('sgate',))
                k.op('V', lambda v: v.reduce_max(scs['m1'], lgb[:], axis=AX.X), ('lgb',), ('m1',))
                k.op('V', lambda v: v.tensor_scalar(eq1b[:], lgb[:], scalar1=scs['m1'], scalar2=None,
                                                    op0=ALU.is_equal), ('lgb', 'm1'), ('eq1b',))
                k.op('V', lambda v: v.scalar_tensor_tensor(mskb[:], eq1b[:], -1e30, lgb[:],
                                                           op0=ALU.mult, op1=ALU.add),
                     ('eq1b', 'lgb'), ('mskb',))
                k.op('V', lambda v: v.reduce_max(scs['m2'], mskb[:], axis=AX.X), ('mskb',), ('m2',))
                k.op('V', lambda v: v.tensor_scalar(eq2b[:], mskb[:], scalar1=scs['m2'], scalar2=None,
                                                    op0=ALU.is_equal), ('mskb', 'm2'), ('eq2b',))
                k.op('V', lambda v: v.tensor_tensor(scs['dl'], scs['m1'], scs['m2'], op=ALU.subtract),
                     ('m1', 'm2'), ('dl',))
                k.op('S', lambda s: s.activation(scs['w1'], scs['dl'], AF.Sigmoid), ('dl',), ('w1',))
                k.op('V', lambda v: v.tensor_scalar(scs['w2'], scs['w1'], scalar1=-1.0, scalar2=1.0,
                                                    op0=ALU.mult, op1=ALU.add), ('w1',), ('w2',))
                k.op('V', lambda v: v.tensor_scalar(eq1b[:], eq1b[:], scalar1=scs['w1'], scalar2=None,
                                                    op0=ALU.mult), ('eq1b', 'w1'), ('eq1b',))
                k.op('V', lambda v, tt=tt: v.scalar_tensor_tensor(combine[:, tt, :], eq2b[:],
                                                                  scs['w2'], eq1b[:],
                                                                  op0=ALU.mult, op1=ALU.add),
                     ('eq2b', 'w2', 'eq1b'), ('combine',))
            k.barrier()

        with ExitStack() as ph:
            psb = lambda n, sh, dt=F32: ph.enter_context(nc.psum_tensor(_un(n), sh, dt))
            ps_g = psb("ps_ge", [128, max(NC, 4), 512])
            ps_u = psb("ps_ue", [128, max(NC, 4), 512])
            gwb = ph.enter_context(nc.sbuf_tensor(_un("gwbe"), [128, 2, 16, 128], BF16))
            uwb = ph.enter_context(nc.sbuf_tensor(_un("uwbe"), [128, 2, 16, 128], BF16))
            sgbe2 = ph.enter_context(nc.sbuf_tensor(_un("sgbe"), [128, NC, 512], F32))
            este = ph.enter_context(nc.sbuf_tensor(_un("este"), [128, 2, 512], F32))
            GeB = ph.enter_context(nc.sbuf_tensor(_un("GeB"), [128, 11 * T], BF16))
            for ex in range(NEXP):
                k.dma('G', 'gW', BIG2[:, 0:11 * E].rearrange("p (a b) -> p a b", b=E),
                      wd1.rearrange("e (a p) c -> e p a c", p=128)[ex], writes=('ewd',))
                for it in range(11):
                    wsl = it % 2
                    k.dma('G', 'gA' if wsl == 0 else 'gB', gwb[:, wsl],
                          wg1.rearrange("e (a p) c -> e p a c", p=128)[ex][:, :, it * 128:(it + 1) * 128],
                          writes=('gw%d' % wsl,))
                    k.dma('G', 'gC' if wsl == 0 else 'gD', uwb[:, wsl],
                          wu1.rearrange("e (a p) c -> e p a c", p=128)[ex][:, :, it * 128:(it + 1) * 128],
                          writes=('uw%d' % wsl,))
                    for et in range(16):
                        for tc in range(NC):
                            k.mm(gwb[:, wsl, et, :], XT(et, tc * 512, (tc + 1) * 512),
                                 ps_g[:, tc, :], ('gw%d' % wsl, 'XT'), 'ps_ge%d' % tc,
                                 start=(et == 0), stop=(et == 15))
                        for tc in range(NC):
                            k.mm(uwb[:, wsl, et, :], XT(et, tc * 512, (tc + 1) * 512),
                                 ps_u[:, tc, :], ('uw%d' % wsl, 'XT'), 'ps_ue%d' % tc,
                                 start=(et == 0), stop=(et == 15))
                    for tc in range(NC):
                        k.op('S', lambda s, tc=tc: s.activation(sgbe2[:, tc, :], ps_g[:, tc, :], AF.Silu),
                             ('ps_ge%d' % tc,), ('sgb%d' % tc,))
                        k.op('V', lambda v, it=it, tc=tc: v.tensor_tensor(
                                GeB[:, it * T + tc * 512: it * T + (tc + 1) * 512], sgbe2[:, tc, :],
                                ps_u[:, tc, :], op=ALU.mult), ('sgb%d' % tc, 'ps_ue%d' % tc), ('Ge',))
                for tt in range(NT):
                    for it in range(11):
                        for hc in range(4):
                            k.mm(GeB[:, it * T + tt * 128: it * T + (tt + 1) * 128],
                                 BIG2[:, it * E + hc * 512: it * E + (hc + 1) * 512],
                                 ps_g[:, hc, :] if tt % 2 == 0 else ps_u[:, hc, :],
                                 ('Ge', 'ewd'), 'ps_%se%d' % ('g' if tt % 2 == 0 else 'u', hc),
                                 start=(it == 0), stop=(it == 10))
                    for hp in range(2):
                        for hh in range(2):
                            hc = hp * 2 + hh
                            psrc = ps_g[:, hc, :] if tt % 2 == 0 else ps_u[:, hc, :]
                            pk = 'ps_%se%d' % ('g' if tt % 2 == 0 else 'u', hc)
                            k.op('V', lambda v, tt=tt, ex=ex, hh=hh, psrc=psrc: v.tensor_scalar(
                                    este[:, hh, :], psrc,
                                    scalar1=combine[:, tt, ex:ex + 1], scalar2=None, op0=ALU.mult),
                                 (pk, 'combine'), ('este%d' % hh,))
                        sem = 'gR' if hp == 0 else 'gM'
                        if ex == 0:
                            k.dma('G', sem, racc[tt * 128:(tt + 1) * 128, hp * 1024:(hp + 1) * 1024],
                                  este[:].rearrange("p a b -> p (a b)"), reads=('este0', 'este1'))
                        else:
                            k.dma('G', sem, racc[tt * 128:(tt + 1) * 128, hp * 1024:(hp + 1) * 1024],
                                  este[:].rearrange("p a b -> p (a b)"), reads=('este0', 'este1'),
                                  accum_op=ALU.add)
            k.barrier()

        swiglu_A(swg, swu, FSH // 128, G1)
        down_B(G1, FSH // 128, swd, racc, scale_col=lambda tt: sgate[:, tt, :], accum=True)

        boundary(res3, racc, y_out, 'bf.', final=True)

    with nc.Block() as block:
        @block.sync
        def _(e):
            for c in k.q['Y']:
                c(e)

        @block.gpsimd
        def _(e):
            for c in k.q['G']:
                c(e)

        @block.tensor
        def _(e):
            for c in k.q['T']:
                c(e)

        @block.vector
        def _(e):
            for c in k.q['V']:
                c(e)

        @block.scalar
        def _(e):
            for c in k.q['S']:
                c(e)

    k.stack.close()
    return nc


def _host_consts(T):
    inv = 1.0 / (10000.0 ** (np.arange(0, D0, 2, dtype=np.float32) / D0))
    t = np.arange(T, dtype=np.float32)[:, None] * inv[None, :].astype(np.float32)
    ropec = np.ascontiguousarray(np.tile(np.cos(t).astype(np.float32), (1, 4))).astype(NPBF)
    ropes = np.ascontiguousarray(np.tile(np.sin(t).astype(np.float32), (1, 4))).astype(NPBF)
    qq = np.arange(512)
    pp = np.arange(128)[:, None]
    mask = np.zeros((128, 4 * 512), np.float32)
    for j in range(4):
        mask[:, j * 512:(j + 1) * 512] = (qq[None, :] >= 128 * j + pp).astype(np.float32)
    mask0 = mask.astype(NPBF)
    bidx = np.zeros((128, 15), np.int32)
    for b in range(15):
        delta = 7 - b
        bidx[:, b] = 128 * delta + 2047 - np.arange(128)
    return ropec, ropes, mask0, bidx


def host_inputs(inp, T=2048):
    f32 = lambda a: np.ascontiguousarray(np.asarray(a, dtype=np.float32))
    ropec, ropes, mask0, bidx = _host_consts(T)
    return {
        "w_qkv": f32(inp["L0_attn_w_qkv"]),
        "w_out0": f32(inp["L0_attn_w_out"]),
        "ln0_s": np.ascontiguousarray(np.tile(f32(inp["L0_attn_ln_scale"]).reshape(1, D0), (1, 4))),
        "ln0_b": np.ascontiguousarray(np.tile(f32(inp["L0_attn_ln_bias"]).reshape(1, D0), (1, 4))),
        "wgT0": np.ascontiguousarray(f32(inp["L0_ffn_w_gate"]).T),
        "wuT0": np.ascontiguousarray(f32(inp["L0_ffn_w_up"]).T),
        "wdT0": np.ascontiguousarray(f32(inp["L0_ffn_w_down"]).T),
        "wqT1": np.ascontiguousarray(f32(inp["L1_attn_wq"]).T),
        "wkT1": np.ascontiguousarray(f32(inp["L1_attn_wk"]).T),
        "wvT1": np.ascontiguousarray(f32(inp["L1_attn_wv"]).T),
        "woT1": np.ascontiguousarray(f32(inp["L1_attn_wo"]).T),
        "relpos": f32(inp["L1_attn_rel_pos"]),
        "rwT": np.ascontiguousarray(f32(inp["L1_ffn_router_w"]).T),
        "sgw": f32(inp["L1_ffn_shared_gate_w"]).reshape(E, 1),
        "wg1": f32(inp["L1_ffn_w_gate"]),
        "wu1": f32(inp["L1_ffn_w_up"]),
        "wd1": f32(inp["L1_ffn_w_down"]),
        "swg": f32(inp["L1_ffn_sw_gate"]),
        "swu": f32(inp["L1_ffn_sw_up"]),
        "swd": f32(inp["L1_ffn_sw_down"]),
        "ropec_d": ropec, "ropes_d": ropes, "mask0_d": mask0, "bidx_d": bidx,
    }


def kernel(**inp):
    T = 2048
    x = np.ascontiguousarray(np.asarray(inp["L0_attn_x"], dtype=np.float32))
    common = host_inputs(inp, T)
    nc = build_program(T=T)
    in_maps = [dict(common, x=np.ascontiguousarray(x[b])) for b in range(8)]
    res = run_bass_kernel_spmd(nc, in_maps, core_ids=list(range(8)))
    return np.stack([np.asarray(r["y"], dtype=np.float32) for r in res.results], axis=0)
